# version 08
# baseline (speedup 1.0000x reference)
import numpy as np
from contextlib import ExitStack

import concourse.bass as bass
import concourse.mybir as mybir
from concourse.bass_utils import run_bass_kernel_spmd
from concourse.aot_env import allow_ici_all_to_all
import ml_dtypes

BF = ml_dtypes.bfloat16
F32 = mybir.dt.float32
BF16 = mybir.dt.bfloat16
I32 = mybir.dt.int32
AF = mybir.ActivationFunctionType
ALU = mybir.AluOpType
AX = mybir.AxisListType

NCORE = 8
_NAME_SEQ = [0]
D = 4096
HC = 32
I_FF = 11008
IBLOCKS = [(i * 512, 512) for i in range(21)] + [(10752, 256)]
IC = 86
TOPK = 5504
BISECT_ITERS = 16
NT = 8
EPS_LN = 1e-5

ENGINES = ("sync", "tensor", "scalar", "vector", "gpsimd")


def cast_bf(x):
    return np.ascontiguousarray(np.asarray(x, dtype=np.float32)).astype(BF)


def core_token_idx(c, n_tiles=NT):
    return np.r_[np.arange(512 * c, 512 * c + 512),
                 np.arange(512 * (15 - c), 512 * (15 - c) + 512)][:n_tiles * 128]


class Emit:

    def __init__(self, nc, ctx=None):
        self.nc = nc
        self.ctx = ctx if ctx is not None else ExitStack()
        self.ops = {e: [] for e in ENGINES}
        self.sems = {}
        self.cnt = {}
        self.dma_sems = set()

    def sem(self, name):
        if name not in self.sems:
            self.sems[name] = self.ctx.enter_context(self.nc.semaphore(name))
            self.cnt[name] = 0
        return self.sems[name]

    def op(self, eng, fn, inc=None, incval=1):
        if inc is not None:
            s = self.sem(inc)
            if incval == 16:
                self.dma_sems.add(inc)
            self.cnt[inc] += incval
            v = self.cnt[inc]
            self.ops[eng].append(lambda e, fn=fn, s=s, iv=incval: fn(e).then_inc(s, iv))
            return v
        self.ops[eng].append(fn)
        return None

    def wait(self, eng, semname, v):
        if v <= 0:
            return
        s = self.sem(semname)
        self.ops[eng].append(lambda e, s=s, v=v: e.wait_ge(s, v))

    def val(self, name):
        self.sem(name)
        return self.cnt[name]

    def barrier(self):
        for sname in sorted(self.dma_sems):
            self.wait("gpsimd", sname, self.val(sname))
        b = self.sem("BAR")
        for eng in ENGINES:
            self.cnt["BAR"] += 1
            self.ops[eng].append(lambda e, b=b: e.sem_inc(b, 1))
        tgt = self.cnt["BAR"]
        for eng in ENGINES:
            self.wait(eng, "BAR", tgt)

    def run(self):
        with self.nc.Block() as blk:
            for name in ENGINES:
                ops = self.ops[name]
                if not ops:
                    continue
                def runner(e, ops=ops):
                    for fn in ops:
                        fn(e)
                getattr(blk, name)(runner)


def dv_chain(em, fn):
    v = em.op("vector", fn, inc="DV")
    em.wait("vector", "DV", v)
    return v


def emit_ln(em, r, trash, ms, va, out_buf):
    dv_chain(em, lambda e: e.reduce_sum(ms[:, 0:1], r, axis=AX.X))
    dv_chain(em, lambda e: e.tensor_scalar(out=ms[:, 1:2], in0=ms[:, 0:1],
                                           scalar1=1.0 / 4096.0, scalar2=None, op0=ALU.mult))
    dv_chain(em, lambda e: e.tensor_scalar(out=r, in0=r,
                                           scalar1=ms[:, 1:2], scalar2=None, op0=ALU.subtract))
    dv_chain(em, lambda e: e.scalar_tensor_tensor(out=trash, in0=r, scalar=1.0,
                                                  in1=r, op0=ALU.mult, op1=ALU.mult,
                                                  accum_out=va[:, 0:1]))
    v = dv_chain(em, lambda e: e.tensor_scalar(out=va[:, 1:2], in0=va[:, 0:1],
                                               scalar1=1.0 / 4096.0, scalar2=EPS_LN,
                                               op0=ALU.mult, op1=ALU.add))
    em.wait("scalar", "DV", v)
    a = em.op("scalar", lambda e: e.activation(out=va[:, 1:2], in_=va[:, 1:2], func=AF.Ln), inc="AC")
    em.wait("vector", "AC", a)
    v = dv_chain(em, lambda e: e.tensor_scalar(out=va[:, 1:2], in0=va[:, 1:2],
                                               scalar1=-0.5, scalar2=None, op0=ALU.mult))
    em.wait("scalar", "DV", v)
    a = em.op("scalar", lambda e: e.activation(out=va[:, 1:2], in_=va[:, 1:2], func=AF.Exp), inc="AC")
    em.wait("vector", "AC", a)
    return dv_chain(em, lambda e: e.tensor_scalar(out=out_buf, in0=r,
                                                  scalar1=va[:, 1:2], scalar2=None, op0=ALU.mult))


def emit_fm_out(em, x_tm, ident, xfmob, ps_t, pst_user, fm_o, row0, lnv):
    em.wait("tensor", "DV", lnv)
    em.wait("scalar", "FM", em.val("FM"))
    for hc in range(HC):
        qk = hc % 2
        em.wait("tensor", "AC", pst_user[qk])
        pe = em.op("tensor", lambda e, hc=hc, qk=qk: e.transpose(
            ps_t[qk], x_tm[:, hc * 128:(hc + 1) * 128], ident[:]), inc="PE")
        em.wait("scalar", "PE", pe)
        ac = em.op("scalar", lambda e, hc=hc, qk=qk: e.copy(
            out=xfmob[:, hc, :], in_=ps_t[qk]), inc="AC")
        pst_user[qk] = ac
    em.wait("gpsimd", "AC", pst_user[0])
    em.wait("gpsimd", "AC", pst_user[1])
    fmv = em.op("gpsimd", lambda e, row0=row0: e.dma_start(
        out=fm_o[:, :, row0:row0 + 128], in_=xfmob[:]), inc="FM", incval=16)
    return pst_user


def emit_weight_gather(em, nc, shards):
    cc = 0
    for ext, internal, full in shards:
        _NAME_SEQ[0] += 1
        wi = "WI_%d" % _NAME_SEQ[0]
        n = em.op("sync", lambda e, a=internal, b=ext: e.dma_start(out=a, in_=b),
                  inc=wi, incval=16)
        em.wait("gpsimd", wi, n)
        cc = em.op("gpsimd", lambda e, i=internal, o=full: e.collective_compute(
            "AllGather", ALU.bypass,
            replica_groups=[list(range(NCORE))],
            ins=[i.opt()], outs=[o.opt()]), inc="CC")
    return cc



F16 = mybir.dt.float16


def emit_p0(nc, em, io, n_tiles=NT, debug=False):
    xfm = io["xfm"]; xtm = io["xtm"]; identb = io["identb"]
    wg_v = io["wg_v"]; wu_v = io["wu_v"]; wd_v = io["wd_v"]
    x1o = io["x1o"]; x1fm_o = io.get("x1fm_o")
    cc = io["cc_wait"]
    cc_g, cc_u, cc_d = cc if isinstance(cc, tuple) else (cc, cc, cc)
    NP = n_tiles // 2

    with ExitStack() as ctx:
        def sb(n, s, d):
            _NAME_SEQ[0] += 1
            return ctx.enter_context(nc.sbuf_tensor("%s_%d" % (n, _NAME_SEQ[0]), s, d))
        def psb(n, s, d=F32):
            _NAME_SEQ[0] += 1
            return ctx.enter_context(nc.psum_tensor("%s_%d" % (n, _NAME_SEQ[0]), s, d))

        xc = [sb("xcA", [128, HC, 256], BF16), sb("xcB", [128, HC, 256], BF16)]
        ident = sb("ident", [128, 128], BF16)
        s16 = [sb("s16_%d" % i, [128, I_FF], F16) for i in range(4)]
        smu = [sb("smuA", [128, IC, 128], BF16), sb("smuB", [128, IC, 128], BF16)]
        wst = sb("wst", [128, 2, HC, 256], BF16)
        wds = [wst[:, k, :, :].rearrange("p hc w -> p (hc w)")[:, 0:4096]
               .rearrange("p (ic w) -> p ic w", ic=8) for k in range(2)]
        up1 = sb("up1", [128, 512], F16)
        smu_tm = sb("smu_tm", [128, 512], BF16)
        mtmp = sb("mtmp", [128, 512], BF16)
        sc = sb("sc", [128, 32], F32)
        trA = sb("trA", [128, 2752], F16)
        trB = sb("trB", [128, 2752], F16)
        ms = sb("ms", [128, 2], F32)
        va = sb("va", [128, 2], F32)

        ps_gA = psb("ps_gA", [128, 2, 512])
        ps_gB = psb("ps_gB", [128, 2, 512])
        ps_tA = psb("ps_tA", [128, 1024], BF16)
        ps_tB = psb("ps_tB", [128, 1024], BF16)
        ps_t = [ps_tA[:, 0:128], ps_tB[:, 0:128]]
        ps_g = [ps_gA, ps_gB]

        LO, HI, MID, CNT, GE, TMP, NMID, CNT2, GN = range(9)
        CA0, CB0 = 16, 20
        IB256 = [(i * 256, 256) for i in range(43)]
        W0 = 16.0

        idl = em.op("gpsimd", lambda e: e.dma_start(out=ident[:], in_=identb),
                    inc="IDL", incval=16)

        wst_user = [0, 0]
        psg_user = [[("AC", 0), ("AC", 0)], [("AC", 0), ("AC", 0)]]
        pst_user = [0, 0]
        psd_user = [0, 0]
        xc_user = [0, 0]
        xcst_user = [0, 0]
        xt_fm_user = [0, 0]
        s_user = [0, 0, 0, 0]
        xc_pe_done = [0, 0]
        s_user_sem = ["DV", "DV", "DV", "DV"]
        smu_user = [0, 0]
        theta_cols = [(8, 10), (12, 14)]
        last_st = 0

        def emit_gate(p):
            k2 = p % 2
            em.wait("gpsimd", "PE", xc_user[k2])
            em.wait("gpsimd", "FM", xt_fm_user[k2])
            em.wait("gpsimd", "ST", xcst_user[k2])
            xl = em.op("gpsimd", lambda e, p=p, k2=k2: e.dma_start(
                out=xc[k2][:], in_=xfm[:, :, p * 256:(p + 1) * 256]),
                inc="XC%d" % k2, incval=16)
            for ib in range(43):
                k = ib % 2
                em.wait("sync", "PE", wst_user[k])
                if p == 0 and ib == 0:
                    em.wait("sync", "CC", cc_g)
                wl = em.op("sync", lambda e, ib=ib, k=k: e.dma_start(
                    out=wst[:, k, :, :], in_=wg_v[:, ib, :, :]),
                    inc="WL%d" % k, incval=16)
                em.wait("tensor", "WL%d" % k, wl)
                if ib == 0:
                    em.wait("tensor", "XC%d" % k2, xl)
                for ti in range(2):
                    sem_n, sem_v = psg_user[ti][k]
                    em.wait("tensor", sem_n, sem_v)
                def mmg(e, k=k, k2=k2):
                    for ti in range(2):
                        for hc in range(HC):
                            m = e.matmul(ps_g[ti][:, k, 0:256],
                                         xc[k2][:, hc, ti * 128:(ti + 1) * 128],
                                         wst[:, k, hc, :], start=(hc == 0), stop=(hc == HC - 1))
                    return m
                pe = em.op("tensor", mmg, inc="PE")
                wst_user[k] = pe
                em.wait("scalar", "PE", pe)
                for ti in range(2):
                    slot = (2 * p + ti) % 4
                    if ib == 0:
                        em.wait("scalar", s_user_sem[slot], s_user[slot])
                    ac = em.op("scalar", lambda e, ib=ib, k=k, ti=ti, slot=slot: e.activation(
                        out=s16[slot][:, ib * 256:(ib + 1) * 256], in_=ps_g[ti][:, k, 0:256],
                        func=AF.Silu), inc="AC")
                    psg_user[ti][k] = ("AC", ac)
            return em.val("AC")

        bis_last_ac = [0]

        def make_bisect(p, gate_ac):
            tcol, ncol = theta_cols[p % 2]
            CHN = [(i * 2752, 2752) for i in range(4)]
            st = {"ti": 0, "it": 0, "init": False, "w": W0 / 2}

            def step(n):
                done = 0
                while done < n and st["ti"] < 2:
                    ti, it = st["ti"], st["it"]
                    slot = (2 * p + ti) % 4
                    if not st["init"]:
                        if ti == 0:
                            em.wait("vector", "AC", gate_ac)
                        em.op("vector", lambda e: e.memset(sc[:, LO:LO + 1], 0.0))
                        v = em.op("vector", lambda e: e.memset(sc[:, MID:MID + 1], W0 / 2),
                                  inc="DV")
                        em.wait("vector", "DV", v)
                        st["init"] = True
                        st["w"] = W0 / 2
                    w = st["w"]
                    em.wait("scalar", "DV", em.val("DV"))
                    for ci, (c0, cw) in enumerate(CHN):
                        em.wait("scalar", "AC", bis_last_ac[0])
                        a = em.op("scalar", lambda e, slot=slot, c0=c0, cw=cw, ci=ci:
                                  e.activation(
                            out=trB[:, 0:cw], in_=s16[slot][:, c0:c0 + cw], func=AF.Sign,
                            bias=sc[:, MID:MID + 1],
                            accum_out=sc[:, CB0 + ci:CB0 + ci + 1]), inc="AC")
                        bis_last_ac[0] = a
                    for ci, (c0, cw) in enumerate(CHN):
                        v = em.op("vector", lambda e, slot=slot, c0=c0, cw=cw, ci=ci:
                                  e.tensor_scalar(
                            out=trA[:, 0:cw], in0=s16[slot][:, c0:c0 + cw],
                            scalar1=sc[:, MID:MID + 1],
                            scalar2=None, op0=ALU.is_gt, op1=ALU.add,
                            accum_out=sc[:, CA0 + ci:CA0 + ci + 1]), inc="DV")
                        em.wait("vector", "DV", v)
                    for c_, d_ in ((CA0, CA0 + 1), (CA0 + 2, CA0 + 3), (CA0, CA0 + 2)):
                        dv_chain(em, lambda e, c_=c_, d_=d_: e.tensor_tensor(
                            out=sc[:, c_:c_ + 1], in0=sc[:, c_:c_ + 1],
                            in1=sc[:, d_:d_ + 1], op=ALU.add))
                    em.wait("vector", "AC", bis_last_ac[0])
                    for c_, d_ in ((CB0, CB0 + 1), (CB0 + 2, CB0 + 3), (CB0, CB0 + 2)):
                        dv_chain(em, lambda e, c_=c_, d_=d_: e.tensor_tensor(
                            out=sc[:, c_:c_ + 1], in0=sc[:, c_:c_ + 1],
                            in1=sc[:, d_:d_ + 1], op=ALU.add))
                    dv_chain(em, lambda e: e.scalar_tensor_tensor(
                        out=sc[:, CNT:CNT + 1], in0=sc[:, CB0:CB0 + 1], scalar=-0.5,
                        in1=sc[:, CA0:CA0 + 1], op0=ALU.mult, op1=ALU.add))
                    dv_chain(em, lambda e: e.tensor_scalar(
                        out=sc[:, GE:GE + 1], in0=sc[:, CNT:CNT + 1], scalar1=0.0,
                        scalar2=None, op0=ALU.is_ge))
                    w2 = w / 2
                    if it < BISECT_ITERS - 1:
                        dv_chain(em, lambda e, w2=w2: e.scalar_tensor_tensor(
                            out=sc[:, LO:LO + 1], in0=sc[:, GE:GE + 1], scalar=w2,
                            in1=sc[:, LO:LO + 1], op0=ALU.mult, op1=ALU.add))
                        dv_chain(em, lambda e, w2=w2: e.tensor_scalar(
                            out=sc[:, MID:MID + 1], in0=sc[:, LO:LO + 1], scalar1=w2 / 2,
                            scalar2=None, op0=ALU.add))
                    else:
                        dv_chain(em, lambda e, w2=w2, tc=tcol + ti: e.scalar_tensor_tensor(
                            out=sc[:, tc:tc + 1], in0=sc[:, GE:GE + 1], scalar=w2,
                            in1=sc[:, LO:LO + 1], op0=ALU.mult, op1=ALU.add))
                        dv_chain(em, lambda e, tc=tcol + ti, nc_=ncol + ti: e.tensor_scalar(
                            out=sc[:, nc_:nc_ + 1], in0=sc[:, tc:tc + 1], scalar1=-1.0,
                            scalar2=None, op0=ALU.mult))
                    st["w"] = w2
                    st["it"] += 1
                    if st["it"] == BISECT_ITERS:
                        st["it"] = 0
                        st["ti"] += 1
                        st["init"] = False
                    done += 1
            return step

        def emit_up(p):
            nonlocal tpose_pe
            k2 = p % 2
            tcol, ncol = theta_cols[p % 2]
            for ti in range(2):
                slot = (2 * p + ti) % 4
                tc = tcol + ti
                nc_ = ncol + ti
                for ib in range(43):
                    k = ib % 2
                    em.wait("sync", "PE", wst_user[k])
                    if p == 0 and ti == 0 and ib == 0:
                        em.wait("sync", "CC", cc_u)
                    wl = em.op("sync", lambda e, ib=ib, k=k: e.dma_start(
                        out=wst[:, k, :, :], in_=wu_v[:, ib, :, :]),
                        inc="WL%d" % k, incval=16)
                    em.wait("tensor", "WL%d" % k, wl)
                    sem_n, sem_v = psg_user[ti][k]
                    em.wait("tensor", sem_n, sem_v)
                    def mmu(e, k=k, k2=k2, ti=ti):
                        for hc in range(HC):
                            m = e.matmul(ps_g[ti][:, k, 0:256],
                                         xc[k2][:, hc, ti * 128:(ti + 1) * 128],
                                         wst[:, k, hc, :], start=(hc == 0), stop=(hc == HC - 1))
                        return m
                    pe = em.op("tensor", mmu, inc="PE")
                    wst_user[k] = pe
                    em.wait("vector", "PE", pe)
                    em.wait("vector", "PE", tpose_pe)
                    if ib == 0:
                        em.wait("vector", "ST", smu_user[ti])
                        em.wait("scalar", "ST", smu_user[ti])
                    ibr = slice(ib * 256, (ib + 1) * 256)
                    dv_chain(em, lambda e, ibr=ibr, slot=slot, tc=tc: e.tensor_scalar(
                        out=mtmp[:, 0:256], in0=s16[slot][:, ibr], scalar1=sc[:, tc:tc + 1],
                        scalar2=None, op0=ALU.is_gt))
                    dv_chain(em, lambda e, ibr=ibr, slot=slot, nc_=nc_: e.tensor_scalar(
                        out=smu_tm[:, 0:256], in0=s16[slot][:, ibr], scalar1=sc[:, nc_:nc_ + 1],
                        scalar2=None, op0=ALU.is_lt))
                    dv_chain(em, lambda e: e.tensor_tensor(
                        out=mtmp[:, 0:256], in0=mtmp[:, 0:256], in1=smu_tm[:, 0:256],
                        op=ALU.add))
                    dv_chain(em, lambda e, ibr=ibr, slot=slot: e.tensor_tensor(
                        out=up1[:, 0:256], in0=s16[slot][:, ibr], in1=mtmp[:, 0:256],
                        op=ALU.mult))
                    dvv = em.op("vector", lambda e, k=k, ti=ti: e.tensor_tensor(
                        out=smu_tm[:, 0:256], in0=up1[:, 0:256], in1=ps_g[ti][:, k, 0:256],
                        op=ALU.mult), inc="DV")
                    psg_user[ti][k] = ("DV", dvv)
                    em.wait("tensor", "DV", dvv)
                    for q in range(2):
                        qk = q % 2
                        if p == 0 and ti == 0 and ib == 0 and q == 0:
                            em.wait("tensor", "IDL", idl)
                        em.wait("tensor", "AC", pst_user[qk])
                        pe = em.op("tensor", lambda e, q=q, qk=qk: e.transpose(
                            ps_t[qk], smu_tm[:, q * 128:(q + 1) * 128], ident[:]),
                            inc="PE")
                        tpose_pe = pe
                        em.wait("scalar", "PE", pe)
                        ac = em.op("scalar", lambda e, ib=ib, q=q, qk=qk, ti=ti: e.copy(
                            out=smu[ti][:, ib * 2 + q, :], in_=ps_t[qk]), inc="AC")
                        pst_user[qk] = ac
            xc_pe_done[p % 2] = em.val("PE")
            return em.val("AC")

        def emit_down_ln(p, smu_ready, filler=None):
            nonlocal last_st
            em.wait("tensor", "AC", smu_ready)
            down_evac = 0
            if filler is not None:
                filler(2)
            for ob in range(8):
                k = ob % 2
                groups = [(i, min(8, IC - i)) for i in range(0, IC, 8)]
                for gi, (ic0, ng) in enumerate(groups):
                    gk = gi % 2
                    em.wait("sync", "PE", wst_user[gk])
                    if p == 0 and ob == 0 and gi == 0:
                        em.wait("sync", "CC", cc_d)
                    wl = em.op("sync", lambda e, ob=ob, ic0=ic0, ng=ng, gk=gk: e.dma_start(
                        out=wds[gk][:, 0:ng, :], in_=wd_v[:, ob, ic0:ic0 + ng, :]),
                        inc="WD%d" % gk, incval=16)
                    em.wait("tensor", "WD%d" % gk, wl)
                    if gi == 0:
                        em.wait("tensor", "AC", psd_user[k])
                        for ti in range(2):
                            sem_n, sem_v = psg_user[ti][k]
                            em.wait("tensor", sem_n, sem_v)
                    def mmdown(e, ic0=ic0, ng=ng, gk=gk, k=k):
                        for ii in range(ng):
                            ic = ic0 + ii
                            st = (ic == 0)
                            sp = (ic == IC - 1)
                            e.matmul(ps_gA[:, k, :], smu[0][:, ic, :],
                                     wds[gk][:, ii, :], start=st, stop=sp)
                            m = e.matmul(ps_gB[:, k, :], smu[1][:, ic, :],
                                         wds[gk][:, ii, :], start=st, stop=sp)
                        return m
                    pe = em.op("tensor", mmdown, inc="PE")
                    wst_user[gk] = pe
                em.wait("scalar", "PE", pe)
                obr = slice(ob * 512, (ob + 1) * 512)
                for ti in range(2):
                    slot = (2 * p + ti) % 4
                    if ob == 0:
                        em.wait("scalar", s_user_sem[slot], s_user[slot])
                rrv = [s16[(2 * p) % 4][:, 0:4096], s16[(2 * p + 1) % 4][:, 0:4096]]
                em.op("scalar", lambda e, k=k, obr=obr, r0=rrv[0]: e.copy(
                    out=r0[:, obr], in_=ps_gA[:, k, :]))
                ac = em.op("scalar", lambda e, k=k, obr=obr, r1=rrv[1]: e.copy(
                    out=r1[:, obr], in_=ps_gB[:, k, :]), inc="AC")
                psd_user[k] = ac
                down_evac = ac
                if filler is not None:
                    filler(4)
            if filler is not None:
                filler(999)
            for ti in range(2):
                psg_user[ti] = [("AC", down_evac), ("AC", down_evac)]

            k2 = p % 2
            xtv = xc[k2][:, 0:16, :].rearrange("p a b -> p (a b)")
            x1bv = [smu[1][:, 0:32, :].rearrange("p a b -> p (a b)"),
                    smu[1][:, 32:64, :].rearrange("p a b -> p (a b)")]
            trashv = smu[1][:, 54:86, :].rearrange("p a b -> p (a b)")
            xfmobv = xc[k2][:, :, 128:256].rearrange("p a b -> p a b")
            rrv = [s16[(2 * p) % 4][:, 0:4096], s16[(2 * p + 1) % 4][:, 0:4096]]
            for pi in range(2):
                slot = (2 * p + pi) % 4
                row0 = (2 * p + pi) * 128
                em.wait("gpsimd", "PE", xc_pe_done[k2])
                if x1fm_o is not None:
                    em.wait("gpsimd", "FM", em.val("FM"))
                xl2 = em.op("gpsimd", lambda e, row0=row0, xtv=xtv: e.dma_start(
                    out=xtv, in_=xtm[row0:row0 + 128, :]), inc="XT", incval=16)
                em.wait("vector", "AC", down_evac)
                em.wait("vector", "XT", xl2)
                em.wait("vector", "ST", em.val("ST"))
                rv = dv_chain(em, lambda e, pi=pi, rrv=rrv, xtv=xtv: e.tensor_tensor(
                    out=rrv[pi], in0=rrv[pi], in1=xtv, op=ALU.add))
                lnv = emit_ln(em, rrv[pi], trashv, ms, va, x1bv[pi])
                s_user[slot] = lnv
                s_user_sem[slot] = "DV"
                em.wait("gpsimd", "DV", lnv)
                stv = em.op("gpsimd", lambda e, pi=pi, row0=row0, x1bv=x1bv: e.dma_start(
                    out=x1o[row0:row0 + 128, :], in_=x1bv[pi]),
                    inc="ST", incval=16)
                last_st = stv
                if x1fm_o is not None:
                    em.wait("tensor", "DV", lnv)
                    em.wait("scalar", "FM", em.val("FM"))
                    for hc in range(HC):
                        qk = hc % 2
                        em.wait("tensor", "AC", pst_user[qk])
                        pe = em.op("tensor", lambda e, hc=hc, pi=pi, qk=qk, x1bv=x1bv: e.transpose(
                            ps_t[qk], x1bv[pi][:, hc * 128:(hc + 1) * 128], ident[:]),
                            inc="PE")
                        em.wait("scalar", "PE", pe)
                        ac = em.op("scalar", lambda e, hc=hc, qk=qk, xfmobv=xfmobv: e.copy(
                            out=xfmobv[:, hc, :], in_=ps_t[qk]), inc="AC")
                        pst_user[qk] = ac
                    em.wait("gpsimd", "AC", pst_user[0])
                    em.wait("gpsimd", "AC", pst_user[1])
                    em.op("gpsimd", lambda e, row0=row0, xfmobv=xfmobv: e.dma_start(
                        out=x1fm_o[:, :, row0:row0 + 128], in_=xfmobv[:]),
                        inc="FM", incval=16)
            smu_user[0] = em.val("ST")
            smu_user[1] = em.val("ST")
            xc_user[k2] = em.val("PE")
            xcst_user[k2] = em.val("ST")
            xt_fm_user[k2] = em.val("FM") if x1fm_o is not None else 0

        tpose_pe = 0
        g_ac = emit_gate(0)
        bis = make_bisect(0, g_ac)
        bis(999)
        smu_ready = emit_up(0)
        for p in range(NP):
            bis = None
            if p + 1 < NP:
                g_ac = emit_gate(p + 1)
                bis = make_bisect(p + 1, g_ac)
            emit_down_ln(p, smu_ready, filler=bis)
            if p + 1 < NP:
                smu_ready = emit_up(p + 1)

        em.wait("gpsimd", "ST", last_st)
        if x1fm_o is not None:
            em.wait("gpsimd", "FM", em.val("FM"))


def build_p0(n_tiles=NT, debug=False):
    nc = bass.Bass("TRN2", target_bir_lowering=False)
    io = {}
    io["xfm"] = nc.dram_tensor("xfm", [128, HC, n_tiles * 128], BF16, kind="ExternalInput").ap()
    io["xtm"] = nc.dram_tensor("xtm", [n_tiles * 128, D], BF16, kind="ExternalInput").ap()
    io["identb"] = nc.dram_tensor("identb", [128, 128], BF16, kind="ExternalInput").ap()
    wg_sh = nc.dram_tensor("wg_sh", [16, HC * I_FF], BF16, kind="ExternalInput").ap()
    wu_sh = nc.dram_tensor("wu_sh", [16, HC * I_FF], BF16, kind="ExternalInput").ap()
    wd_sh = nc.dram_tensor("wd_sh", [16, IC * D], BF16, kind="ExternalInput").ap()
    io["x1o"] = nc.dram_tensor("x1", [n_tiles * 128, D], BF16, kind="ExternalOutput").ap()
    if debug:
        io["s_dump"] = nc.dram_tensor("s_dump", [128, I_FF], F32, kind="ExternalOutput").ap()
        io["sc_dump"] = nc.dram_tensor("sc_dump", [128, 8], F32, kind="ExternalOutput").ap()
        io["smu_dump"] = nc.dram_tensor("smu_dump", [128, IC, 128], BF16, kind="ExternalOutput").ap()
        io["rr_dump"] = nc.dram_tensor("rr_dump", [128, D], F32, kind="ExternalOutput").ap()
    wg_in = nc.dram_tensor("wg_in", [16, HC * I_FF], BF16).ap()
    wu_in = nc.dram_tensor("wu_in", [16, HC * I_FF], BF16).ap()
    wd_in = nc.dram_tensor("wd_in", [16, IC * D], BF16).ap()
    wg_f = nc.dram_tensor("wg_f", [128, HC * I_FF], BF16, addr_space="Shared").ap()
    wu_f = nc.dram_tensor("wu_f", [128, HC * I_FF], BF16, addr_space="Shared").ap()
    wd_f = nc.dram_tensor("wd_f", [128, IC * D], BF16, addr_space="Shared").ap()
    io["wg_v"] = wg_f.rearrange("p (ib hc w) -> p ib hc w", ib=43, hc=HC)
    io["wu_v"] = wu_f.rearrange("p (ib hc w) -> p ib hc w", ib=43, hc=HC)
    io["wd_v"] = wd_f.rearrange("p (ob ic w) -> p ob ic w", ob=8, ic=IC)
    em = Emit(nc)
    emit_weight_gather(em, nc, [(wg_sh, wg_in, wg_f)])
    emit_weight_gather(em, nc, [(wu_sh, wu_in, wu_f)])
    emit_weight_gather(em, nc, [(wd_sh, wd_in, wd_f)])
    io["cc_wait"] = (1, 2, 3)
    emit_p0(nc, em, io, n_tiles=n_tiles, debug=debug)
    em.run()
    em.ctx.close()
    return nc


def p0_host_inputs(x, w_gate, w_up, w_down, n_tiles=NT):
    xb = cast_bf(x)
    xfm_full = np.ascontiguousarray(xb.T).reshape(HC, 128, 8192).transpose(1, 0, 2)
    wg = np.ascontiguousarray(cast_bf(np.asarray(w_gate).T).reshape(HC, 128, 43, 256)
                              .transpose(1, 2, 0, 3)).reshape(128, HC * I_FF)
    wu = np.ascontiguousarray(cast_bf(np.asarray(w_up).T).reshape(HC, 128, 43, 256)
                              .transpose(1, 2, 0, 3)).reshape(128, HC * I_FF)
    wd = np.ascontiguousarray(cast_bf(np.asarray(w_down).T).reshape(IC, 128, 8, 512)
                              .transpose(1, 2, 0, 3)).reshape(128, IC * D)
    identb = np.eye(128, dtype=np.float32).astype(BF)
    maps = []
    for c in range(NCORE):
        sl = core_token_idx(c, n_tiles)
        maps.append({
            "xfm": np.ascontiguousarray(xfm_full[:, :, sl]),
            "xtm": np.ascontiguousarray(xb[sl]),
            "identb": identb,
            "wg_sh": np.ascontiguousarray(wg[16 * c:16 * c + 16]),
            "wu_sh": np.ascontiguousarray(wu[16 * c:16 * c + 16]),
            "wd_sh": np.ascontiguousarray(wd[16 * c:16 * c + 16]),
        })
    return maps


def gather_tokens(results, key, n_tiles=NT):
    d1 = results[0][key].shape[-1]
    out = np.zeros((8192, d1), results[0][key].dtype)
    for c in range(NCORE):
        out[core_token_idx(c, n_tiles)] = results[c][key]
    return out



def emit_p1(nc, em, io, n_tiles=NT, debug=False):
    xfm = io["xfm"]; xtm = io["xtm"]; wb_v = io["wb_v"]
    aT = io["aT"]; bmat = io["bmat"]; rwt = io["rwt"]; rbp = io["rbp"]
    actv = io["actv"]; brow = io["brow"]; consts = io["consts"]; onesi = io["onesi"]
    x2o = io["x2o"]; x2fm_o = io.get("x2fm_o"); identb = io.get("identb")
    cc = io["cc_wait"]
    if debug:
        d_oh = io["d_oh"]; d_ssel = io["d_ssel"]; d_alpha = io["d_alpha"]
        d_zw = io["d_zw"]; d_lg = io["d_lg"]; d_rrt = io["d_rrt"]; d_z = io["d_z"]

    with ExitStack() as ctx:
        def sb(n, s, d):
            _NAME_SEQ[0] += 1
            return ctx.enter_context(nc.sbuf_tensor("%s_%d" % (n, _NAME_SEQ[0]), s, d))
        def psb(n, s, d=F32):
            _NAME_SEQ[0] += 1
            return ctx.enter_context(nc.psum_tensor("%s_%d" % (n, _NAME_SEQ[0]), s, d))

        xg = sb("xg", [128, HC, 512], BF16)
        ats = sb("ats", [128, HC, 128], BF16)
        cns = sb("cns", [128, 232], F32)
        onesf = sb("onesf", [128, 128], F32)
        rbps = sb("rbps", [16, 2], F32)
        rwts = sb("rwts", [128, HC, 16], BF16)
        acts = sb("acts", [1, 4], I32)
        brows = sb("brows", [1, D], F32)
        wbs = sb("wbs", [128, 2, HC, 512], BF16)
        bms = sb("bms", [128, 2, 512], BF16)
        z_sb = sb("z_sb", [128, 512], F32)
        lg_sb = sb("lg_sb", [16, 512], F32)
        small = sb("small", [128, 16], F32)
        eq8 = sb("eq8", [128, 4], F32)
        ssel = sb("ssel", [128, 32], F32)
        ssel_bf = sb("ssel_bf", [128, 32], BF16)
        e_sb = sb("e_sb", [4, 512], F32)
        rb4 = sb("rb4", [4, 512], F32)
        alpha = sb("alpha", [4, 512], F32)
        a32_sb = sb("a32_sb", [32, 512], F32)
        zw_bf = sb("zw_bf", [32, 512], BF16)
        bsel = sb("bsel", [32, 8, 512], BF16)
        x1t = sb("x1t", [128, D], BF16)
        rrt = sb("rrt", [128, D], F32)
        x2b = sb("x2b", [128, D], BF16)
        trash = sb("trashb", [128, D], BF16)
        ms = sb("ms", [128, 2], F32)
        va = sb("va", [128, 2], F32)

        ps_y = psb("ps_y", [128, 2, 512])
        ps_r1 = psb("ps_r1", [128, 512])
        ps_r2 = psb("ps_r2", [128, 512])
        ps_l = psb("ps_l", [128, 512])
        if x2fm_o is not None:
            identx = sb("identx", [128, 128], BF16)
            xfmob = sb("xfmob1", [128, HC, 128], BF16)
            ps_tA = psb("ps_tA1", [128, 1024], BF16)
            ps_tB = psb("ps_tB1", [128, 1024], BF16)
            ps_t = [ps_tA[:, 0:128], ps_tB[:, 0:128]]
            pst_user = [0, 0]

        IDENT = slice(0, 128)
        ONES = slice(128, 129)
        IOTA16 = slice(129, 130)
        C1M = slice(131, 163)
        E48 = slice(163, 195)
        E48ONE = slice(200, 232)

        ld0 = em.op("gpsimd", lambda e: e.dma_start(out=ats[:], in_=aT), inc="IDL", incval=16)
        if x2fm_o is not None:
            ld0 = em.op("gpsimd", lambda e: e.dma_start(out=identx[:], in_=identb),
                        inc="IDL", incval=16)
        ld0 = em.op("gpsimd", lambda e: e.dma_start(out=cns[:], in_=consts), inc="IDL", incval=16)
        ld0 = em.op("gpsimd", lambda e: e.dma_start(out=onesf[:], in_=onesi), inc="IDL", incval=16)
        ld0 = em.op("gpsimd", lambda e: e.dma_start(out=rbps[:], in_=rbp), inc="IDL", incval=16)
        ld0 = em.op("gpsimd", lambda e: e.dma_start(out=rwts[:], in_=rwt), inc="IDL", incval=16)
        ld0 = em.op("gpsimd", lambda e: e.dma_start(out=acts[:], in_=actv), inc="IDL", incval=16)
        ld0 = em.op("gpsimd", lambda e: e.dma_start(out=brows[:], in_=brow), inc="IDL", incval=16)
        em.wait("sync", "CC", cc)
        em.wait("vector", "IDL", ld0)
        em.wait("tensor", "IDL", ld0)
        em.wait("scalar", "IDL", ld0)

        dv_chain(em, lambda e: e.tensor_scalar(out=small[0:16, 4:5], in0=rbps[0:16, 1:2],
                                               scalar1=1e-12, scalar2=None, op0=ALU.add))
        v = em.val("DV")
        em.wait("scalar", "DV", v)
        a = em.op("scalar", lambda e: e.activation(out=small[0:16, 4:5], in_=small[0:16, 4:5],
                                                   func=AF.Ln), inc="AC")
        em.wait("vector", "AC", a)
        dv_chain(em, lambda e: e.tensor_tensor(out=small[0:16, 5:6], in0=rbps[0:16, 0:1],
                                               in1=small[0:16, 4:5], op=ALU.add))
        v = dv_chain(em, lambda e: e.tensor_copy(out=small[0:1, 0:4], in_=acts[:]))
        em.wait("tensor", "DV", v)
        pe = em.op("tensor", lambda e: e.matmul(ps_r2[0:16, 0:4], onesf[0:1, 0:16],
                                                small[0:1, 0:4], start=True, stop=True), inc="PE")
        em.wait("vector", "PE", pe)
        oh = sb("oh", [16, 4], F32)
        dv_chain(em, lambda e: e.tensor_scalar(out=oh[:], in0=ps_r2[0:16, 0:4],
                                               scalar1=cns[0:16, IOTA16], scalar2=None,
                                               op0=ALU.is_equal))
        pe = em.op("tensor", lambda e: e.matmul(ps_r2[:, 4:8], onesf[0:1, 0:128],
                                                small[0:1, 0:4], start=True, stop=True), inc="PE")
        em.wait("vector", "PE", pe)
        v = dv_chain(em, lambda e: e.tensor_scalar(out=eq8[:], in0=ps_r2[:, 4:8],
                                                   scalar1=cns[:, 130:131], scalar2=None,
                                                   op0=ALU.is_equal))
        em.wait("tensor", "DV", v)
        pe = em.op("tensor", lambda e: e.transpose(ps_r1[0:4, 0:128],
                                                   eq8[:], cns[:, IDENT]), inc="PE")
        em.wait("scalar", "PE", pe)
        eq8t = sb("eq8t", [4, 128], F32)
        a = em.op("scalar", lambda e: e.copy(out=eq8t[:], in_=ps_r1[0:4, 0:128]), inc="AC")
        em.wait("tensor", "AC", a)
        pe = em.op("tensor", lambda e: e.matmul(ps_r2[:, 8:40], eq8t[:], cns[0:4, E48ONE],
                                                start=True, stop=True), inc="PE")
        em.wait("vector", "PE", pe)
        dv_chain(em, lambda e: e.tensor_tensor(out=ssel[:], in0=ps_r2[:, 8:40],
                                               in1=cns[:, C1M], op=ALU.mult))
        v = dv_chain(em, lambda e: e.tensor_copy(out=ssel_bf[:], in_=ssel[:]))
        em.wait("tensor", "DV", v)
        bms_user = [0, 0]
        psr1_user = ("AC", a)
        for ob in range(8):
            k = ob % 2
            em.wait("gpsimd", "PE", bms_user[k])
            bl = em.op("gpsimd", lambda e, ob=ob, k=k: e.dma_start(
                out=bms[:, k, :], in_=bmat[:, ob * 512:(ob + 1) * 512]),
                inc="BM%d" % k, incval=16)
            em.wait("tensor", "BM%d" % k, bl)
            sn, sv = psr1_user
            em.wait("tensor", sn, sv)
            pe = em.op("tensor", lambda e, k=k: e.matmul(
                ps_r1[0:32, :], ssel_bf[:], bms[:, k, :], start=True, stop=True), inc="PE")
            bms_user[k] = pe
            em.wait("scalar", "PE", pe)
            a = em.op("scalar", lambda e, ob=ob: e.copy(out=bsel[:, ob, :],
                                                        in_=ps_r1[0:32, :]), inc="AC")
            psr1_user = ("AC", a)

        wbs_user = [0, 0]
        psy_user = [0, 0]
        x1t_user = 0
        x2b_user = 0
        xg_user = 0
        last_st = 0

        for g2 in range(n_tiles // 4):
            em.wait("gpsimd", "PE", xg_user)
            xl = em.op("gpsimd", lambda e, g2=g2: e.dma_start(
                out=xg[:], in_=xfm[:, :, g2 * 512:(g2 + 1) * 512]), inc="XC", incval=16)
            sn, sv = psr1_user
            em.wait("tensor", sn, sv)
            em.wait("tensor", "XC", xl)
            def mmz(e):
                for hc in range(HC):
                    m = e.matmul(ps_r1[:, :], ats[:, hc, :], xg[:, hc, :],
                                 start=(hc == 0), stop=(hc == HC - 1))
                return m
            pe = em.op("tensor", mmz, inc="PE")
            em.wait("scalar", "PE", pe)
            a = em.op("scalar", lambda e: e.copy(out=z_sb[:], in_=ps_r1[:]), inc="AC")
            psr1_user = ("AC", a)
            def mml(e):
                for hc in range(HC):
                    m = e.matmul(ps_l[0:16, :], rwts[:, hc, :], xg[:, hc, :],
                                 start=(hc == 0), stop=(hc == HC - 1))
                return m
            pe = em.op("tensor", mml, inc="PE")
            em.wait("vector", "PE", pe)
            vv = dv_chain(em, lambda e: e.tensor_scalar(out=lg_sb[:], in0=ps_l[0:16, :],
                                                        scalar1=small[0:16, 5:6], scalar2=None,
                                                        op0=ALU.add))
            em.wait("vector", "AC", em.val("AC"))
            v = dv_chain(em, lambda e: e.tensor_copy(out=lg_sb[:], in_=lg_sb[:]))
            em.wait("tensor", "DV", v)
            pe = em.op("tensor", lambda e: e.matmul(ps_r2[0:4, 0:512], oh[:], lg_sb[:],
                                                    start=True, stop=True), inc="PE")
            em.wait("scalar", "PE", pe)
            a = em.op("scalar", lambda e: e.activation(out=e_sb[:], in_=ps_r2[0:4, 0:512],
                                                       func=AF.Exp), inc="AC")
            em.wait("tensor", "AC", a)
            pe = em.op("tensor", lambda e: e.matmul(ps_r2[0:1, 0:512], onesf[0:4, 0:1],
                                                    e_sb[:], start=True, stop=True), inc="PE")
            em.wait("vector", "PE", pe)
            rd = small[0:1, 8:9]
            v = dv_chain(em, lambda e: e.reciprocal(out=lg_sb[0:1, 0:512],
                                                    in_=ps_r2[0:1, 0:512]))
            em.wait("tensor", "DV", v)
            pe = em.op("tensor", lambda e: e.matmul(ps_r2[0:4, 0:512], onesf[0:1, 0:4],
                                                    lg_sb[0:1, 0:512], start=True, stop=True),
                       inc="PE")
            em.wait("scalar", "PE", pe)
            a = em.op("scalar", lambda e: e.copy(out=rb4[:], in_=ps_r2[0:4, 0:512]), inc="AC")
            em.wait("vector", "AC", a)
            v = dv_chain(em, lambda e: e.tensor_tensor(out=alpha[:], in0=e_sb[:],
                                                       in1=rb4[:], op=ALU.mult))
            em.wait("tensor", "DV", v)
            pe = em.op("tensor", lambda e: e.matmul(ps_r2[0:32, 0:512], cns[0:4, E48],
                                                    alpha[:], start=True, stop=True), inc="PE")
            em.wait("scalar", "PE", pe)
            a = em.op("scalar", lambda e: e.copy(out=a32_sb[:], in_=ps_r2[0:32, 0:512]),
                      inc="AC")
            sn, sv = psr1_user
            em.wait("tensor", sn, sv)
            em.wait("tensor", "AC", em.val("AC"))
            pe = em.op("tensor", lambda e: e.matmul(ps_r1[0:32, :], ssel[:], z_sb[:],
                                                    start=True, stop=True), inc="PE")
            em.wait("vector", "PE", pe)
            em.wait("vector", "AC", a)
            v = dv_chain(em, lambda e: e.tensor_tensor(out=zw_bf[:], in0=ps_r1[0:32, :],
                                                       in1=a32_sb[:], op=ALU.mult))
            psr1_user = ("DV", v)
            zw_ready = v

            for tt in range(4):
                t = g2 * 4 + tt
                tsl = slice(tt * 128, (tt + 1) * 128)
                em.wait("gpsimd", "DV", x1t_user)
                xl2 = em.op("gpsimd", lambda e, t=t: e.dma_start(
                    out=x1t[:], in_=xtm[t * 128:(t + 1) * 128, :]), inc="XT", incval=16)
                radds = 0
                for ob in range(8):
                    k = ob % 2
                    obr = slice(ob * 512, (ob + 1) * 512)
                    em.wait("sync", "PE", wbs_user[k])
                    wl = em.op("sync", lambda e, ob=ob, k=k: e.dma_start(
                        out=wbs[:, k, :, :], in_=wb_v[:, ob, :, :]), inc="WB%d" % k, incval=16)
                    em.wait("tensor", "WB%d" % k, wl)
                    em.wait("tensor", "DV", psy_user[k])
                    if ob == 0:
                        em.wait("tensor", "DV", zw_ready)
                    def mmbase(e, k=k, obr=obr, tsl=tsl, ob=ob):
                        e.matmul(ps_y[:, k, :], onesf[0:1, 0:128], brows[0:1, obr],
                                 start=True, stop=False)
                        for hc in range(HC):
                            e.matmul(ps_y[:, k, :], xg[:, hc, tsl], wbs[:, k, hc, :],
                                     start=False, stop=False)
                        return e.matmul(ps_y[:, k, :], zw_bf[0:32, tsl], bsel[:, ob, :],
                                        start=False, stop=True)
                    pe = em.op("tensor", mmbase, inc="PE")
                    wbs_user[k] = pe
                    em.wait("vector", "PE", pe)
                    if ob == 0:
                        em.wait("vector", "XT", xl2)
                    v = dv_chain(em, lambda e, k=k, obr=obr: e.tensor_tensor(
                        out=rrt[:, obr], in0=ps_y[:, k, :], in1=x1t[:, obr], op=ALU.add))
                    psy_user[k] = v
                    radds = v
                x1t_user = radds
                if debug and g2 == 0 and tt == 0:
                    em.wait("gpsimd", "DV", radds)
                    em.op("gpsimd", lambda e: e.dma_start(out=d_oh, in_=oh[:]), inc="DBG", incval=16)
                    em.op("gpsimd", lambda e: e.dma_start(out=d_ssel, in_=ssel[:]), inc="DBG", incval=16)
                    em.op("gpsimd", lambda e: e.dma_start(out=d_alpha, in_=alpha[:]), inc="DBG", incval=16)
                    em.op("gpsimd", lambda e: e.dma_start(out=d_zw, in_=zw_bf[:]), inc="DBG", incval=16)
                    em.op("gpsimd", lambda e: e.dma_start(out=d_lg, in_=lg_sb[0:16, :]), inc="DBG", incval=16)
                    em.op("gpsimd", lambda e: e.dma_start(out=d_rrt, in_=rrt[:]), inc="DBG", incval=16)
                    em.op("gpsimd", lambda e: e.dma_start(out=d_z, in_=z_sb[:]), inc="DBG", incval=16)
                    em.wait("vector", "DBG", 112)
                em.wait("vector", "ST", x2b_user)
                lnv = emit_ln(em, rrt[:], trash[:], ms, va, x2b[:])
                em.wait("gpsimd", "DV", lnv)
                stv = em.op("gpsimd", lambda e, t=t: e.dma_start(
                    out=x2o[t * 128:(t + 1) * 128, :], in_=x2b[:]), inc="ST", incval=16)
                if x2fm_o is not None:
                    pst_user = emit_fm_out(em, x2b[:], identx, xfmob, ps_t, pst_user,
                                           x2fm_o, t * 128, lnv)
                x2b_user = em.val("ST")
                last_st = stv
            xg_user = em.val("PE")

        em.wait("gpsimd", "ST", last_st)
        if x2fm_o is not None:
            em.wait("gpsimd", "FM", em.val("FM"))


def build_p1(n_tiles=NT, debug=False):
    nc = bass.Bass("TRN2", target_bir_lowering=False)
    io = {}
    io["xfm"] = nc.dram_tensor("xfm", [128, HC, n_tiles * 128], BF16, kind="ExternalInput").ap()
    io["xtm"] = nc.dram_tensor("xtm", [n_tiles * 128, D], BF16, kind="ExternalInput").ap()
    wb_sh = nc.dram_tensor("wb_sh", [16, HC * D], BF16, kind="ExternalInput").ap()
    io["aT"] = nc.dram_tensor("aT", [128, HC, 128], BF16, kind="ExternalInput").ap()
    io["bmat"] = nc.dram_tensor("bmat", [128, D], BF16, kind="ExternalInput").ap()
    io["rwt"] = nc.dram_tensor("rwt", [128, HC, 16], BF16, kind="ExternalInput").ap()
    io["rbp"] = nc.dram_tensor("rbp", [16, 2], F32, kind="ExternalInput").ap()
    io["actv"] = nc.dram_tensor("actv", [1, 4], I32, kind="ExternalInput").ap()
    io["brow"] = nc.dram_tensor("brow", [1, D], F32, kind="ExternalInput").ap()
    io["consts"] = nc.dram_tensor("consts", [128, 232], F32, kind="ExternalInput").ap()
    io["onesi"] = nc.dram_tensor("onesi", [128, 128], F32, kind="ExternalInput").ap()
    io["x2o"] = nc.dram_tensor("x2", [n_tiles * 128, D], BF16, kind="ExternalOutput").ap()
    if debug:
        io["d_oh"] = nc.dram_tensor("d_oh", [16, 4], F32, kind="ExternalOutput").ap()
        io["d_ssel"] = nc.dram_tensor("d_ssel", [128, 32], F32, kind="ExternalOutput").ap()
        io["d_alpha"] = nc.dram_tensor("d_alpha", [4, 512], F32, kind="ExternalOutput").ap()
        io["d_zw"] = nc.dram_tensor("d_zw", [32, 512], BF16, kind="ExternalOutput").ap()
        io["d_lg"] = nc.dram_tensor("d_lg", [16, 512], F32, kind="ExternalOutput").ap()
        io["d_rrt"] = nc.dram_tensor("d_rrt", [128, D], F32, kind="ExternalOutput").ap()
        io["d_z"] = nc.dram_tensor("d_z", [128, 512], F32, kind="ExternalOutput").ap()
    wb_in = nc.dram_tensor("wb_in", [16, HC * D], BF16).ap()
    wb_f = nc.dram_tensor("wb_f", [128, HC * D], BF16).ap()
    io["wb_v"] = wb_f.rearrange("p (ob hc w) -> p ob hc w", ob=8, hc=HC)
    em = Emit(nc)
    io["cc_wait"] = emit_weight_gather(em, nc, [(wb_sh, wb_in, wb_f)])
    emit_p1(nc, em, io, n_tiles=n_tiles, debug=debug)
    em.run()
    em.ctx.close()
    return nc


def p1_consts():
    cns = np.zeros((128, 232), np.float32)
    cns[:, 0:128] = np.eye(128, dtype=np.float32)
    cns[:, 128] = 1.0
    cns[0:16, 129] = np.arange(16, dtype=np.float32)
    cns[:, 130] = np.arange(128, dtype=np.float32) // 8
    ii = np.arange(128)[:, None]
    jj = np.arange(32)[None, :]
    cns[:, 131:163] = (ii % 8 == jj % 8).astype(np.float32)
    e48 = np.zeros((4, 32), np.float32)
    for e in range(4):
        e48[e, e * 8:(e + 1) * 8] = 2.0
    cns[0:4, 163:195] = e48
    cns[0:4, 200:232] = e48 * 0.5
    return cns


def p1_host_inputs(x1, W_base, b_base, lora_A, lora_B, router_W, router_b,
                   domain_prior, active, n_tiles=NT):
    x1b = cast_bf(x1)
    xfm_full = np.ascontiguousarray(x1b.astype(np.float32).T.astype(BF)).reshape(
        HC, 128, 8192).transpose(1, 0, 2)
    wb = np.ascontiguousarray(cast_bf(np.asarray(W_base).T).reshape(HC, 128, 8, 512)
                              .transpose(1, 2, 0, 3)).reshape(128, HC * D)
    amat = np.asarray(lora_A, np.float32).reshape(128, D)
    aT = np.ascontiguousarray(cast_bf(amat.T).reshape(HC, 128, 128).transpose(1, 0, 2))
    bmat = cast_bf(np.transpose(np.asarray(lora_B, np.float32), (0, 2, 1)).reshape(128, D))
    rwt = np.ascontiguousarray(cast_bf(np.asarray(router_W, np.float32).T)
                               .reshape(HC, 128, 16).transpose(1, 0, 2))
    rbp = np.stack([np.asarray(router_b, np.float32),
                    np.asarray(domain_prior, np.float32)], axis=1)
    actv = np.asarray(active, np.int32).reshape(1, 4)
    brow = np.asarray(b_base, np.float32).reshape(1, D)
    cns = p1_consts()
    maps = []
    for c in range(NCORE):
        sl = core_token_idx(c, n_tiles)
        maps.append({
            "xfm": np.ascontiguousarray(xfm_full[:, :, sl]),
            "xtm": np.ascontiguousarray(x1b[sl]),
            "wb_sh": np.ascontiguousarray(wb[16 * c:16 * c + 16]),
            "aT": aT, "bmat": bmat, "rwt": rwt, "rbp": rbp,
            "actv": actv, "brow": brow, "consts": cns,
            "onesi": np.ones((128, 128), np.float32),
        })
    return maps



def emit_p2a(nc, em, io):
    NTOK = 8192
    NB = 16
    xg_src = io["xg_src"]
    ctx_dst = io["ctx_dst"]
    wq_sl = io["wq_sl"]; wk_sl = io["wk_sl"]; wv_sl = io["wv_sl"]
    cosf_d = io["cosf_d"]; sinf_d = io["sinf_d"]; dmask_d = io["dmask_d"]
    onesi = io["onesi"]; q_d = io["q_d"]
    xg_wait = io.get("xg_wait")

    with ExitStack() as ctx:
        def sb(n, s, d):
            _NAME_SEQ[0] += 1
            return ctx.enter_context(nc.sbuf_tensor("%s_%d" % (n, _NAME_SEQ[0]), s, d))
        def psb(n, s, d=F32):
            _NAME_SEQ[0] += 1
            return ctx.enter_context(nc.psum_tensor("%s_%d" % (n, _NAME_SEQ[0]), s, d))

        arenaB = sb("arenaB", [128, 40960], BF16)
        xg = arenaB[:, 0:16384].rearrange("p (hc w) -> p hc w", hc=HC)
        wqs = arenaB[:, 16384:32768].rearrange("p (hc w) -> p hc w", hc=HC)
        wks = arenaB[:, 32768:36864].rearrange("p (hc w) -> p hc w", hc=HC)
        wvs = arenaB[:, 36864:40960].rearrange("p (hc w) -> p hc w", hc=HC)
        wos = arenaB[:, 0:16384].rearrange("p (b h w) -> p b h w", b=2, h=16)
        cgt = arenaB[:, 16384:24576].rearrange("p (b hc w) -> p b hc w", b=2, hc=HC)
        x2t = arenaB[:, 24576:28672]
        x3b = arenaB[:, 28672:32768]
        trash = arenaB[:, 32768:36864]
        arenaF = sb("arenaF", [128, 6144], F32)
        qk5 = arenaF[:, 0:2560].rearrange("p (h w) -> p h w", h=5)
        rot5 = arenaF[:, 2560:5120].rearrange("p (h w) -> p h w", h=5)
        rtmp = arenaF[:, 5120:6144].rearrange("p (b w) -> p b w", b=2)
        rrt = arenaF[:, 0:4096]
        Ksb = sb("Ksb", [128, NTOK], BF16)
        Vsb = sb("Vsb", [128, 64, 128], BF16)
        cst = sb("cst", [128, 2, 512], F32)
        qrope = sb("qrope", [128, 512], BF16)
        onesf = sb("onesf", [128, 128], F32)
        dmsk = sb("dmsk", [128, 4, 512], BF16)
        qt = sb("qt", [128, 2, 512], BF16)
        e_bf = sb("e_bf", [128, 3, 512], BF16)
        ones_bf = sb("ones_bf", [128, 1], BF16)
        rd = sb("rd", [1, 512], F32)
        rbc = sb("rbc", [128, 512], F32)
        ctxn = sb("ctxn", [128, 2, 512], BF16)
        ms = sb("ms", [128, 2], F32)
        va = sb("va", [128, 2], F32)

        ps_q = psb("ps_q", [128, 2, 512])
        ps_s = psb("ps_s", [128, 3, 512])
        ps_c = psb("ps_c", [128, 2, 512])
        ps_den = psb("ps_den", [128, 512])

        ld0 = em.op("gpsimd", lambda e: e.dma_start(out=onesf[:], in_=onesi), inc="IDL", incval=16)
        ld0 = em.op("gpsimd", lambda e: e.dma_start(out=dmsk[:], in_=dmask_d), inc="IDL", incval=16)
        ld0 = em.op("gpsimd", lambda e: e.dma_start(out=wqs[:], in_=wq_sl), inc="IDL", incval=16)
        ld0 = em.op("gpsimd", lambda e: e.dma_start(out=wks[:], in_=wk_sl), inc="IDL", incval=16)
        ld0 = em.op("gpsimd", lambda e: e.dma_start(out=wvs[:], in_=wv_sl), inc="IDL", incval=16)
        em.wait("tensor", "IDL", ld0)
        em.wait("vector", "IDL", ld0)

        xg_user = 0
        cst_user = 0
        qk5_user = 0
        psq_user = [0, 0]
        qst = 0
        for tb in range(NB):
            tsl = slice(tb * 512, (tb + 1) * 512)
            em.wait("sync", "PE", xg_user)
            if tb == 0 and xg_wait is not None:
                em.wait("sync", xg_wait[0], xg_wait[1])
            xl = em.op("sync", lambda e, tb=tb: e.dma_start(
                out=xg[:], in_=xg_src(tb)), inc="XG", incval=16)
            em.wait("sync", "DV", cst_user)
            cl = em.op("sync", lambda e, tsl=tsl: e.dma_start(
                out=cst[:, 0, :], in_=cosf_d[:, tsl]), inc="CS0", incval=16)
            cl2 = em.op("sync", lambda e, tsl=tsl: e.dma_start(
                out=cst[:, 1, :], in_=sinf_d[:, tsl]), inc="CS1", incval=16)
            em.wait("gpsimd", "DV", qk5_user)
            em.wait("scalar", "DV", qk5_user)
            for hh in range(5):
                k = hh % 2
                em.wait("tensor", "XG", xl)
                em.wait("tensor", "AC", psq_user[k])
                def mmq(e, hh=hh, k=k):
                    w = wqs if hh < 4 else wks
                    c0 = hh * 128 if hh < 4 else 0
                    for hc in range(HC):
                        m = e.matmul(ps_q[:, k, :], w[:, hc, c0:c0 + 128] if hh < 4 else w[:, hc, :],
                                     xg[:, hc, :], start=(hc == 0), stop=(hc == HC - 1))
                    return m
                pe = em.op("tensor", mmq, inc="PE")
                em.wait("scalar", "PE", pe)
                ac = em.op("scalar", lambda e, hh=hh, k=k: e.copy(
                    out=qk5[:, hh, :], in_=ps_q[:, k, :]), inc="AC")
                psq_user[k] = ac
            qk_done = em.val("AC")
            em.wait("gpsimd", "AC", qk_done)
            r1 = em.op("gpsimd", lambda e: e.dma_start(
                out=rot5[0:64, :, :], in_=qk5[64:128, :, :]), inc="RT", incval=16)
            r1 = em.op("gpsimd", lambda e: e.dma_start(
                out=rot5[64:128, :, :], in_=qk5[0:64, :, :]), inc="RT", incval=16)
            em.wait("vector", "RT", r1)
            em.wait("vector", "CS0", cl)
            em.wait("vector", "CS1", cl2)
            for hh in range(5):
                dv_chain(em, lambda e, hh=hh: e.tensor_tensor(
                    out=rtmp[:, 0, :], in0=qk5[:, hh, :], in1=cst[:, 0, :], op=ALU.mult))
                dv_chain(em, lambda e, hh=hh: e.tensor_tensor(
                    out=rtmp[:, 1, :], in0=rot5[:, hh, :], in1=cst[:, 1, :], op=ALU.mult))
                if hh < 4:
                    dv_chain(em, lambda e: e.tensor_tensor(
                        out=rtmp[:, 0, :], in0=rtmp[:, 0, :], in1=rtmp[:, 1, :], op=ALU.add))
                    em.wait("vector", "QS", qst)
                    v = dv_chain(em, lambda e: e.tensor_scalar(
                        out=qrope[:], in0=rtmp[:, 0, :], scalar1=0.08838834764831845,
                        scalar2=None, op0=ALU.mult))
                    em.wait("gpsimd", "DV", v)
                    qst = em.op("gpsimd", lambda e, hh=hh, tsl=tsl: e.dma_start(
                        out=q_d[hh, :, tsl], in_=qrope[:]), inc="QS", incval=16)
                else:
                    v = dv_chain(em, lambda e, tsl=tsl: e.tensor_tensor(
                        out=Ksb[:, tsl], in0=rtmp[:, 0, :], in1=rtmp[:, 1, :], op=ALU.add))
            qk5_user = em.val("DV")
            cst_user = em.val("DV")
            for tt in range(4):
                k = tt % 2
                vt = tb * 4 + tt
                em.wait("tensor", "AC", psq_user[k])
                def mmv(e, tt=tt, k=k):
                    for hc in range(HC):
                        m = e.matmul(ps_q[:, k, :128], xg[:, hc, tt * 128:(tt + 1) * 128],
                                     wvs[:, hc, :], start=(hc == 0), stop=(hc == HC - 1))
                    return m
                pe = em.op("tensor", mmv, inc="PE")
                em.wait("scalar", "PE", pe)
                ac = em.op("scalar", lambda e, vt=vt, k=k: e.copy(
                    out=Vsb[:, vt, :], in_=ps_q[:, k, :128]), inc="AC")
                psq_user[k] = ac
            xg_user = em.val("PE")

        kv_done_ac = em.val("AC")
        kv_done_dv = em.val("DV")

        qt_user = [0, 0]
        pss_user = [0, 0, 0]
        ebf_user = [(0, 0), (0, 0), (0, 0)]
        psc_user = [0, 0]
        ctxn_user = [0, 0]
        psden_user = [0]
        cstore = 0
        em.wait("tensor", "AC", kv_done_ac)
        em.wait("tensor", "DV", kv_done_dv)
        em.wait("scalar", "QS", em.val("QS"))
        for g in range(NB):
            for h4 in range(4):
                gi = g * 4 + h4
                cp = gi % 2
                NK = 4 * g + 4
                k = gi % 2
                em.wait("scalar", "PE", qt_user[k])
                ql = em.op("scalar", lambda e, h4=h4, g=g, k=k: e.dma_start(
                    out=qt[:, k, :], in_=q_d[h4, :, g * 512:(g + 1) * 512]),
                    inc="QT%d" % k, incval=16)
                if gi == 0:
                    v_ones = em.op("vector", lambda e: e.memset(ones_bf[:], 1.0), inc="DV")
                    em.wait("tensor", "DV", v_ones)
                exp_ac = [0] * NK
                acc_dv = [0] * NK

                def emit_ctx(j):
                    sk = j % 3
                    if j >= NK - 4:
                        em.wait("tensor", "DV", acc_dv[j])
                    else:
                        em.wait("tensor", "AC", exp_ac[j])
                    if j == 0:
                        em.wait("tensor", "DV", psc_user[cp])
                        em.wait("tensor", "DV", psden_user[0])
                    pe = em.op("tensor", lambda e, j=j, sk=sk, cp=cp, NK=NK: e.matmul(
                        ps_c[:, cp, :], Vsb[:, j, :], e_bf[:, sk, :],
                        start=(j == 0), stop=(j == NK - 1)), inc="PE")
                    pe = em.op("tensor", lambda e, j=j, sk=sk, NK=NK: e.matmul(
                        ps_den[0:1, :], ones_bf[:, 0:1], e_bf[:, sk, :],
                        start=(j == 0), stop=(j == NK - 1)), inc="PE")
                    ebf_user[sk] = (pe, acc_dv[j])

                for j in range(NK):
                    sk = j % 3
                    em.wait("tensor", "AC", pss_user[sk])
                    if j == 0:
                        em.wait("tensor", "QT%d" % k, ql)
                    pe_s = em.op("tensor", lambda e, j=j, sk=sk, k=k: e.matmul(
                        ps_s[:, sk, :], Ksb[:, j * 128:(j + 1) * 128],
                        qt[:, k, :], start=True, stop=True), inc="PE")
                    if j >= 1:
                        emit_ctx(j - 1)
                    em.wait("scalar", "PE", pe_s)
                    epe, edv = ebf_user[sk]
                    em.wait("scalar", "PE", epe)
                    em.wait("scalar", "DV", edv)
                    ac = em.op("scalar", lambda e, sk=sk: e.activation(
                        out=e_bf[:, sk, :], in_=ps_s[:, sk, :], func=AF.Exp), inc="AC")
                    pss_user[sk] = ac
                    exp_ac[j] = ac
                    if j >= NK - 4:
                        em.wait("vector", "AC", ac)
                        dd = j - (NK - 4)
                        acc_dv[j] = dv_chain(em, lambda e, sk=sk, dd=dd: e.tensor_tensor(
                            out=e_bf[:, sk, :], in0=e_bf[:, sk, :], in1=dmsk[:, dd, :],
                            op=ALU.mult))
                emit_ctx(NK - 1)
                qt_user[k] = em.val("PE")
                den_pe = em.val("PE")
                em.wait("tensor", "AC", pss_user[1])
                em.wait("vector", "PE", den_pe)
                v = dv_chain(em, lambda e: e.reciprocal(out=rd[:], in_=ps_den[0:1, :]))
                psden_user[0] = v
                em.wait("tensor", "DV", v)
                pe = em.op("tensor", lambda e: e.matmul(ps_s[:, 1, :], onesf[0:1, :],
                                                        rd[:], start=True, stop=True), inc="PE")
                em.wait("scalar", "PE", pe)
                ac = em.op("scalar", lambda e: e.copy(out=rbc[:], in_=ps_s[:, 1, :]), inc="AC")
                pss_user = [pss_user[0], ac, pss_user[2]]
                em.wait("vector", "AC", ac)
                em.wait("vector", "CT%d" % cp, ctxn_user[cp])
                cv = dv_chain(em, lambda e, cp=cp: e.tensor_tensor(
                    out=ctxn[:, cp, :], in0=ps_c[:, cp, :], in1=rbc[:], op=ALU.mult))
                psc_user[cp] = cv
                em.wait("gpsimd", "DV", cv)
                cstore = em.op("gpsimd", lambda e, h4=h4, g=g, cp=cp: e.dma_start(
                    out=ctx_dst(g, h4),
                    in_=ctxn[:, cp, :]), inc="CT%d" % cp, incval=16)
                ctxn_user[cp] = cstore

        em.wait("gpsimd", "CT0", em.val("CT0"))
        em.wait("gpsimd", "CT1", em.val("CT1"))


def build_p2():
    nc = bass.Bass("TRN2", target_bir_lowering=False)
    NTOK = 8192
    io = {}
    xfm = nc.dram_tensor("xfm", [128, HC, NTOK], BF16, kind="ExternalInput").ap()
    io["wq_sl"] = nc.dram_tensor("wq_sl", [128, HC, 512], BF16, kind="ExternalInput").ap()
    io["wk_sl"] = nc.dram_tensor("wk_sl", [128, HC, 128], BF16, kind="ExternalInput").ap()
    io["wv_sl"] = nc.dram_tensor("wv_sl", [128, HC, 128], BF16, kind="ExternalInput").ap()
    io["cosf_d"] = nc.dram_tensor("cosf_d", [128, NTOK], F32, kind="ExternalInput").ap()
    io["sinf_d"] = nc.dram_tensor("sinf_d", [128, NTOK], F32, kind="ExternalInput").ap()
    io["dmask_d"] = nc.dram_tensor("dmask_d", [128, 4, 512], BF16, kind="ExternalInput").ap()
    io["onesi"] = nc.dram_tensor("onesi", [128, 128], F32, kind="ExternalInput").ap()
    ctxo = nc.dram_tensor("ctxo", [512, NTOK], BF16, kind="ExternalOutput").ap()
    io["q_d"] = nc.dram_tensor("q_d", [4, 128, NTOK], BF16).ap()
    io["xg_src"] = lambda tb: xfm[:, :, tb * 512:(tb + 1) * 512]
    io["ctx_dst"] = lambda g, h4: ctxo[h4 * 128:(h4 + 1) * 128, g * 512:(g + 1) * 512]
    em = Emit(nc)
    emit_p2a(nc, em, io)
    em.run()
    em.ctx.close()
    return nc


def emit_p2b(nc, em, io):
    ctxsl = io["ctxsl"]; xtm = io["xtm"]; wo_v = io["wo_v"]
    x3o = io["x3o"]; x3fm_o = io.get("x3fm_o"); identb = io.get("identb")
    cc = io["cc_wait"]
    ctx_wait = io.get("ctx_wait")

    with ExitStack() as ctx:
        def sb(n, s, d):
            _NAME_SEQ[0] += 1
            return ctx.enter_context(nc.sbuf_tensor("%s_%d" % (n, _NAME_SEQ[0]), s, d))
        def psb(n, s, d=F32):
            _NAME_SEQ[0] += 1
            return ctx.enter_context(nc.psum_tensor("%s_%d" % (n, _NAME_SEQ[0]), s, d))
        wos = sb("wos", [128, 2, 16, 512], BF16)
        cgt = sb("cgt", [128, 2, HC, 128], BF16)
        x2t = sb("x2t", [128, D], BF16)
        rrt = sb("rrt", [128, D], F32)
        x3b = sb("x3b", [128, D], BF16)
        trash = sb("trashd", [128, D], BF16)
        ms = sb("ms", [128, 2], F32)
        va = sb("va", [128, 2], F32)
        ps_o = psb("ps_o", [128, 2, 512])
        if x3fm_o is not None:
            identx = sb("identx2", [128, 128], BF16)
            xfmob = sb("xfmob2", [128, HC, 128], BF16)
            ps_tA = psb("ps_tA2", [128, 1024], BF16)
            ps_tB = psb("ps_tB2", [128, 1024], BF16)
            ps_t = [ps_tA[:, 0:128], ps_tB[:, 0:128]]
            pst_user = [0, 0]
            idl = em.op("gpsimd", lambda e: e.dma_start(out=identx[:], in_=identb),
                        inc="IDL", incval=16)
            em.wait("tensor", "IDL", idl)

        em.wait("sync", "CC", cc)
        em.wait("scalar", "CC", cc)
        if ctx_wait is not None:
            em.wait("scalar", ctx_wait[0], ctx_wait[1])
            em.wait("gpsimd", ctx_wait[0], ctx_wait[1])

        cgt_user = [0, 0]
        wos_user = [0, 0]
        pso_user = [0, 0]
        x2t_user = 0
        x3b_user = 0
        last_st = 0
        for t in range(NT):
            k = t % 2
            em.wait("scalar", "PE", cgt_user[k])
            gl = em.op("scalar", lambda e, t=t, k=k: e.dma_start(
                out=cgt[:, k, :, :], in_=ctxsl[:, :, t * 128:(t + 1) * 128]),
                inc="CG%d" % k, incval=16)
            em.wait("gpsimd", "DV", x2t_user)
            xl2 = em.op("gpsimd", lambda e, t=t: e.dma_start(
                out=x2t[:], in_=xtm[t * 128:(t + 1) * 128, :]), inc="XT", incval=16)
            radds = 0
            for ob in range(8):
                ok = ob % 2
                obr = slice(ob * 512, (ob + 1) * 512)
                for half in range(2):
                    em.wait("sync", "PE", wos_user[half])
                    wl = em.op("sync", lambda e, ob=ob, half=half: e.dma_start(
                        out=wos[:, half, :, :], in_=wo_v[:, ob, half * 16:(half + 1) * 16, :]),
                        inc="WO%d" % half, incval=16)
                    em.wait("tensor", "WO%d" % half, wl)
                    if half == 0:
                        em.wait("tensor", "CG%d" % k, gl)
                        em.wait("tensor", "DV", pso_user[ok])
                    def mmo(e, k=k, ok=ok, half=half):
                        for hc2 in range(16):
                            hc = half * 16 + hc2
                            m = e.matmul(ps_o[:, ok, :], cgt[:, k, hc, :], wos[:, half, hc2, :],
                                         start=(hc == 0), stop=(hc == HC - 1))
                        return m
                    pe = em.op("tensor", mmo, inc="PE")
                    wos_user[half] = pe
                em.wait("vector", "PE", pe)
                if ob == 0:
                    em.wait("vector", "XT", xl2)
                v = dv_chain(em, lambda e, ok=ok, obr=obr: e.tensor_tensor(
                    out=rrt[:, obr], in0=ps_o[:, ok, :], in1=x2t[:, obr], op=ALU.add))
                pso_user[ok] = v
                radds = v
            cgt_user[k] = em.val("PE")
            x2t_user = radds
            em.wait("vector", "ST", x3b_user)
            lnv = emit_ln(em, rrt[:], trash[:], ms, va, x3b[:])
            em.wait("gpsimd", "DV", lnv)
            stv = em.op("gpsimd", lambda e, t=t: e.dma_start(
                out=x3o[t * 128:(t + 1) * 128, :], in_=x3b[:]), inc="ST", incval=16)
            if x3fm_o is not None:
                pst_user = emit_fm_out(em, x3b[:], identx, xfmob, ps_t, pst_user,
                                       x3fm_o, t * 128, lnv)
            x3b_user = em.val("ST")
            last_st = stv
        em.wait("gpsimd", "ST", last_st)
        if x3fm_o is not None:
            em.wait("gpsimd", "FM", em.val("FM"))


def build_p2b():
    nc = bass.Bass("TRN2", target_bir_lowering=False)
    io = {}
    io["ctxsl"] = nc.dram_tensor("ctxsl", [128, HC, NT * 128], BF16, kind="ExternalInput").ap()
    io["xtm"] = nc.dram_tensor("xtm", [NT * 128, D], BF16, kind="ExternalInput").ap()
    wo_sh = nc.dram_tensor("wo_sh", [16, HC * D], BF16, kind="ExternalInput").ap()
    io["x3o"] = nc.dram_tensor("x3", [NT * 128, D], BF16, kind="ExternalOutput").ap()
    wo_in = nc.dram_tensor("wo_in", [16, HC * D], BF16).ap()
    wo_f = nc.dram_tensor("wo_f", [128, HC * D], BF16).ap()
    io["wo_v"] = wo_f.rearrange("p (ob hc w) -> p ob hc w", ob=8, hc=HC)
    em = Emit(nc)
    io["cc_wait"] = emit_weight_gather(em, nc, [(wo_sh, wo_in, wo_f)])
    emit_p2b(nc, em, io)
    em.run()
    em.ctx.close()
    return nc


def p2b_host_inputs(ctx_full, x2, Wo):
    x2b = cast_bf(x2)
    wo = np.ascontiguousarray(cast_bf(np.asarray(Wo).T).reshape(HC, 128, 8, 512)
                              .transpose(1, 2, 0, 3)).reshape(128, HC * D)
    cv = np.ascontiguousarray(np.asarray(ctx_full).reshape(HC, 128, 8192)
                              .transpose(1, 0, 2))
    maps = []
    for c in range(NCORE):
        sl = core_token_idx(c)
        maps.append({
            "ctxsl": np.ascontiguousarray(cv[:, :, sl]),
            "xtm": np.ascontiguousarray(x2b[sl]),
            "wo_sh": np.ascontiguousarray(wo[16 * c:16 * c + 16]),
        })
    return maps


def rope_tables():
    import math
    DH = 128
    inv = 1.0 / 500000.0 ** (np.arange(0, DH, 2, dtype=np.float32) / DH)
    wav = np.float32(2.0 * math.pi) / inv
    inv2 = np.where(wav > 8192.0, inv / np.float32(32.0), inv)
    smooth = (np.float32(8192.0) / wav - np.float32(1.0)) / np.float32(3.0)
    smoothed = (np.float32(1.0) - smooth) * inv2 / np.float32(32.0) + smooth * inv2
    is_med = ~(wav < 8192.0) & ~(wav > 2048.0)
    inv3 = np.where(is_med, smoothed, inv2)
    pos = np.arange(8192, dtype=np.float32)
    fr = pos[:, None] * inv3[None, :]
    cos = np.cos(fr).astype(np.float32)
    sin = np.sin(fr).astype(np.float32)
    cosf = np.concatenate([cos, cos], axis=1).T.copy()
    sinf = np.concatenate([-sin, sin], axis=1).T.copy()
    return cosf, sinf


def p2_host_inputs(x2, Wq, Wk, Wv, Wo):
    x2b = cast_bf(x2)
    xfm_full = np.ascontiguousarray(
        x2b.astype(np.float32).T.astype(BF)).reshape(HC, 128, 8192).transpose(1, 0, 2)
    xfm_full = np.ascontiguousarray(xfm_full)
    wqT = np.ascontiguousarray(cast_bf(np.asarray(Wq).T).reshape(HC, 128, D).transpose(1, 0, 2))
    wkT = np.ascontiguousarray(cast_bf(np.asarray(Wk).T).reshape(HC, 128, 1024).transpose(1, 0, 2))
    wvT = np.ascontiguousarray(cast_bf(np.asarray(Wv).T).reshape(HC, 128, 1024).transpose(1, 0, 2))
    wo = np.ascontiguousarray(cast_bf(np.asarray(Wo).T).reshape(HC, 128, D)
                              .transpose(1, 0, 2)).reshape(128, HC * D)
    cosf, sinf = rope_tables()
    kk = np.arange(128)[:, None]
    qq = np.arange(512)[None, :]
    dm = np.stack([(qq >= 128 * d + kk) for d in range(4)], axis=1).astype(np.float32).astype(BF)
    ones = np.ones((128, 128), np.float32)
    maps = []
    for c in range(NCORE):
        sl = core_token_idx(c)
        maps.append({
            "xfm": xfm_full,
            "xtm": np.ascontiguousarray(x2b[sl]),
            "wq_sl": np.ascontiguousarray(wqT[:, :, 512 * c:512 * c + 512]),
            "wk_sl": np.ascontiguousarray(wkT[:, :, 128 * c:128 * c + 128]),
            "wv_sl": np.ascontiguousarray(wvT[:, :, 128 * c:128 * c + 128]),
            "cosf_d": cosf, "sinf_d": sinf, "dmask_d": np.ascontiguousarray(dm),
            "onesi": ones,
        })
    return maps



def emit_p3(nc, em, io, n_tiles=NT):
    xfm = io["xfm"]; xtm = io["xtm"]; rts_d = io["rts_d"]; a1_d = io["a1_d"]
    b1_d = io["b1_d"]; e816_d = io["e816_d"]; identf_d = io["identf_d"]
    x4o = io["x4o"]

    with ExitStack() as ctx:
        def sb(n, s, d):
            _NAME_SEQ[0] += 1
            return ctx.enter_context(nc.sbuf_tensor("%s_%d" % (n, _NAME_SEQ[0]), s, d))
        def psb(n, s, d=F32):
            _NAME_SEQ[0] += 1
            return ctx.enter_context(nc.psum_tensor("%s_%d" % (n, _NAME_SEQ[0]), s, d))
        xg = sb("xg", [128, HC, 512], BF16)
        rts = sb("rts", [128, HC, 8], BF16)
        a1s = sb("a1s", [128, HC, 128], BF16)
        b1m = sb("b1m", [128, D], BF16)
        e816 = sb("e816", [8, 128], F32)
        identf = sb("identf", [128, 128], F32)
        lg8 = sb("lg8", [8, 512], F32)
        lgt = sb("lgt", [128, 8], F32)
        val = sb("val", [128, 2], F32)
        rw_tm = sb("rw_tm", [128, 8], F32)
        rw_fm = sb("rw_fm", [8, 512], F32)
        rwe = sb("rwe", [128, 512], F32)
        zw_bf = sb("zw_bf", [128, 512], BF16)
        xt = sb("xt", [128, D], BF16)
        rrt = sb("rrt", [128, D], F32)
        x4f = sb("x4f", [128, D], F32)
        trash = sb("trashe", [128, D], BF16)
        ms = sb("ms", [128, 2], F32)
        va = sb("va", [128, 2], F32)
        ps_z = psb("ps_z", [128, 512])
        ps_l = psb("ps_l", [128, 512])
        ps_t = psb("ps_t", [128, 512])
        ps_f = psb("ps_f", [128, 2, 512])

        ld0 = 0
        for (dst, srcp) in ((rts, rts_d), (a1s, a1_d), (b1m, b1_d),
                            (e816, e816_d), (identf, identf_d)):
            ld0 = em.op("gpsimd", lambda e, dst=dst, srcp=srcp: e.dma_start(
                out=dst[:], in_=srcp), inc="IDL", incval=16)
        for eng in ("tensor", "vector", "scalar"):
            em.wait(eng, "IDL", ld0)

        psz_user = ("AC", 0)
        psl_user = ("AC", 0)
        pst_user = ("AC", 0)
        psf_user = [0, 0]
        xt_user = 0
        x4_user = 0
        xg_user = 0
        last_st = 0
        for g2 in range(n_tiles // 4):
            em.wait("gpsimd", "PE", xg_user)
            xl = em.op("gpsimd", lambda e, g2=g2: e.dma_start(
                out=xg[:], in_=xfm[:, :, g2 * 512:(g2 + 1) * 512]), inc="XC", incval=16)
            em.wait("tensor", "XC", xl)
            sn, sv = psl_user
            em.wait("tensor", sn, sv)
            def mml(e):
                for hc in range(HC):
                    m = e.matmul(ps_l[0:8, :], rts[:, hc, :], xg[:, hc, :],
                                 start=(hc == 0), stop=(hc == HC - 1))
                return m
            pe = em.op("tensor", mml, inc="PE")
            em.wait("scalar", "PE", pe)
            a = em.op("scalar", lambda e: e.copy(out=lg8[:], in_=ps_l[0:8, :]), inc="AC")
            psl_user = ("AC", a)
            sn, sv = psz_user
            em.wait("tensor", sn, sv)
            def mmz(e):
                for hc in range(HC):
                    m = e.matmul(ps_z[:, :], a1s[:, hc, :], xg[:, hc, :],
                                 start=(hc == 0), stop=(hc == HC - 1))
                return m
            pez = em.op("tensor", mmz, inc="PE")
            em.wait("tensor", "AC", a)
            for q in range(4):
                sn, sv = pst_user
                em.wait("tensor", sn, sv)
                pe = em.op("tensor", lambda e, q=q: e.transpose(
                    ps_t[0:128, 0:8], lg8[:, q * 128:(q + 1) * 128], identf[0:8, 0:8]), inc="PE")
                em.wait("scalar", "PE", pe)
                a = em.op("scalar", lambda e: e.copy(out=lgt[:], in_=ps_t[0:128, 0:8]),
                          inc="AC")
                em.wait("vector", "AC", a)
                dv_chain(em, lambda e: e.reduce_max(val[:, 0:1], lgt[:], axis=AX.X))
                dv_chain(em, lambda e: e.tensor_scalar(
                    out=rw_tm[:], in0=lgt[:], scalar1=val[:, 0:1], scalar2=None,
                    op0=ALU.is_equal))
                v = dv_chain(em, lambda e: e.tensor_scalar(
                    out=rw_tm[:], in0=rw_tm[:], scalar1=val[:, 0:1], scalar2=None,
                    op0=ALU.mult))
                em.wait("tensor", "DV", v)
                pe = em.op("tensor", lambda e, q=q: e.transpose(
                    ps_t[0:8, 0:128], rw_tm[:], identf[:]), inc="PE")
                em.wait("scalar", "PE", pe)
                a = em.op("scalar", lambda e, q=q: e.copy(
                    out=rw_fm[:, q * 128:(q + 1) * 128], in_=ps_t[0:8, 0:128]), inc="AC")
                pst_user = ("AC", a)
            em.wait("tensor", "AC", a)
            sn, sv = pst_user
            pe = em.op("tensor", lambda e: e.matmul(ps_t[:, :], e816[:], rw_fm[:],
                                                    start=True, stop=True), inc="PE")
            pst_user = ("PE", pe)
            em.wait("scalar", "PE", pe)
            a = em.op("scalar", lambda e: e.copy(out=rwe[:], in_=ps_t[:, :]), inc="AC")
            pst_user = ("AC", a)
            em.wait("vector", "AC", a)
            em.wait("vector", "PE", pez)
            v = dv_chain(em, lambda e: e.tensor_tensor(
                out=zw_bf[:], in0=ps_z[:, :], in1=rwe[:], op=ALU.mult))
            psz_user = ("DV", v)
            zw_ready = v
            for tt in range(4):
                t = g2 * 4 + tt
                tsl = slice(tt * 128, (tt + 1) * 128)
                em.wait("gpsimd", "DV", xt_user)
                xl2 = em.op("gpsimd", lambda e, t=t: e.dma_start(
                    out=xt[:], in_=xtm[t * 128:(t + 1) * 128, :]), inc="XT", incval=16)
                radds = 0
                for ob in range(8):
                    k = ob % 2
                    obr = slice(ob * 512, (ob + 1) * 512)
                    em.wait("tensor", "DV", psf_user[k])
                    if ob == 0 and tt == 0:
                        em.wait("tensor", "DV", zw_ready)
                    pe = em.op("tensor", lambda e, k=k, obr=obr, tsl=tsl: e.matmul(
                        ps_f[:, k, :], zw_bf[:, tsl], b1m[:, obr],
                        start=True, stop=True), inc="PE")
                    em.wait("vector", "PE", pe)
                    if ob == 0:
                        em.wait("vector", "XT", xl2)
                    v = dv_chain(em, lambda e, k=k, obr=obr: e.tensor_tensor(
                        out=rrt[:, obr], in0=ps_f[:, k, :], in1=xt[:, obr], op=ALU.add))
                    psf_user[k] = v
                    radds = v
                xt_user = radds
                em.wait("vector", "ST", x4_user)
                lnv = emit_ln(em, rrt[:], trash[:], ms, va, x4f[:])
                em.wait("gpsimd", "DV", lnv)
                stv = em.op("gpsimd", lambda e, t=t: e.dma_start(
                    out=x4o[t * 128:(t + 1) * 128, :], in_=x4f[:]), inc="ST", incval=16)
                x4_user = stv
                last_st = stv
            xg_user = em.val("PE")
        em.wait("gpsimd", "ST", last_st)


def build_p3(n_tiles=NT):
    nc = bass.Bass("TRN2", target_bir_lowering=False)
    io = {}
    io["xfm"] = nc.dram_tensor("xfm", [128, HC, n_tiles * 128], BF16, kind="ExternalInput").ap()
    io["xtm"] = nc.dram_tensor("xtm", [n_tiles * 128, D], BF16, kind="ExternalInput").ap()
    io["rts_d"] = nc.dram_tensor("rts_d", [128, HC, 8], BF16, kind="ExternalInput").ap()
    io["a1_d"] = nc.dram_tensor("a1_d", [128, HC, 128], BF16, kind="ExternalInput").ap()
    io["b1_d"] = nc.dram_tensor("b1_d", [128, D], BF16, kind="ExternalInput").ap()
    io["e816_d"] = nc.dram_tensor("e816_d", [8, 128], F32, kind="ExternalInput").ap()
    io["identf_d"] = nc.dram_tensor("identf_d", [128, 128], F32, kind="ExternalInput").ap()
    io["x4o"] = nc.dram_tensor("x4", [n_tiles * 128, D], F32, kind="ExternalOutput").ap()
    em = Emit(nc)
    emit_p3(nc, em, io, n_tiles=n_tiles)
    em.run()
    em.ctx.close()
    return nc


def p3_host_inputs(x3, route_w, A, Bw, n_tiles=NT):
    x3b = cast_bf(x3)
    xfm_full = np.ascontiguousarray(
        x3b.astype(np.float32).T.astype(BF)).reshape(HC, 128, 8192).transpose(1, 0, 2)
    rts = np.ascontiguousarray(cast_bf(np.asarray(route_w, np.float32).T)
                               .reshape(HC, 128, 8).transpose(1, 0, 2))
    a1 = np.ascontiguousarray(cast_bf(np.asarray(A, np.float32).reshape(128, D).T)
                              .reshape(HC, 128, 128).transpose(1, 0, 2))
    b1 = cast_bf(np.transpose(np.asarray(Bw, np.float32), (0, 2, 1)).reshape(128, D))
    e816 = np.zeros((8, 128), np.float32)
    for e in range(8):
        e816[e, e * 16:(e + 1) * 16] = 1.0
    identf = np.eye(128, dtype=np.float32)
    maps = []
    for c in range(NCORE):
        sl = core_token_idx(c, n_tiles)
        maps.append({
            "xfm": np.ascontiguousarray(xfm_full[:, :, sl]),
            "xtm": np.ascontiguousarray(x3b[sl]),
            "rts_d": rts, "a1_d": a1, "b1_d": b1, "e816_d": e816,
            "identf_d": identf,
        })
    return maps



def build_fused():
    nc = bass.Bass("TRN2", target_bir_lowering=False)
    NTOK = 8192
    ei = lambda n, s, d: nc.dram_tensor(n, s, d, kind="ExternalInput").ap()
    xfm_i = ei("xfm", [128, HC, NT * 128], BF16)
    xtm_i = ei("xtm", [NT * 128, D], BF16)
    identb = ei("identb", [128, 128], BF16)
    wg_sh = ei("wg_sh", [16, HC * I_FF], BF16)
    wu_sh = ei("wu_sh", [16, HC * I_FF], BF16)
    wd_sh = ei("wd_sh", [16, IC * D], BF16)
    wb_sh = ei("wb_sh", [16, HC * D], BF16)
    wo_sh = ei("wo_sh", [16, HC * D], BF16)
    aT = ei("aT", [128, HC, 128], BF16)
    bmat = ei("bmat", [128, D], BF16)
    rwt = ei("rwt", [128, HC, 16], BF16)
    rbp = ei("rbp", [16, 2], F32)
    actv = ei("actv", [1, 4], I32)
    brow = ei("brow", [1, D], F32)
    consts = ei("consts", [128, 232], F32)
    onesi = ei("onesi", [128, 128], F32)
    wq_sl = ei("wq_sl", [128, HC, 512], BF16)
    wk_sl = ei("wk_sl", [128, HC, 128], BF16)
    wv_sl = ei("wv_sl", [128, HC, 128], BF16)
    cosf_d = ei("cosf_d", [128, NTOK], F32)
    sinf_d = ei("sinf_d", [128, NTOK], F32)
    dmask_d = ei("dmask_d", [128, 4, 512], BF16)
    rts_d = ei("rts_d", [128, HC, 8], BF16)
    a1_d = ei("a1_d", [128, HC, 128], BF16)
    b1_d = ei("b1_d", [128, D], BF16)
    e816_d = ei("e816_d", [8, 128], F32)
    identf_d = ei("identf_d", [128, 128], F32)
    x4o = nc.dram_tensor("x4", [NT * 128, D], F32, kind="ExternalOutput").ap()

    it = lambda n, s, d: nc.dram_tensor(n, s, d).ap()
    wg_in = it("wg_in", [16, HC * I_FF], BF16)
    wu_in = it("wu_in", [16, HC * I_FF], BF16)
    wd_in = it("wd_in", [16, IC * D], BF16)
    wb_in = it("wb_in", [16, HC * D], BF16)
    wo_in = it("wo_in", [16, HC * D], BF16)
    its = lambda n, s, d: nc.dram_tensor(n, s, d, addr_space="Shared").ap()
    wg_f = its("wg_f", [128, HC * I_FF], BF16)
    wu_f = its("wu_f", [128, HC * I_FF], BF16)
    wd_f = its("wd_f", [128, IC * D], BF16)
    wb_f = its("wb_f", [128, HC * D], BF16)
    wo_f = its("wo_f", [128, HC * D], BF16)
    x1_d = it("x1_d", [NT * 128, D], BF16)
    x1fm_d = it("x1fm_d", [128, HC, NT * 128], BF16)
    x2_d = it("x2_d", [NT * 128, D], BF16)
    x2ag_in = it("x2ag_in", [4096, NT * 128], BF16)
    x2ag_out = its("x2ag_out", [8 * 4096, NT * 128], BF16)
    q_d = it("q_d", [4, 128, NTOK], BF16)
    cx_in = it("cx_in", [4096, NT * 128], BF16)
    cx_out = it("cx_out", [4096, NT * 128], BF16)
    x3_d = it("x3_d", [NT * 128, D], BF16)
    x3fm_d = it("x3fm_d", [128, HC, NT * 128], BF16)

    em = Emit(nc)
    cc3 = emit_weight_gather(em, nc, [
        (wg_sh, wg_in, wg_f), (wu_sh, wu_in, wu_f), (wd_sh, wd_in, wd_f)])
    cc4 = emit_weight_gather(em, nc, [(wb_sh, wb_in, wb_f)])
    cc5 = emit_weight_gather(em, nc, [(wo_sh, wo_in, wo_f)])
    cc3 = (1, 2, 3)

    io0 = {
        "xfm": xfm_i, "xtm": xtm_i, "identb": identb,
        "wg_v": wg_f.rearrange("p (ib hc w) -> p ib hc w", ib=43, hc=HC),
        "wu_v": wu_f.rearrange("p (ib hc w) -> p ib hc w", ib=43, hc=HC),
        "wd_v": wd_f.rearrange("p (ob ic w) -> p ob ic w", ob=8, ic=IC),
        "x1o": x1_d, "x1fm_o": x1fm_d, "cc_wait": cc3,
    }
    emit_p0(nc, em, io0)
    em.barrier()

    io1 = {
        "xfm": x1fm_d, "xtm": x1_d,
        "wb_v": wb_f.rearrange("p (ob hc w) -> p ob hc w", ob=8, hc=HC),
        "aT": aT, "bmat": bmat, "rwt": rwt, "rbp": rbp, "actv": actv,
        "brow": brow, "consts": consts, "onesi": onesi,
        "x2o": x2_d, "x2fm_o": x2ag_in.rearrange("(hc p) t -> p hc t", p=128),
        "identb": identb, "cc_wait": cc4,
    }
    emit_p1(nc, em, io1)
    em.barrier()

    cc6 = em.op("gpsimd", lambda e: e.collective_compute(
        "AllGather", ALU.bypass, replica_groups=[list(range(NCORE))],
        ins=[x2ag_in.opt()], outs=[x2ag_out.opt()]), inc="CC")

    x2v = x2ag_out.rearrange("(r hc p) t -> r p hc t", r=8, p=128)

    def xg_src(tb):
        r = tb if tb < 8 else 15 - tb
        col0 = 0 if tb < 8 else 512
        return x2v[r, :, :, col0:col0 + 512]

    def ctx_dst(g, h4):
        dd = g if g < 8 else 15 - g
        lcol = 0 if g < 8 else 512
        return cx_in[dd * 512 + h4 * 128:dd * 512 + (h4 + 1) * 128, lcol:lcol + 512]

    io2 = {
        "xg_src": xg_src, "ctx_dst": ctx_dst, "q_d": q_d,
        "wq_sl": wq_sl, "wk_sl": wk_sl, "wv_sl": wv_sl,
        "cosf_d": cosf_d, "sinf_d": sinf_d, "dmask_d": dmask_d, "onesi": onesi,
        "xg_wait": ("CC", cc6),
    }
    emit_p2a(nc, em, io2)
    em.barrier()

    def a2afn(e):
        with allow_ici_all_to_all():
            return e.collective_compute(
                "AllToAll", ALU.bypass, replica_groups=[list(range(NCORE))],
                ins=[cx_in.opt()], outs=[cx_out.opt()])
    cc7 = em.op("gpsimd", a2afn, inc="CC")

    io2b = {
        "ctxsl": cx_out.rearrange("(hc p) t -> p hc t", p=128),
        "xtm": x2_d,
        "wo_v": wo_f.rearrange("p (ob hc w) -> p ob hc w", ob=8, hc=HC),
        "x3o": x3_d, "x3fm_o": x3fm_d, "identb": identb,
        "cc_wait": cc5, "ctx_wait": ("CC", cc7),
    }
    emit_p2b(nc, em, io2b)
    em.barrier()

    io3 = {
        "xfm": x3fm_d, "xtm": x3_d, "rts_d": rts_d, "a1_d": a1_d,
        "b1_d": b1_d, "e816_d": e816_d, "identf_d": identf_d, "x4o": x4o,
    }
    emit_p3(nc, em, io3)
    em.run()
    em.ctx.close()
    return nc


def fused_host_inputs(inputs):
    x = np.asarray(inputs["L0_attn_x"], np.float32).reshape(8192, 4096)
    m0 = p0_host_inputs(x, inputs["L0_attn_w_gate"], inputs["L0_attn_w_up"],
                        inputs["L0_attn_w_down"])
    zeros = np.zeros((8192, 4096), np.float32)
    m1 = p1_host_inputs(zeros, inputs["L0_ffn_W_base"], inputs["L0_ffn_b_base"],
                        inputs["L0_ffn_lora_A"], inputs["L0_ffn_lora_B"],
                        inputs["L0_ffn_router_W"], inputs["L0_ffn_router_b"],
                        inputs["L0_ffn_domain_prior"], inputs["L0_ffn_active"])
    m2 = p2_host_inputs(zeros, inputs["L1_attn_Wq"], inputs["L1_attn_Wk"],
                        inputs["L1_attn_Wv"], inputs["L1_attn_Wo"])
    m2b = p2b_host_inputs(np.zeros((4096, 8192), np.float32), zeros,
                          inputs["L1_attn_Wo"])
    m3 = p3_host_inputs(zeros, inputs["L1_ffn_route_w"], inputs["L1_ffn_A"],
                        inputs["L1_ffn_Bw"])
    maps = []
    for c in range(NCORE):
        mm = {}
        mm.update({k: v for k, v in m0[c].items()})
        mm.update({k: v for k, v in m1[c].items() if k not in ("xfm", "xtm")})
        mm.update({k: v for k, v in m2[c].items() if k not in ("xfm", "xtm")})
        mm.update({k: v for k, v in m2b[c].items() if k not in ("ctxsl", "xtm")})
        mm.update({k: v for k, v in m3[c].items() if k not in ("xfm", "xtm")})
        maps.append(mm)
    return maps


FUSED = True

_BUILD_CACHE = {}


def _built(name, fn):
    if name not in _BUILD_CACHE:
        _BUILD_CACHE[name] = fn()
    return _BUILD_CACHE[name]


def kernel(**inputs):
    if FUSED:
        maps = fused_host_inputs(inputs)
        nc = _built("fused", build_fused)
        res = run_bass_kernel_spmd(nc, maps, core_ids=list(range(NCORE)))
        x4 = gather_tokens(res.results, "x4")
        return np.ascontiguousarray(x4.astype(np.float32)).reshape(1, 8192, 4096)

    x = np.asarray(inputs["L0_attn_x"], np.float32).reshape(8192, 4096)

    nc0 = _built("p0", build_p0)
    maps0 = p0_host_inputs(x, inputs["L0_attn_w_gate"], inputs["L0_attn_w_up"],
                           inputs["L0_attn_w_down"])
    res0 = run_bass_kernel_spmd(nc0, maps0, core_ids=list(range(NCORE)))
    x1 = gather_tokens(res0.results, "x1")

    nc1 = _built("p1", build_p1)
    maps1 = p1_host_inputs(x1, inputs["L0_ffn_W_base"], inputs["L0_ffn_b_base"],
                           inputs["L0_ffn_lora_A"], inputs["L0_ffn_lora_B"],
                           inputs["L0_ffn_router_W"], inputs["L0_ffn_router_b"],
                           inputs["L0_ffn_domain_prior"], inputs["L0_ffn_active"])
    res1 = run_bass_kernel_spmd(nc1, maps1, core_ids=list(range(NCORE)))
    x2 = gather_tokens(res1.results, "x2")

    nc2 = _built("p2", build_p2)
    maps2 = p2_host_inputs(x2, inputs["L1_attn_Wq"], inputs["L1_attn_Wk"],
                           inputs["L1_attn_Wv"], inputs["L1_attn_Wo"])
    res2 = run_bass_kernel_spmd(nc2, maps2, core_ids=list(range(NCORE)))
    ctx_full = np.concatenate([res2.results[c]["ctxo"] for c in range(NCORE)], axis=0)

    nc2b = _built("p2b", build_p2b)
    maps2b = p2b_host_inputs(ctx_full, x2, inputs["L1_attn_Wo"])
    res2b = run_bass_kernel_spmd(nc2b, maps2b, core_ids=list(range(NCORE)))
    x3 = gather_tokens(res2b.results, "x3")

    nc3 = _built("p3", build_p3)
    maps3 = p3_host_inputs(x3, inputs["L1_ffn_route_w"], inputs["L1_ffn_A"],
                           inputs["L1_ffn_Bw"])
    res3 = run_bass_kernel_spmd(nc3, maps3, core_ids=list(range(NCORE)))
    x4 = gather_tokens(res3.results, "x4")

    return np.ascontiguousarray(x4.astype(np.float32)).reshape(1, 8192, 4096)


# version 10 (final)
# speedup vs baseline: 1.0112x; 1.0112x over previous
import numpy as np
from contextlib import ExitStack

import concourse.bass as bass
import concourse.mybir as mybir
from concourse.bass_utils import run_bass_kernel_spmd
from concourse.aot_env import allow_ici_all_to_all
import ml_dtypes

BF = ml_dtypes.bfloat16
F32 = mybir.dt.float32
BF16 = mybir.dt.bfloat16
I32 = mybir.dt.int32
AF = mybir.ActivationFunctionType
ALU = mybir.AluOpType
AX = mybir.AxisListType

NCORE = 8
_NAME_SEQ = [0]
D = 4096
HC = 32
I_FF = 11008
IBLOCKS = [(i * 512, 512) for i in range(21)] + [(10752, 256)]
IC = 86
TOPK = 5504
BISECT_ITERS = 16
NT = 8
EPS_LN = 1e-5

ENGINES = ("sync", "tensor", "scalar", "vector", "gpsimd")


def cast_bf(x):
    return np.ascontiguousarray(np.asarray(x, dtype=np.float32)).astype(BF)


def core_token_idx(c, n_tiles=NT):
    return np.r_[np.arange(512 * c, 512 * c + 512),
                 np.arange(512 * (15 - c), 512 * (15 - c) + 512)][:n_tiles * 128]


class Emit:

    def __init__(self, nc, ctx=None):
        self.nc = nc
        self.ctx = ctx if ctx is not None else ExitStack()
        self.ops = {e: [] for e in ENGINES}
        self.sems = {}
        self.cnt = {}
        self.dma_sems = set()

    def sem(self, name):
        if name not in self.sems:
            self.sems[name] = self.ctx.enter_context(self.nc.semaphore(name))
            self.cnt[name] = 0
        return self.sems[name]

    def op(self, eng, fn, inc=None, incval=1):
        if inc is not None:
            s = self.sem(inc)
            if incval == 16:
                self.dma_sems.add(inc)
            self.cnt[inc] += incval
            v = self.cnt[inc]
            self.ops[eng].append(lambda e, fn=fn, s=s, iv=incval: fn(e).then_inc(s, iv))
            return v
        self.ops[eng].append(fn)
        return None

    def wait(self, eng, semname, v):
        if v <= 0:
            return
        s = self.sem(semname)
        self.ops[eng].append(lambda e, s=s, v=v: e.wait_ge(s, v))

    def val(self, name):
        self.sem(name)
        return self.cnt[name]

    def barrier(self):
        for sname in sorted(self.dma_sems):
            self.wait("gpsimd", sname, self.val(sname))
        b = self.sem("BAR")
        for eng in ENGINES:
            self.cnt["BAR"] += 1
            self.ops[eng].append(lambda e, b=b: e.sem_inc(b, 1))
        tgt = self.cnt["BAR"]
        for eng in ENGINES:
            self.wait(eng, "BAR", tgt)

    def run(self):
        with self.nc.Block() as blk:
            for name in ENGINES:
                ops = self.ops[name]
                if not ops:
                    continue
                def runner(e, ops=ops):
                    for fn in ops:
                        fn(e)
                getattr(blk, name)(runner)


def dv_chain(em, fn):
    v = em.op("vector", fn, inc="DV")
    em.wait("vector", "DV", v)
    return v


def emit_ln(em, r, trash, ms, va, out_buf):
    dv_chain(em, lambda e: e.reduce_sum(ms[:, 0:1], r, axis=AX.X))
    dv_chain(em, lambda e: e.tensor_scalar(out=ms[:, 1:2], in0=ms[:, 0:1],
                                           scalar1=1.0 / 4096.0, scalar2=None, op0=ALU.mult))
    dv_chain(em, lambda e: e.tensor_scalar(out=r, in0=r,
                                           scalar1=ms[:, 1:2], scalar2=None, op0=ALU.subtract))
    dv_chain(em, lambda e: e.scalar_tensor_tensor(out=trash, in0=r, scalar=1.0,
                                                  in1=r, op0=ALU.mult, op1=ALU.mult,
                                                  accum_out=va[:, 0:1]))
    v = dv_chain(em, lambda e: e.tensor_scalar(out=va[:, 1:2], in0=va[:, 0:1],
                                               scalar1=1.0 / 4096.0, scalar2=EPS_LN,
                                               op0=ALU.mult, op1=ALU.add))
    em.wait("scalar", "DV", v)
    a = em.op("scalar", lambda e: e.activation(out=va[:, 1:2], in_=va[:, 1:2], func=AF.Ln), inc="AC")
    em.wait("vector", "AC", a)
    v = dv_chain(em, lambda e: e.tensor_scalar(out=va[:, 1:2], in0=va[:, 1:2],
                                               scalar1=-0.5, scalar2=None, op0=ALU.mult))
    em.wait("scalar", "DV", v)
    a = em.op("scalar", lambda e: e.activation(out=va[:, 1:2], in_=va[:, 1:2], func=AF.Exp), inc="AC")
    em.wait("vector", "AC", a)
    return dv_chain(em, lambda e: e.tensor_scalar(out=out_buf, in0=r,
                                                  scalar1=va[:, 1:2], scalar2=None, op0=ALU.mult))


def emit_fm_out(em, x_tm, ident, xfmob, ps_t, pst_user, fm_o, row0, lnv):
    em.wait("tensor", "DV", lnv)
    em.wait("scalar", "FM", em.val("FM"))
    for hc in range(HC):
        qk = hc % 2
        em.wait("tensor", "AC", pst_user[qk])
        pe = em.op("tensor", lambda e, hc=hc, qk=qk: e.transpose(
            ps_t[qk], x_tm[:, hc * 128:(hc + 1) * 128], ident[:]), inc="PE")
        em.wait("scalar", "PE", pe)
        ac = em.op("scalar", lambda e, hc=hc, qk=qk: e.copy(
            out=xfmob[:, hc, :], in_=ps_t[qk]), inc="AC")
        pst_user[qk] = ac
    em.wait("gpsimd", "AC", pst_user[0])
    em.wait("gpsimd", "AC", pst_user[1])
    fmv = em.op("gpsimd", lambda e, row0=row0: e.dma_start(
        out=fm_o[:, :, row0:row0 + 128], in_=xfmob[:]), inc="FM", incval=16)
    return pst_user


def emit_weight_gather(em, nc, shards):
    cc = 0
    for ext, internal, full in shards:
        _NAME_SEQ[0] += 1
        wi = "WI_%d" % _NAME_SEQ[0]
        n = em.op("sync", lambda e, a=internal, b=ext: e.dma_start(out=a, in_=b),
                  inc=wi, incval=16)
        em.wait("gpsimd", wi, n)
        cc = em.op("gpsimd", lambda e, i=internal, o=full: e.collective_compute(
            "AllGather", ALU.bypass,
            replica_groups=[list(range(NCORE))],
            ins=[i.opt()], outs=[o.opt()]), inc="CC")
    return cc



F16 = mybir.dt.float16


def emit_p0(nc, em, io, n_tiles=NT, debug=False):
    xfm = io["xfm"]; xtm = io["xtm"]; identb = io["identb"]
    wg_v = io["wg_v"]; wu_v = io["wu_v"]; wd_v = io["wd_v"]
    wg_v2 = io.get("wg_v2")
    def wg_src(ib):
        if wg_v2 is None:
            return wg_v[:, ib, :, :]
        if ib < 22:
            return wg_v2[0][:, ib, :, :]
        return wg_v2[1][:, ib - 22, :, :]
    x1o = io["x1o"]; x1fm_o = io.get("x1fm_o")
    cc = io["cc_wait"]
    cc_t = cc if isinstance(cc, tuple) else (cc, cc, cc)
    if len(cc_t) == 4:
        cc_g, cc_g2, cc_u, cc_d = cc_t
    else:
        cc_g, cc_u, cc_d = cc_t
        cc_g2 = cc_g
    NP = n_tiles // 2

    with ExitStack() as ctx:
        def sb(n, s, d):
            _NAME_SEQ[0] += 1
            return ctx.enter_context(nc.sbuf_tensor("%s_%d" % (n, _NAME_SEQ[0]), s, d))
        def psb(n, s, d=F32):
            _NAME_SEQ[0] += 1
            return ctx.enter_context(nc.psum_tensor("%s_%d" % (n, _NAME_SEQ[0]), s, d))

        xc = [sb("xcA", [128, HC, 256], BF16), sb("xcB", [128, HC, 256], BF16)]
        ident = sb("ident", [128, 128], BF16)
        s16 = [sb("s16_%d" % i, [128, I_FF], F16) for i in range(4)]
        smu = [sb("smuA", [128, IC, 128], BF16), sb("smuB", [128, IC, 128], BF16)]
        wst = sb("wst", [128, 2, HC, 256], BF16)
        wds = [wst[:, k, :, :].rearrange("p hc w -> p (hc w)")[:, 0:4096]
               .rearrange("p (ic w) -> p ic w", ic=8) for k in range(2)]
        up1 = sb("up1", [128, 512], F16)
        smu_tm = sb("smu_tm", [128, 512], BF16)
        mtmp = sb("mtmp", [128, 512], BF16)
        sc = sb("sc", [128, 32], F32)
        trA = sb("trA", [128, 2752], F16)
        trB = sb("trB", [128, 2752], F16)
        ms = sb("ms", [128, 2], F32)
        va = sb("va", [128, 2], F32)

        ps_gA = psb("ps_gA", [128, 2, 512])
        ps_gB = psb("ps_gB", [128, 2, 512])
        ps_tA = psb("ps_tA", [128, 1024], BF16)
        ps_tB = psb("ps_tB", [128, 1024], BF16)
        ps_t = [ps_tA[:, 0:128], ps_tB[:, 0:128]]
        ps_g = [ps_gA, ps_gB]

        LO, HI, MID, CNT, GE, TMP, NMID, CNT2, GN = range(9)
        CA0, CB0 = 16, 20
        IB256 = [(i * 256, 256) for i in range(43)]
        W0 = 16.0

        idl = em.op("gpsimd", lambda e: e.dma_start(out=ident[:], in_=identb),
                    inc="IDL", incval=16)

        wst_user = [0, 0]
        psg_user = [[("AC", 0), ("AC", 0)], [("AC", 0), ("AC", 0)]]
        pst_user = [0, 0]
        psd_user = [0, 0]
        xc_user = [0, 0]
        xcst_user = [0, 0]
        xt_fm_user = [0, 0]
        s_user = [0, 0, 0, 0]
        xc_pe_done = [0, 0]
        s_user_sem = ["DV", "DV", "DV", "DV"]
        smu_user = [0, 0]
        theta_cols = [(8, 10), (12, 14)]
        last_st = 0

        def emit_gate(p):
            k2 = p % 2
            em.wait("gpsimd", "PE", xc_user[k2])
            em.wait("gpsimd", "FM", xt_fm_user[k2])
            em.wait("gpsimd", "ST", xcst_user[k2])
            xl = em.op("gpsimd", lambda e, p=p, k2=k2: e.dma_start(
                out=xc[k2][:], in_=xfm[:, :, p * 256:(p + 1) * 256]),
                inc="XC%d" % k2, incval=16)
            for ib in range(43):
                k = ib % 2
                em.wait("sync", "PE", wst_user[k])
                if p == 0 and ib == 0:
                    em.wait("sync", "CC", cc_g)
                if p == 0 and ib == 22:
                    em.wait("sync", "CC", cc_g2)
                wl = em.op("sync", lambda e, ib=ib, k=k: e.dma_start(
                    out=wst[:, k, :, :], in_=wg_src(ib)),
                    inc="WL%d" % k, incval=16)
                em.wait("tensor", "WL%d" % k, wl)
                if ib == 0:
                    em.wait("tensor", "XC%d" % k2, xl)
                for ti in range(2):
                    sem_n, sem_v = psg_user[ti][k]
                    em.wait("tensor", sem_n, sem_v)
                def mmg(e, k=k, k2=k2):
                    for ti in range(2):
                        for hc in range(HC):
                            m = e.matmul(ps_g[ti][:, k, 0:256],
                                         xc[k2][:, hc, ti * 128:(ti + 1) * 128],
                                         wst[:, k, hc, :], start=(hc == 0), stop=(hc == HC - 1))
                    return m
                pe = em.op("tensor", mmg, inc="PE")
                wst_user[k] = pe
                em.wait("scalar", "PE", pe)
                for ti in range(2):
                    slot = (2 * p + ti) % 4
                    if ib == 0:
                        em.wait("scalar", s_user_sem[slot], s_user[slot])
                    ac = em.op("scalar", lambda e, ib=ib, k=k, ti=ti, slot=slot: e.activation(
                        out=s16[slot][:, ib * 256:(ib + 1) * 256], in_=ps_g[ti][:, k, 0:256],
                        func=AF.Silu), inc="AC")
                    psg_user[ti][k] = ("AC", ac)
            return em.val("AC")

        bis_last_ac = [0]

        def make_bisect(p, gate_ac):
            tcol, ncol = theta_cols[p % 2]
            CHN = [(i * 2752, 2752) for i in range(4)]
            st = {"ti": 0, "it": 0, "init": False, "w": W0 / 2}

            def step(n):
                done = 0
                while done < n and st["ti"] < 2:
                    ti, it = st["ti"], st["it"]
                    slot = (2 * p + ti) % 4
                    if not st["init"]:
                        if ti == 0:
                            em.wait("vector", "AC", gate_ac)
                        em.op("vector", lambda e: e.memset(sc[:, LO:LO + 1], 0.0))
                        v = em.op("vector", lambda e: e.memset(sc[:, MID:MID + 1], W0 / 2),
                                  inc="DV")
                        em.wait("vector", "DV", v)
                        st["init"] = True
                        st["w"] = W0 / 2
                    w = st["w"]
                    em.wait("scalar", "DV", em.val("DV"))
                    for ci, (c0, cw) in enumerate(CHN):
                        em.wait("scalar", "AC", bis_last_ac[0])
                        a = em.op("scalar", lambda e, slot=slot, c0=c0, cw=cw, ci=ci:
                                  e.activation(
                            out=trB[:, 0:cw], in_=s16[slot][:, c0:c0 + cw], func=AF.Sign,
                            bias=sc[:, MID:MID + 1],
                            accum_out=sc[:, CB0 + ci:CB0 + ci + 1]), inc="AC")
                        bis_last_ac[0] = a
                    for ci, (c0, cw) in enumerate(CHN):
                        v = em.op("vector", lambda e, slot=slot, c0=c0, cw=cw, ci=ci:
                                  e.tensor_scalar(
                            out=trA[:, 0:cw], in0=s16[slot][:, c0:c0 + cw],
                            scalar1=sc[:, MID:MID + 1],
                            scalar2=None, op0=ALU.is_gt, op1=ALU.add,
                            accum_out=sc[:, CA0 + ci:CA0 + ci + 1]), inc="DV")
                        em.wait("vector", "DV", v)
                    for c_, d_ in ((CA0, CA0 + 1), (CA0 + 2, CA0 + 3), (CA0, CA0 + 2)):
                        dv_chain(em, lambda e, c_=c_, d_=d_: e.tensor_tensor(
                            out=sc[:, c_:c_ + 1], in0=sc[:, c_:c_ + 1],
                            in1=sc[:, d_:d_ + 1], op=ALU.add))
                    em.wait("vector", "AC", bis_last_ac[0])
                    for c_, d_ in ((CB0, CB0 + 1), (CB0 + 2, CB0 + 3), (CB0, CB0 + 2)):
                        dv_chain(em, lambda e, c_=c_, d_=d_: e.tensor_tensor(
                            out=sc[:, c_:c_ + 1], in0=sc[:, c_:c_ + 1],
                            in1=sc[:, d_:d_ + 1], op=ALU.add))
                    dv_chain(em, lambda e: e.scalar_tensor_tensor(
                        out=sc[:, CNT:CNT + 1], in0=sc[:, CB0:CB0 + 1], scalar=-0.5,
                        in1=sc[:, CA0:CA0 + 1], op0=ALU.mult, op1=ALU.add))
                    dv_chain(em, lambda e: e.tensor_scalar(
                        out=sc[:, GE:GE + 1], in0=sc[:, CNT:CNT + 1], scalar1=0.0,
                        scalar2=None, op0=ALU.is_ge))
                    w2 = w / 2
                    if it < BISECT_ITERS - 1:
                        dv_chain(em, lambda e, w2=w2: e.scalar_tensor_tensor(
                            out=sc[:, LO:LO + 1], in0=sc[:, GE:GE + 1], scalar=w2,
                            in1=sc[:, LO:LO + 1], op0=ALU.mult, op1=ALU.add))
                        dv_chain(em, lambda e, w2=w2: e.tensor_scalar(
                            out=sc[:, MID:MID + 1], in0=sc[:, LO:LO + 1], scalar1=w2 / 2,
                            scalar2=None, op0=ALU.add))
                    else:
                        dv_chain(em, lambda e, w2=w2, tc=tcol + ti: e.scalar_tensor_tensor(
                            out=sc[:, tc:tc + 1], in0=sc[:, GE:GE + 1], scalar=w2,
                            in1=sc[:, LO:LO + 1], op0=ALU.mult, op1=ALU.add))
                        dv_chain(em, lambda e, tc=tcol + ti, nc_=ncol + ti: e.tensor_scalar(
                            out=sc[:, nc_:nc_ + 1], in0=sc[:, tc:tc + 1], scalar1=-1.0,
                            scalar2=None, op0=ALU.mult))
                    st["w"] = w2
                    st["it"] += 1
                    if st["it"] == BISECT_ITERS:
                        st["it"] = 0
                        st["ti"] += 1
                        st["init"] = False
                    done += 1
            return step

        def emit_up(p):
            nonlocal tpose_pe
            k2 = p % 2
            tcol, ncol = theta_cols[p % 2]
            for ti in range(2):
                slot = (2 * p + ti) % 4
                tc = tcol + ti
                nc_ = ncol + ti
                for ib in range(43):
                    k = ib % 2
                    em.wait("sync", "PE", wst_user[k])
                    if p == 0 and ti == 0 and ib == 0:
                        em.wait("sync", "CC", cc_u)
                    wl = em.op("sync", lambda e, ib=ib, k=k: e.dma_start(
                        out=wst[:, k, :, :], in_=wu_v[:, ib, :, :]),
                        inc="WL%d" % k, incval=16)
                    em.wait("tensor", "WL%d" % k, wl)
                    sem_n, sem_v = psg_user[ti][k]
                    em.wait("tensor", sem_n, sem_v)
                    def mmu(e, k=k, k2=k2, ti=ti):
                        for hc in range(HC):
                            m = e.matmul(ps_g[ti][:, k, 0:256],
                                         xc[k2][:, hc, ti * 128:(ti + 1) * 128],
                                         wst[:, k, hc, :], start=(hc == 0), stop=(hc == HC - 1))
                        return m
                    pe = em.op("tensor", mmu, inc="PE")
                    wst_user[k] = pe
                    em.wait("vector", "PE", pe)
                    em.wait("vector", "PE", tpose_pe)
                    if ib == 0:
                        em.wait("vector", "ST", smu_user[ti])
                        em.wait("scalar", "ST", smu_user[ti])
                    ibr = slice(ib * 256, (ib + 1) * 256)
                    dv_chain(em, lambda e, ibr=ibr, slot=slot, tc=tc: e.tensor_scalar(
                        out=mtmp[:, 0:256], in0=s16[slot][:, ibr], scalar1=sc[:, tc:tc + 1],
                        scalar2=None, op0=ALU.is_gt))
                    dv_chain(em, lambda e, ibr=ibr, slot=slot, nc_=nc_: e.tensor_scalar(
                        out=smu_tm[:, 0:256], in0=s16[slot][:, ibr], scalar1=sc[:, nc_:nc_ + 1],
                        scalar2=None, op0=ALU.is_lt))
                    dv_chain(em, lambda e: e.tensor_tensor(
                        out=mtmp[:, 0:256], in0=mtmp[:, 0:256], in1=smu_tm[:, 0:256],
                        op=ALU.add))
                    dv_chain(em, lambda e, ibr=ibr, slot=slot: e.tensor_tensor(
                        out=up1[:, 0:256], in0=s16[slot][:, ibr], in1=mtmp[:, 0:256],
                        op=ALU.mult))
                    dvv = em.op("vector", lambda e, k=k, ti=ti: e.tensor_tensor(
                        out=smu_tm[:, 0:256], in0=up1[:, 0:256], in1=ps_g[ti][:, k, 0:256],
                        op=ALU.mult), inc="DV")
                    psg_user[ti][k] = ("DV", dvv)
                    em.wait("tensor", "DV", dvv)
                    for q in range(2):
                        qk = q % 2
                        if p == 0 and ti == 0 and ib == 0 and q == 0:
                            em.wait("tensor", "IDL", idl)
                        em.wait("tensor", "AC", pst_user[qk])
                        pe = em.op("tensor", lambda e, q=q, qk=qk: e.transpose(
                            ps_t[qk], smu_tm[:, q * 128:(q + 1) * 128], ident[:]),
                            inc="PE")
                        tpose_pe = pe
                        em.wait("scalar", "PE", pe)
                        ac = em.op("scalar", lambda e, ib=ib, q=q, qk=qk, ti=ti: e.copy(
                            out=smu[ti][:, ib * 2 + q, :], in_=ps_t[qk]), inc="AC")
                        pst_user[qk] = ac
            xc_pe_done[p % 2] = em.val("PE")
            return em.val("AC")

        def emit_down_ln(p, smu_ready, filler=None):
            nonlocal last_st
            em.wait("tensor", "AC", smu_ready)
            down_evac = 0
            if filler is not None:
                filler(2)
            for ob in range(8):
                k = ob % 2
                groups = [(i, min(8, IC - i)) for i in range(0, IC, 8)]
                for gi, (ic0, ng) in enumerate(groups):
                    gk = gi % 2
                    em.wait("sync", "PE", wst_user[gk])
                    if p == 0 and ob == 0 and gi == 0:
                        em.wait("sync", "CC", cc_d)
                    wl = em.op("sync", lambda e, ob=ob, ic0=ic0, ng=ng, gk=gk: e.dma_start(
                        out=wds[gk][:, 0:ng, :], in_=wd_v[:, ob, ic0:ic0 + ng, :]),
                        inc="WD%d" % gk, incval=16)
                    em.wait("tensor", "WD%d" % gk, wl)
                    if gi == 0:
                        em.wait("tensor", "AC", psd_user[k])
                        for ti in range(2):
                            sem_n, sem_v = psg_user[ti][k]
                            em.wait("tensor", sem_n, sem_v)
                    def mmdown(e, ic0=ic0, ng=ng, gk=gk, k=k):
                        for ii in range(ng):
                            ic = ic0 + ii
                            st = (ic == 0)
                            sp = (ic == IC - 1)
                            e.matmul(ps_gA[:, k, :], smu[0][:, ic, :],
                                     wds[gk][:, ii, :], start=st, stop=sp)
                            m = e.matmul(ps_gB[:, k, :], smu[1][:, ic, :],
                                         wds[gk][:, ii, :], start=st, stop=sp)
                        return m
                    pe = em.op("tensor", mmdown, inc="PE")
                    wst_user[gk] = pe
                em.wait("scalar", "PE", pe)
                obr = slice(ob * 512, (ob + 1) * 512)
                for ti in range(2):
                    slot = (2 * p + ti) % 4
                    if ob == 0:
                        em.wait("scalar", s_user_sem[slot], s_user[slot])
                rrv = [s16[(2 * p) % 4][:, 0:4096], s16[(2 * p + 1) % 4][:, 0:4096]]
                em.op("scalar", lambda e, k=k, obr=obr, r0=rrv[0]: e.copy(
                    out=r0[:, obr], in_=ps_gA[:, k, :]))
                ac = em.op("scalar", lambda e, k=k, obr=obr, r1=rrv[1]: e.copy(
                    out=r1[:, obr], in_=ps_gB[:, k, :]), inc="AC")
                psd_user[k] = ac
                down_evac = ac
                if filler is not None:
                    filler(4)
            if filler is not None:
                filler(999)
            for ti in range(2):
                psg_user[ti] = [("AC", down_evac), ("AC", down_evac)]

            k2 = p % 2
            xtv = xc[k2][:, 0:16, :].rearrange("p a b -> p (a b)")
            x1bv = [smu[1][:, 0:32, :].rearrange("p a b -> p (a b)"),
                    smu[1][:, 32:64, :].rearrange("p a b -> p (a b)")]
            trashv = smu[1][:, 54:86, :].rearrange("p a b -> p (a b)")
            xfmobv = xc[k2][:, :, 128:256].rearrange("p a b -> p a b")
            rrv = [s16[(2 * p) % 4][:, 0:4096], s16[(2 * p + 1) % 4][:, 0:4096]]
            for pi in range(2):
                slot = (2 * p + pi) % 4
                row0 = (2 * p + pi) * 128
                em.wait("gpsimd", "PE", xc_pe_done[k2])
                if x1fm_o is not None:
                    em.wait("gpsimd", "FM", em.val("FM"))
                xl2 = em.op("gpsimd", lambda e, row0=row0, xtv=xtv: e.dma_start(
                    out=xtv, in_=xtm[row0:row0 + 128, :]), inc="XT", incval=16)
                em.wait("vector", "AC", down_evac)
                em.wait("vector", "XT", xl2)
                em.wait("vector", "ST", em.val("ST"))
                rv = dv_chain(em, lambda e, pi=pi, rrv=rrv, xtv=xtv: e.tensor_tensor(
                    out=rrv[pi], in0=rrv[pi], in1=xtv, op=ALU.add))
                lnv = emit_ln(em, rrv[pi], trashv, ms, va, x1bv[pi])
                s_user[slot] = lnv
                s_user_sem[slot] = "DV"
                em.wait("gpsimd", "DV", lnv)
                stv = em.op("gpsimd", lambda e, pi=pi, row0=row0, x1bv=x1bv: e.dma_start(
                    out=x1o[row0:row0 + 128, :], in_=x1bv[pi]),
                    inc="ST", incval=16)
                last_st = stv
                if x1fm_o is not None:
                    em.wait("tensor", "DV", lnv)
                    em.wait("scalar", "FM", em.val("FM"))
                    for hc in range(HC):
                        qk = hc % 2
                        em.wait("tensor", "AC", pst_user[qk])
                        pe = em.op("tensor", lambda e, hc=hc, pi=pi, qk=qk, x1bv=x1bv: e.transpose(
                            ps_t[qk], x1bv[pi][:, hc * 128:(hc + 1) * 128], ident[:]),
                            inc="PE")
                        em.wait("scalar", "PE", pe)
                        ac = em.op("scalar", lambda e, hc=hc, qk=qk, xfmobv=xfmobv: e.copy(
                            out=xfmobv[:, hc, :], in_=ps_t[qk]), inc="AC")
                        pst_user[qk] = ac
                    em.wait("gpsimd", "AC", pst_user[0])
                    em.wait("gpsimd", "AC", pst_user[1])
                    em.op("gpsimd", lambda e, row0=row0, xfmobv=xfmobv: e.dma_start(
                        out=x1fm_o[:, :, row0:row0 + 128], in_=xfmobv[:]),
                        inc="FM", incval=16)
            smu_user[0] = em.val("ST")
            smu_user[1] = em.val("ST")
            xc_user[k2] = em.val("PE")
            xcst_user[k2] = em.val("ST")
            xt_fm_user[k2] = em.val("FM") if x1fm_o is not None else 0

        tpose_pe = 0
        g_ac = emit_gate(0)
        bis = make_bisect(0, g_ac)
        bis(999)
        smu_ready = emit_up(0)
        for p in range(NP):
            bis = None
            if p + 1 < NP:
                g_ac = emit_gate(p + 1)
                bis = make_bisect(p + 1, g_ac)
            emit_down_ln(p, smu_ready, filler=bis)
            if p + 1 < NP:
                smu_ready = emit_up(p + 1)

        em.wait("gpsimd", "ST", last_st)
        if x1fm_o is not None:
            em.wait("gpsimd", "FM", em.val("FM"))


def build_p0(n_tiles=NT, debug=False):
    nc = bass.Bass("TRN2", target_bir_lowering=False)
    io = {}
    io["xfm"] = nc.dram_tensor("xfm", [128, HC, n_tiles * 128], BF16, kind="ExternalInput").ap()
    io["xtm"] = nc.dram_tensor("xtm", [n_tiles * 128, D], BF16, kind="ExternalInput").ap()
    io["identb"] = nc.dram_tensor("identb", [128, 128], BF16, kind="ExternalInput").ap()
    wg_sh = nc.dram_tensor("wg_sh", [16, HC * I_FF], BF16, kind="ExternalInput").ap()
    wu_sh = nc.dram_tensor("wu_sh", [16, HC * I_FF], BF16, kind="ExternalInput").ap()
    wd_sh = nc.dram_tensor("wd_sh", [16, IC * D], BF16, kind="ExternalInput").ap()
    io["x1o"] = nc.dram_tensor("x1", [n_tiles * 128, D], BF16, kind="ExternalOutput").ap()
    if debug:
        io["s_dump"] = nc.dram_tensor("s_dump", [128, I_FF], F32, kind="ExternalOutput").ap()
        io["sc_dump"] = nc.dram_tensor("sc_dump", [128, 8], F32, kind="ExternalOutput").ap()
        io["smu_dump"] = nc.dram_tensor("smu_dump", [128, IC, 128], BF16, kind="ExternalOutput").ap()
        io["rr_dump"] = nc.dram_tensor("rr_dump", [128, D], F32, kind="ExternalOutput").ap()
    wg_in = nc.dram_tensor("wg_in", [16, HC * I_FF], BF16).ap()
    wu_in = nc.dram_tensor("wu_in", [16, HC * I_FF], BF16).ap()
    wd_in = nc.dram_tensor("wd_in", [16, IC * D], BF16).ap()
    wg_f = nc.dram_tensor("wg_f", [128, HC * I_FF], BF16, addr_space="Shared").ap()
    wu_f = nc.dram_tensor("wu_f", [128, HC * I_FF], BF16, addr_space="Shared").ap()
    wd_f = nc.dram_tensor("wd_f", [128, IC * D], BF16, addr_space="Shared").ap()
    io["wg_v"] = wg_f.rearrange("p (ib hc w) -> p ib hc w", ib=43, hc=HC)
    io["wu_v"] = wu_f.rearrange("p (ib hc w) -> p ib hc w", ib=43, hc=HC)
    io["wd_v"] = wd_f.rearrange("p (ob ic w) -> p ob ic w", ob=8, ic=IC)
    em = Emit(nc)
    emit_weight_gather(em, nc, [(wg_sh, wg_in, wg_f)])
    emit_weight_gather(em, nc, [(wu_sh, wu_in, wu_f)])
    emit_weight_gather(em, nc, [(wd_sh, wd_in, wd_f)])
    io["cc_wait"] = (1, 2, 3)
    emit_p0(nc, em, io, n_tiles=n_tiles, debug=debug)
    em.run()
    em.ctx.close()
    return nc


def p0_host_inputs(x, w_gate, w_up, w_down, n_tiles=NT):
    xb = cast_bf(x)
    xfm_full = np.ascontiguousarray(xb.T).reshape(HC, 128, 8192).transpose(1, 0, 2)
    wg = np.ascontiguousarray(cast_bf(np.asarray(w_gate).T).reshape(HC, 128, 43, 256)
                              .transpose(1, 2, 0, 3)).reshape(128, HC * I_FF)
    wu = np.ascontiguousarray(cast_bf(np.asarray(w_up).T).reshape(HC, 128, 43, 256)
                              .transpose(1, 2, 0, 3)).reshape(128, HC * I_FF)
    wd = np.ascontiguousarray(cast_bf(np.asarray(w_down).T).reshape(IC, 128, 8, 512)
                              .transpose(1, 2, 0, 3)).reshape(128, IC * D)
    identb = np.eye(128, dtype=np.float32).astype(BF)
    maps = []
    for c in range(NCORE):
        sl = core_token_idx(c, n_tiles)
        maps.append({
            "xfm": np.ascontiguousarray(xfm_full[:, :, sl]),
            "xtm": np.ascontiguousarray(xb[sl]),
            "identb": identb,
            "wg_sh": np.ascontiguousarray(wg[16 * c:16 * c + 16]),
            "wu_sh": np.ascontiguousarray(wu[16 * c:16 * c + 16]),
            "wd_sh": np.ascontiguousarray(wd[16 * c:16 * c + 16]),
        })
    return maps


def gather_tokens(results, key, n_tiles=NT):
    d1 = results[0][key].shape[-1]
    out = np.zeros((8192, d1), results[0][key].dtype)
    for c in range(NCORE):
        out[core_token_idx(c, n_tiles)] = results[c][key]
    return out



def emit_p1(nc, em, io, n_tiles=NT, debug=False):
    xfm = io["xfm"]; xtm = io["xtm"]; wb_v = io["wb_v"]
    aT = io["aT"]; bmat = io["bmat"]; rwt = io["rwt"]; rbp = io["rbp"]
    actv = io["actv"]; brow = io["brow"]; consts = io["consts"]; onesi = io["onesi"]
    x2o = io["x2o"]; x2fm_o = io.get("x2fm_o"); identb = io.get("identb")
    cc = io["cc_wait"]
    if debug:
        d_oh = io["d_oh"]; d_ssel = io["d_ssel"]; d_alpha = io["d_alpha"]
        d_zw = io["d_zw"]; d_lg = io["d_lg"]; d_rrt = io["d_rrt"]; d_z = io["d_z"]

    with ExitStack() as ctx:
        def sb(n, s, d):
            _NAME_SEQ[0] += 1
            return ctx.enter_context(nc.sbuf_tensor("%s_%d" % (n, _NAME_SEQ[0]), s, d))
        def psb(n, s, d=F32):
            _NAME_SEQ[0] += 1
            return ctx.enter_context(nc.psum_tensor("%s_%d" % (n, _NAME_SEQ[0]), s, d))

        xg = sb("xg", [128, HC, 512], BF16)
        ats = sb("ats", [128, HC, 128], BF16)
        cns = sb("cns", [128, 232], F32)
        onesf = sb("onesf", [128, 128], F32)
        rbps = sb("rbps", [16, 2], F32)
        rwts = sb("rwts", [128, HC, 16], BF16)
        acts = sb("acts", [1, 4], I32)
        brows = sb("brows", [1, D], F32)
        wbs = sb("wbs", [128, 2, HC, 512], BF16)
        bms = sb("bms", [128, 2, 512], BF16)
        z_sb = sb("z_sb", [128, 512], F32)
        lg_sb = sb("lg_sb", [16, 512], F32)
        small = sb("small", [128, 16], F32)
        eq8 = sb("eq8", [128, 4], F32)
        ssel = sb("ssel", [128, 32], F32)
        ssel_bf = sb("ssel_bf", [128, 32], BF16)
        e_sb = sb("e_sb", [4, 512], F32)
        rb4 = sb("rb4", [4, 512], F32)
        alpha = sb("alpha", [4, 512], F32)
        a32_sb = sb("a32_sb", [32, 512], F32)
        zw_bf = sb("zw_bf", [32, 512], BF16)
        bsel = sb("bsel", [32, 8, 512], BF16)
        x1t = sb("x1t", [128, D], BF16)
        rrt = sb("rrt", [128, D], F32)
        x2b = sb("x2b", [128, D], BF16)
        trash = sb("trashb", [128, D], BF16)
        ms = sb("ms", [128, 2], F32)
        va = sb("va", [128, 2], F32)

        ps_y = psb("ps_y", [128, 2, 512])
        ps_r1 = psb("ps_r1", [128, 512])
        ps_r2 = psb("ps_r2", [128, 512])
        ps_l = psb("ps_l", [128, 512])
        if x2fm_o is not None:
            identx = sb("identx", [128, 128], BF16)
            xfmob = sb("xfmob1", [128, HC, 128], BF16)
            ps_tA = psb("ps_tA1", [128, 1024], BF16)
            ps_tB = psb("ps_tB1", [128, 1024], BF16)
            ps_t = [ps_tA[:, 0:128], ps_tB[:, 0:128]]
            pst_user = [0, 0]

        IDENT = slice(0, 128)
        ONES = slice(128, 129)
        IOTA16 = slice(129, 130)
        C1M = slice(131, 163)
        E48 = slice(163, 195)
        E48ONE = slice(200, 232)

        ld0 = em.op("gpsimd", lambda e: e.dma_start(out=ats[:], in_=aT), inc="IDL", incval=16)
        if x2fm_o is not None:
            ld0 = em.op("gpsimd", lambda e: e.dma_start(out=identx[:], in_=identb),
                        inc="IDL", incval=16)
        ld0 = em.op("gpsimd", lambda e: e.dma_start(out=cns[:], in_=consts), inc="IDL", incval=16)
        ld0 = em.op("gpsimd", lambda e: e.dma_start(out=onesf[:], in_=onesi), inc="IDL", incval=16)
        ld0 = em.op("gpsimd", lambda e: e.dma_start(out=rbps[:], in_=rbp), inc="IDL", incval=16)
        ld0 = em.op("gpsimd", lambda e: e.dma_start(out=rwts[:], in_=rwt), inc="IDL", incval=16)
        ld0 = em.op("gpsimd", lambda e: e.dma_start(out=acts[:], in_=actv), inc="IDL", incval=16)
        ld0 = em.op("gpsimd", lambda e: e.dma_start(out=brows[:], in_=brow), inc="IDL", incval=16)
        em.wait("sync", "CC", cc)
        em.wait("vector", "IDL", ld0)
        em.wait("tensor", "IDL", ld0)
        em.wait("scalar", "IDL", ld0)

        dv_chain(em, lambda e: e.tensor_scalar(out=small[0:16, 4:5], in0=rbps[0:16, 1:2],
                                               scalar1=1e-12, scalar2=None, op0=ALU.add))
        v = em.val("DV")
        em.wait("scalar", "DV", v)
        a = em.op("scalar", lambda e: e.activation(out=small[0:16, 4:5], in_=small[0:16, 4:5],
                                                   func=AF.Ln), inc="AC")
        em.wait("vector", "AC", a)
        dv_chain(em, lambda e: e.tensor_tensor(out=small[0:16, 5:6], in0=rbps[0:16, 0:1],
                                               in1=small[0:16, 4:5], op=ALU.add))
        v = dv_chain(em, lambda e: e.tensor_copy(out=small[0:1, 0:4], in_=acts[:]))
        em.wait("tensor", "DV", v)
        pe = em.op("tensor", lambda e: e.matmul(ps_r2[0:16, 0:4], onesf[0:1, 0:16],
                                                small[0:1, 0:4], start=True, stop=True), inc="PE")
        em.wait("vector", "PE", pe)
        oh = sb("oh", [16, 4], F32)
        dv_chain(em, lambda e: e.tensor_scalar(out=oh[:], in0=ps_r2[0:16, 0:4],
                                               scalar1=cns[0:16, IOTA16], scalar2=None,
                                               op0=ALU.is_equal))
        pe = em.op("tensor", lambda e: e.matmul(ps_r2[:, 4:8], onesf[0:1, 0:128],
                                                small[0:1, 0:4], start=True, stop=True), inc="PE")
        em.wait("vector", "PE", pe)
        v = dv_chain(em, lambda e: e.tensor_scalar(out=eq8[:], in0=ps_r2[:, 4:8],
                                                   scalar1=cns[:, 130:131], scalar2=None,
                                                   op0=ALU.is_equal))
        em.wait("tensor", "DV", v)
        pe = em.op("tensor", lambda e: e.transpose(ps_r1[0:4, 0:128],
                                                   eq8[:], cns[:, IDENT]), inc="PE")
        em.wait("scalar", "PE", pe)
        eq8t = sb("eq8t", [4, 128], F32)
        a = em.op("scalar", lambda e: e.copy(out=eq8t[:], in_=ps_r1[0:4, 0:128]), inc="AC")
        em.wait("tensor", "AC", a)
        pe = em.op("tensor", lambda e: e.matmul(ps_r2[:, 8:40], eq8t[:], cns[0:4, E48ONE],
                                                start=True, stop=True), inc="PE")
        em.wait("vector", "PE", pe)
        dv_chain(em, lambda e: e.tensor_tensor(out=ssel[:], in0=ps_r2[:, 8:40],
                                               in1=cns[:, C1M], op=ALU.mult))
        v = dv_chain(em, lambda e: e.tensor_copy(out=ssel_bf[:], in_=ssel[:]))
        em.wait("tensor", "DV", v)
        bms_user = [0, 0]
        psr1_user = ("AC", a)
        for ob in range(8):
            k = ob % 2
            em.wait("gpsimd", "PE", bms_user[k])
            bl = em.op("gpsimd", lambda e, ob=ob, k=k: e.dma_start(
                out=bms[:, k, :], in_=bmat[:, ob * 512:(ob + 1) * 512]),
                inc="BM%d" % k, incval=16)
            em.wait("tensor", "BM%d" % k, bl)
            sn, sv = psr1_user
            em.wait("tensor", sn, sv)
            pe = em.op("tensor", lambda e, k=k: e.matmul(
                ps_r1[0:32, :], ssel_bf[:], bms[:, k, :], start=True, stop=True), inc="PE")
            bms_user[k] = pe
            em.wait("scalar", "PE", pe)
            a = em.op("scalar", lambda e, ob=ob: e.copy(out=bsel[:, ob, :],
                                                        in_=ps_r1[0:32, :]), inc="AC")
            psr1_user = ("AC", a)

        wbs_user = [0, 0]
        psy_user = [0, 0]
        x1t_user = 0
        x2b_user = 0
        xg_user = 0
        last_st = 0

        for g2 in range(n_tiles // 4):
            em.wait("gpsimd", "PE", xg_user)
            xl = em.op("gpsimd", lambda e, g2=g2: e.dma_start(
                out=xg[:], in_=xfm[:, :, g2 * 512:(g2 + 1) * 512]), inc="XC", incval=16)
            sn, sv = psr1_user
            em.wait("tensor", sn, sv)
            em.wait("tensor", "XC", xl)
            def mmz(e):
                for hc in range(HC):
                    m = e.matmul(ps_r1[:, :], ats[:, hc, :], xg[:, hc, :],
                                 start=(hc == 0), stop=(hc == HC - 1))
                return m
            pe = em.op("tensor", mmz, inc="PE")
            em.wait("scalar", "PE", pe)
            a = em.op("scalar", lambda e: e.copy(out=z_sb[:], in_=ps_r1[:]), inc="AC")
            psr1_user = ("AC", a)
            def mml(e):
                for hc in range(HC):
                    m = e.matmul(ps_l[0:16, :], rwts[:, hc, :], xg[:, hc, :],
                                 start=(hc == 0), stop=(hc == HC - 1))
                return m
            pe = em.op("tensor", mml, inc="PE")
            em.wait("vector", "PE", pe)
            vv = dv_chain(em, lambda e: e.tensor_scalar(out=lg_sb[:], in0=ps_l[0:16, :],
                                                        scalar1=small[0:16, 5:6], scalar2=None,
                                                        op0=ALU.add))
            em.wait("vector", "AC", em.val("AC"))
            v = dv_chain(em, lambda e: e.tensor_copy(out=lg_sb[:], in_=lg_sb[:]))
            em.wait("tensor", "DV", v)
            pe = em.op("tensor", lambda e: e.matmul(ps_r2[0:4, 0:512], oh[:], lg_sb[:],
                                                    start=True, stop=True), inc="PE")
            em.wait("scalar", "PE", pe)
            a = em.op("scalar", lambda e: e.activation(out=e_sb[:], in_=ps_r2[0:4, 0:512],
                                                       func=AF.Exp), inc="AC")
            em.wait("tensor", "AC", a)
            pe = em.op("tensor", lambda e: e.matmul(ps_r2[0:1, 0:512], onesf[0:4, 0:1],
                                                    e_sb[:], start=True, stop=True), inc="PE")
            em.wait("vector", "PE", pe)
            rd = small[0:1, 8:9]
            v = dv_chain(em, lambda e: e.reciprocal(out=lg_sb[0:1, 0:512],
                                                    in_=ps_r2[0:1, 0:512]))
            em.wait("tensor", "DV", v)
            pe = em.op("tensor", lambda e: e.matmul(ps_r2[0:4, 0:512], onesf[0:1, 0:4],
                                                    lg_sb[0:1, 0:512], start=True, stop=True),
                       inc="PE")
            em.wait("scalar", "PE", pe)
            a = em.op("scalar", lambda e: e.copy(out=rb4[:], in_=ps_r2[0:4, 0:512]), inc="AC")
            em.wait("vector", "AC", a)
            v = dv_chain(em, lambda e: e.tensor_tensor(out=alpha[:], in0=e_sb[:],
                                                       in1=rb4[:], op=ALU.mult))
            em.wait("tensor", "DV", v)
            pe = em.op("tensor", lambda e: e.matmul(ps_r2[0:32, 0:512], cns[0:4, E48],
                                                    alpha[:], start=True, stop=True), inc="PE")
            em.wait("scalar", "PE", pe)
            a = em.op("scalar", lambda e: e.copy(out=a32_sb[:], in_=ps_r2[0:32, 0:512]),
                      inc="AC")
            sn, sv = psr1_user
            em.wait("tensor", sn, sv)
            em.wait("tensor", "AC", em.val("AC"))
            pe = em.op("tensor", lambda e: e.matmul(ps_r1[0:32, :], ssel[:], z_sb[:],
                                                    start=True, stop=True), inc="PE")
            em.wait("vector", "PE", pe)
            em.wait("vector", "AC", a)
            v = dv_chain(em, lambda e: e.tensor_tensor(out=zw_bf[:], in0=ps_r1[0:32, :],
                                                       in1=a32_sb[:], op=ALU.mult))
            psr1_user = ("DV", v)
            zw_ready = v

            for tt in range(4):
                t = g2 * 4 + tt
                tsl = slice(tt * 128, (tt + 1) * 128)
                em.wait("gpsimd", "DV", x1t_user)
                xl2 = em.op("gpsimd", lambda e, t=t: e.dma_start(
                    out=x1t[:], in_=xtm[t * 128:(t + 1) * 128, :]), inc="XT", incval=16)
                radds = 0
                for ob in range(8):
                    k = ob % 2
                    obr = slice(ob * 512, (ob + 1) * 512)
                    em.wait("sync", "PE", wbs_user[k])
                    wl = em.op("sync", lambda e, ob=ob, k=k: e.dma_start(
                        out=wbs[:, k, :, :], in_=wb_v[:, ob, :, :]), inc="WB%d" % k, incval=16)
                    em.wait("tensor", "WB%d" % k, wl)
                    em.wait("tensor", "DV", psy_user[k])
                    if ob == 0:
                        em.wait("tensor", "DV", zw_ready)
                    def mmbase(e, k=k, obr=obr, tsl=tsl, ob=ob):
                        e.matmul(ps_y[:, k, :], onesf[0:1, 0:128], brows[0:1, obr],
                                 start=True, stop=False)
                        for hc in range(HC):
                            e.matmul(ps_y[:, k, :], xg[:, hc, tsl], wbs[:, k, hc, :],
                                     start=False, stop=False)
                        return e.matmul(ps_y[:, k, :], zw_bf[0:32, tsl], bsel[:, ob, :],
                                        start=False, stop=True)
                    pe = em.op("tensor", mmbase, inc="PE")
                    wbs_user[k] = pe
                    em.wait("vector", "PE", pe)
                    if ob == 0:
                        em.wait("vector", "XT", xl2)
                    v = dv_chain(em, lambda e, k=k, obr=obr: e.tensor_tensor(
                        out=rrt[:, obr], in0=ps_y[:, k, :], in1=x1t[:, obr], op=ALU.add))
                    psy_user[k] = v
                    radds = v
                x1t_user = radds
                if debug and g2 == 0 and tt == 0:
                    em.wait("gpsimd", "DV", radds)
                    em.op("gpsimd", lambda e: e.dma_start(out=d_oh, in_=oh[:]), inc="DBG", incval=16)
                    em.op("gpsimd", lambda e: e.dma_start(out=d_ssel, in_=ssel[:]), inc="DBG", incval=16)
                    em.op("gpsimd", lambda e: e.dma_start(out=d_alpha, in_=alpha[:]), inc="DBG", incval=16)
                    em.op("gpsimd", lambda e: e.dma_start(out=d_zw, in_=zw_bf[:]), inc="DBG", incval=16)
                    em.op("gpsimd", lambda e: e.dma_start(out=d_lg, in_=lg_sb[0:16, :]), inc="DBG", incval=16)
                    em.op("gpsimd", lambda e: e.dma_start(out=d_rrt, in_=rrt[:]), inc="DBG", incval=16)
                    em.op("gpsimd", lambda e: e.dma_start(out=d_z, in_=z_sb[:]), inc="DBG", incval=16)
                    em.wait("vector", "DBG", 112)
                em.wait("vector", "ST", x2b_user)
                lnv = emit_ln(em, rrt[:], trash[:], ms, va, x2b[:])
                em.wait("gpsimd", "DV", lnv)
                stv = em.op("gpsimd", lambda e, t=t: e.dma_start(
                    out=x2o[t * 128:(t + 1) * 128, :], in_=x2b[:]), inc="ST", incval=16)
                if x2fm_o is not None:
                    pst_user = emit_fm_out(em, x2b[:], identx, xfmob, ps_t, pst_user,
                                           x2fm_o, t * 128, lnv)
                x2b_user = em.val("ST")
                last_st = stv
            xg_user = em.val("PE")

        em.wait("gpsimd", "ST", last_st)
        if x2fm_o is not None:
            em.wait("gpsimd", "FM", em.val("FM"))


def build_p1(n_tiles=NT, debug=False):
    nc = bass.Bass("TRN2", target_bir_lowering=False)
    io = {}
    io["xfm"] = nc.dram_tensor("xfm", [128, HC, n_tiles * 128], BF16, kind="ExternalInput").ap()
    io["xtm"] = nc.dram_tensor("xtm", [n_tiles * 128, D], BF16, kind="ExternalInput").ap()
    wb_sh = nc.dram_tensor("wb_sh", [16, HC * D], BF16, kind="ExternalInput").ap()
    io["aT"] = nc.dram_tensor("aT", [128, HC, 128], BF16, kind="ExternalInput").ap()
    io["bmat"] = nc.dram_tensor("bmat", [128, D], BF16, kind="ExternalInput").ap()
    io["rwt"] = nc.dram_tensor("rwt", [128, HC, 16], BF16, kind="ExternalInput").ap()
    io["rbp"] = nc.dram_tensor("rbp", [16, 2], F32, kind="ExternalInput").ap()
    io["actv"] = nc.dram_tensor("actv", [1, 4], I32, kind="ExternalInput").ap()
    io["brow"] = nc.dram_tensor("brow", [1, D], F32, kind="ExternalInput").ap()
    io["consts"] = nc.dram_tensor("consts", [128, 232], F32, kind="ExternalInput").ap()
    io["onesi"] = nc.dram_tensor("onesi", [128, 128], F32, kind="ExternalInput").ap()
    io["x2o"] = nc.dram_tensor("x2", [n_tiles * 128, D], BF16, kind="ExternalOutput").ap()
    if debug:
        io["d_oh"] = nc.dram_tensor("d_oh", [16, 4], F32, kind="ExternalOutput").ap()
        io["d_ssel"] = nc.dram_tensor("d_ssel", [128, 32], F32, kind="ExternalOutput").ap()
        io["d_alpha"] = nc.dram_tensor("d_alpha", [4, 512], F32, kind="ExternalOutput").ap()
        io["d_zw"] = nc.dram_tensor("d_zw", [32, 512], BF16, kind="ExternalOutput").ap()
        io["d_lg"] = nc.dram_tensor("d_lg", [16, 512], F32, kind="ExternalOutput").ap()
        io["d_rrt"] = nc.dram_tensor("d_rrt", [128, D], F32, kind="ExternalOutput").ap()
        io["d_z"] = nc.dram_tensor("d_z", [128, 512], F32, kind="ExternalOutput").ap()
    wb_in = nc.dram_tensor("wb_in", [16, HC * D], BF16).ap()
    wb_f = nc.dram_tensor("wb_f", [128, HC * D], BF16).ap()
    io["wb_v"] = wb_f.rearrange("p (ob hc w) -> p ob hc w", ob=8, hc=HC)
    em = Emit(nc)
    io["cc_wait"] = emit_weight_gather(em, nc, [(wb_sh, wb_in, wb_f)])
    emit_p1(nc, em, io, n_tiles=n_tiles, debug=debug)
    em.run()
    em.ctx.close()
    return nc


def p1_consts():
    cns = np.zeros((128, 232), np.float32)
    cns[:, 0:128] = np.eye(128, dtype=np.float32)
    cns[:, 128] = 1.0
    cns[0:16, 129] = np.arange(16, dtype=np.float32)
    cns[:, 130] = np.arange(128, dtype=np.float32) // 8
    ii = np.arange(128)[:, None]
    jj = np.arange(32)[None, :]
    cns[:, 131:163] = (ii % 8 == jj % 8).astype(np.float32)
    e48 = np.zeros((4, 32), np.float32)
    for e in range(4):
        e48[e, e * 8:(e + 1) * 8] = 2.0
    cns[0:4, 163:195] = e48
    cns[0:4, 200:232] = e48 * 0.5
    return cns


def p1_host_inputs(x1, W_base, b_base, lora_A, lora_B, router_W, router_b,
                   domain_prior, active, n_tiles=NT):
    x1b = cast_bf(x1)
    xfm_full = np.ascontiguousarray(x1b.astype(np.float32).T.astype(BF)).reshape(
        HC, 128, 8192).transpose(1, 0, 2)
    wb = np.ascontiguousarray(cast_bf(np.asarray(W_base).T).reshape(HC, 128, 8, 512)
                              .transpose(1, 2, 0, 3)).reshape(128, HC * D)
    amat = np.asarray(lora_A, np.float32).reshape(128, D)
    aT = np.ascontiguousarray(cast_bf(amat.T).reshape(HC, 128, 128).transpose(1, 0, 2))
    bmat = cast_bf(np.transpose(np.asarray(lora_B, np.float32), (0, 2, 1)).reshape(128, D))
    rwt = np.ascontiguousarray(cast_bf(np.asarray(router_W, np.float32).T)
                               .reshape(HC, 128, 16).transpose(1, 0, 2))
    rbp = np.stack([np.asarray(router_b, np.float32),
                    np.asarray(domain_prior, np.float32)], axis=1)
    actv = np.asarray(active, np.int32).reshape(1, 4)
    brow = np.asarray(b_base, np.float32).reshape(1, D)
    cns = p1_consts()
    maps = []
    for c in range(NCORE):
        sl = core_token_idx(c, n_tiles)
        maps.append({
            "xfm": np.ascontiguousarray(xfm_full[:, :, sl]),
            "xtm": np.ascontiguousarray(x1b[sl]),
            "wb_sh": np.ascontiguousarray(wb[16 * c:16 * c + 16]),
            "aT": aT, "bmat": bmat, "rwt": rwt, "rbp": rbp,
            "actv": actv, "brow": brow, "consts": cns,
            "onesi": np.ones((128, 128), np.float32),
        })
    return maps



def emit_p2a(nc, em, io):
    NTOK = 8192
    NB = 16
    xg_src = io["xg_src"]
    ctx_dst = io["ctx_dst"]
    wq_sl = io["wq_sl"]; wk_sl = io["wk_sl"]; wv_sl = io["wv_sl"]
    cosf_d = io["cosf_d"]; sinf_d = io["sinf_d"]; dmask_d = io["dmask_d"]
    onesi = io["onesi"]; q_d = io["q_d"]
    xg_wait = io.get("xg_wait")

    with ExitStack() as ctx:
        def sb(n, s, d):
            _NAME_SEQ[0] += 1
            return ctx.enter_context(nc.sbuf_tensor("%s_%d" % (n, _NAME_SEQ[0]), s, d))
        def psb(n, s, d=F32):
            _NAME_SEQ[0] += 1
            return ctx.enter_context(nc.psum_tensor("%s_%d" % (n, _NAME_SEQ[0]), s, d))

        arenaB = sb("arenaB", [128, 40960], BF16)
        xg = arenaB[:, 0:16384].rearrange("p (hc w) -> p hc w", hc=HC)
        wqs = arenaB[:, 16384:32768].rearrange("p (hc w) -> p hc w", hc=HC)
        wks = arenaB[:, 32768:36864].rearrange("p (hc w) -> p hc w", hc=HC)
        wvs = arenaB[:, 36864:40960].rearrange("p (hc w) -> p hc w", hc=HC)
        wos = arenaB[:, 0:16384].rearrange("p (b h w) -> p b h w", b=2, h=16)
        cgt = arenaB[:, 16384:24576].rearrange("p (b hc w) -> p b hc w", b=2, hc=HC)
        x2t = arenaB[:, 24576:28672]
        x3b = arenaB[:, 28672:32768]
        trash = arenaB[:, 32768:36864]
        arenaF = sb("arenaF", [128, 6144], F32)
        qk5 = arenaF[:, 0:2560].rearrange("p (h w) -> p h w", h=5)
        rot5 = arenaF[:, 2560:5120].rearrange("p (h w) -> p h w", h=5)
        rtmp = arenaF[:, 5120:6144].rearrange("p (b w) -> p b w", b=2)
        rrt = arenaF[:, 0:4096]
        Ksb = sb("Ksb", [128, NTOK], BF16)
        Vsb = sb("Vsb", [128, 64, 128], BF16)
        cst = sb("cst", [128, 2, 512], F32)
        qrope = sb("qrope", [128, 512], BF16)
        onesf = sb("onesf", [128, 128], F32)
        dmsk = sb("dmsk", [128, 4, 512], BF16)
        qt = sb("qt", [128, 2, 512], BF16)
        e_bf = sb("e_bf", [128, 3, 512], BF16)
        acc = sb("acc", [128, 512], F32)
        rd = sb("rd", [1, 512], F32)
        rbc = sb("rbc", [128, 512], F32)
        ctxn = sb("ctxn", [128, 2, 512], BF16)
        ms = sb("ms", [128, 2], F32)
        va = sb("va", [128, 2], F32)

        ps_q = psb("ps_q", [128, 2, 512])
        ps_s = psb("ps_s", [128, 3, 512])
        ps_c = psb("ps_c", [128, 2, 512])

        ld0 = em.op("gpsimd", lambda e: e.dma_start(out=onesf[:], in_=onesi), inc="IDL", incval=16)
        ld0 = em.op("gpsimd", lambda e: e.dma_start(out=dmsk[:], in_=dmask_d), inc="IDL", incval=16)
        ld0 = em.op("gpsimd", lambda e: e.dma_start(out=wqs[:], in_=wq_sl), inc="IDL", incval=16)
        ld0 = em.op("gpsimd", lambda e: e.dma_start(out=wks[:], in_=wk_sl), inc="IDL", incval=16)
        ld0 = em.op("gpsimd", lambda e: e.dma_start(out=wvs[:], in_=wv_sl), inc="IDL", incval=16)
        em.wait("tensor", "IDL", ld0)
        em.wait("vector", "IDL", ld0)

        xg_user = 0
        cst_user = 0
        qk5_user = 0
        psq_user = [0, 0]
        qst = 0
        for tb in range(NB):
            tsl = slice(tb * 512, (tb + 1) * 512)
            em.wait("sync", "PE", xg_user)
            if tb == 0 and xg_wait is not None:
                em.wait("sync", xg_wait[0], xg_wait[1])
            xl = em.op("sync", lambda e, tb=tb: e.dma_start(
                out=xg[:], in_=xg_src(tb)), inc="XG", incval=16)
            em.wait("sync", "DV", cst_user)
            cl = em.op("sync", lambda e, tsl=tsl: e.dma_start(
                out=cst[:, 0, :], in_=cosf_d[:, tsl]), inc="CS0", incval=16)
            cl2 = em.op("sync", lambda e, tsl=tsl: e.dma_start(
                out=cst[:, 1, :], in_=sinf_d[:, tsl]), inc="CS1", incval=16)
            em.wait("gpsimd", "DV", qk5_user)
            em.wait("scalar", "DV", qk5_user)
            for hh in range(5):
                k = hh % 2
                em.wait("tensor", "XG", xl)
                em.wait("tensor", "AC", psq_user[k])
                def mmq(e, hh=hh, k=k):
                    w = wqs if hh < 4 else wks
                    c0 = hh * 128 if hh < 4 else 0
                    for hc in range(HC):
                        m = e.matmul(ps_q[:, k, :], w[:, hc, c0:c0 + 128] if hh < 4 else w[:, hc, :],
                                     xg[:, hc, :], start=(hc == 0), stop=(hc == HC - 1))
                    return m
                pe = em.op("tensor", mmq, inc="PE")
                em.wait("scalar", "PE", pe)
                ac = em.op("scalar", lambda e, hh=hh, k=k: e.copy(
                    out=qk5[:, hh, :], in_=ps_q[:, k, :]), inc="AC")
                psq_user[k] = ac
            qk_done = em.val("AC")
            em.wait("gpsimd", "AC", qk_done)
            r1 = em.op("gpsimd", lambda e: e.dma_start(
                out=rot5[0:64, :, :], in_=qk5[64:128, :, :]), inc="RT", incval=16)
            r1 = em.op("gpsimd", lambda e: e.dma_start(
                out=rot5[64:128, :, :], in_=qk5[0:64, :, :]), inc="RT", incval=16)
            em.wait("vector", "RT", r1)
            em.wait("vector", "CS0", cl)
            em.wait("vector", "CS1", cl2)
            for hh in range(5):
                dv_chain(em, lambda e, hh=hh: e.tensor_tensor(
                    out=rtmp[:, 0, :], in0=qk5[:, hh, :], in1=cst[:, 0, :], op=ALU.mult))
                dv_chain(em, lambda e, hh=hh: e.tensor_tensor(
                    out=rtmp[:, 1, :], in0=rot5[:, hh, :], in1=cst[:, 1, :], op=ALU.mult))
                if hh < 4:
                    dv_chain(em, lambda e: e.tensor_tensor(
                        out=rtmp[:, 0, :], in0=rtmp[:, 0, :], in1=rtmp[:, 1, :], op=ALU.add))
                    em.wait("vector", "QS", qst)
                    v = dv_chain(em, lambda e: e.tensor_scalar(
                        out=qrope[:], in0=rtmp[:, 0, :], scalar1=0.08838834764831845,
                        scalar2=None, op0=ALU.mult))
                    em.wait("gpsimd", "DV", v)
                    qst = em.op("gpsimd", lambda e, hh=hh, tsl=tsl: e.dma_start(
                        out=q_d[hh, :, tsl], in_=qrope[:]), inc="QS", incval=16)
                else:
                    v = dv_chain(em, lambda e, tsl=tsl: e.tensor_tensor(
                        out=Ksb[:, tsl], in0=rtmp[:, 0, :], in1=rtmp[:, 1, :], op=ALU.add))
            qk5_user = em.val("DV")
            cst_user = em.val("DV")
            for tt in range(4):
                k = tt % 2
                vt = tb * 4 + tt
                em.wait("tensor", "AC", psq_user[k])
                def mmv(e, tt=tt, k=k):
                    for hc in range(HC):
                        m = e.matmul(ps_q[:, k, :128], xg[:, hc, tt * 128:(tt + 1) * 128],
                                     wvs[:, hc, :], start=(hc == 0), stop=(hc == HC - 1))
                    return m
                pe = em.op("tensor", mmv, inc="PE")
                em.wait("scalar", "PE", pe)
                ac = em.op("scalar", lambda e, vt=vt, k=k: e.copy(
                    out=Vsb[:, vt, :], in_=ps_q[:, k, :128]), inc="AC")
                psq_user[k] = ac
            xg_user = em.val("PE")

        kv_done_ac = em.val("AC")
        kv_done_dv = em.val("DV")

        qt_user = [0, 0]
        pss_user = [0, 0, 0]
        ebf_user = [(0, 0), (0, 0), (0, 0)]
        psc_user = [0, 0]
        ctxn_user = [0, 0]
        acc_user = 0
        acc_pe_user = 0
        cstore = 0
        em.wait("tensor", "AC", kv_done_ac)
        em.wait("tensor", "DV", kv_done_dv)
        em.wait("scalar", "QS", em.val("QS"))
        for g in range(NB):
            for h4 in range(4):
                gi = g * 4 + h4
                cp = gi % 2
                NK = 4 * g + 4
                k = gi % 2
                em.wait("scalar", "PE", qt_user[k])
                ql = em.op("scalar", lambda e, h4=h4, g=g, k=k: e.dma_start(
                    out=qt[:, k, :], in_=q_d[h4, :, g * 512:(g + 1) * 512]),
                    inc="QT%d" % k, incval=16)
                v = em.op("vector", lambda e: e.memset(acc[:], 0.0), inc="DV")
                em.wait("vector", "DV", v)
                exp_ac = [0] * NK
                acc_dv = [0] * NK

                def emit_ctx(j):
                    sk = j % 3
                    if j >= NK - 4:
                        em.wait("tensor", "DV", acc_dv[j])
                    else:
                        em.wait("tensor", "AC", exp_ac[j])
                    if j == 0:
                        em.wait("tensor", "DV", psc_user[cp])
                    pe = em.op("tensor", lambda e, j=j, sk=sk, cp=cp, NK=NK: e.matmul(
                        ps_c[:, cp, :], Vsb[:, j, :], e_bf[:, sk, :],
                        start=(j == 0), stop=(j == NK - 1)), inc="PE")
                    ebf_user[sk] = (pe, acc_dv[j])

                for j in range(NK):
                    sk = j % 3
                    em.wait("tensor", "AC", pss_user[sk])
                    if j == 0:
                        em.wait("tensor", "QT%d" % k, ql)
                    pe_s = em.op("tensor", lambda e, j=j, sk=sk, k=k: e.matmul(
                        ps_s[:, sk, :], Ksb[:, j * 128:(j + 1) * 128],
                        qt[:, k, :], start=True, stop=True), inc="PE")
                    if j >= 1:
                        emit_ctx(j - 1)
                    em.wait("scalar", "PE", pe_s)
                    epe, edv = ebf_user[sk]
                    em.wait("scalar", "PE", epe)
                    em.wait("scalar", "DV", edv)
                    ac = em.op("scalar", lambda e, sk=sk: e.activation(
                        out=e_bf[:, sk, :], in_=ps_s[:, sk, :], func=AF.Exp), inc="AC")
                    pss_user[sk] = ac
                    exp_ac[j] = ac
                    em.wait("vector", "AC", ac)
                    if j >= NK - 4:
                        dd = j - (NK - 4)
                        dv_chain(em, lambda e, sk=sk, dd=dd: e.tensor_tensor(
                            out=e_bf[:, sk, :], in0=e_bf[:, sk, :], in1=dmsk[:, dd, :],
                            op=ALU.mult))
                    acc_dv[j] = dv_chain(em, lambda e, sk=sk: e.tensor_tensor(
                        out=acc[:], in0=acc[:], in1=e_bf[:, sk, :], op=ALU.add))
                emit_ctx(NK - 1)
                qt_user[k] = em.val("PE")
                em.wait("tensor", "DV", em.val("DV"))
                em.wait("tensor", "AC", pss_user[0])
                em.wait("tensor", "AC", pss_user[1])
                pe = em.op("tensor", lambda e: e.matmul(ps_s[:, 0, :][0:1, :], onesf[:, 0:1],
                                                        acc[:], start=True, stop=True), inc="PE")
                em.wait("vector", "PE", pe)
                v = dv_chain(em, lambda e: e.reciprocal(out=rd[:], in_=ps_s[0:1, 0, :]))
                em.wait("tensor", "DV", v)
                pe = em.op("tensor", lambda e: e.matmul(ps_s[:, 1, :], onesf[0:1, :],
                                                        rd[:], start=True, stop=True), inc="PE")
                em.wait("scalar", "PE", pe)
                ac = em.op("scalar", lambda e: e.copy(out=rbc[:], in_=ps_s[:, 1, :]), inc="AC")
                pss_user = [em.val("AC"), ac, pss_user[2]]
                em.wait("vector", "AC", ac)
                em.wait("vector", "CT%d" % cp, ctxn_user[cp])
                cv = dv_chain(em, lambda e, cp=cp: e.tensor_tensor(
                    out=ctxn[:, cp, :], in0=ps_c[:, cp, :], in1=rbc[:], op=ALU.mult))
                psc_user[cp] = cv
                em.wait("gpsimd", "DV", cv)
                cstore = em.op("gpsimd", lambda e, h4=h4, g=g, cp=cp: e.dma_start(
                    out=ctx_dst(g, h4),
                    in_=ctxn[:, cp, :]), inc="CT%d" % cp, incval=16)
                ctxn_user[cp] = cstore

        em.wait("gpsimd", "CT0", em.val("CT0"))
        em.wait("gpsimd", "CT1", em.val("CT1"))


def build_p2():
    nc = bass.Bass("TRN2", target_bir_lowering=False)
    NTOK = 8192
    io = {}
    xfm = nc.dram_tensor("xfm", [128, HC, NTOK], BF16, kind="ExternalInput").ap()
    io["wq_sl"] = nc.dram_tensor("wq_sl", [128, HC, 512], BF16, kind="ExternalInput").ap()
    io["wk_sl"] = nc.dram_tensor("wk_sl", [128, HC, 128], BF16, kind="ExternalInput").ap()
    io["wv_sl"] = nc.dram_tensor("wv_sl", [128, HC, 128], BF16, kind="ExternalInput").ap()
    io["cosf_d"] = nc.dram_tensor("cosf_d", [128, NTOK], F32, kind="ExternalInput").ap()
    io["sinf_d"] = nc.dram_tensor("sinf_d", [128, NTOK], F32, kind="ExternalInput").ap()
    io["dmask_d"] = nc.dram_tensor("dmask_d", [128, 4, 512], BF16, kind="ExternalInput").ap()
    io["onesi"] = nc.dram_tensor("onesi", [128, 128], F32, kind="ExternalInput").ap()
    ctxo = nc.dram_tensor("ctxo", [512, NTOK], BF16, kind="ExternalOutput").ap()
    io["q_d"] = nc.dram_tensor("q_d", [4, 128, NTOK], BF16).ap()
    io["xg_src"] = lambda tb: xfm[:, :, tb * 512:(tb + 1) * 512]
    io["ctx_dst"] = lambda g, h4: ctxo[h4 * 128:(h4 + 1) * 128, g * 512:(g + 1) * 512]
    em = Emit(nc)
    emit_p2a(nc, em, io)
    em.run()
    em.ctx.close()
    return nc


def emit_p2b(nc, em, io):
    ctxsl = io["ctxsl"]; xtm = io["xtm"]; wo_v = io["wo_v"]
    x3o = io["x3o"]; x3fm_o = io.get("x3fm_o"); identb = io.get("identb")
    cc = io["cc_wait"]
    ctx_wait = io.get("ctx_wait")

    with ExitStack() as ctx:
        def sb(n, s, d):
            _NAME_SEQ[0] += 1
            return ctx.enter_context(nc.sbuf_tensor("%s_%d" % (n, _NAME_SEQ[0]), s, d))
        def psb(n, s, d=F32):
            _NAME_SEQ[0] += 1
            return ctx.enter_context(nc.psum_tensor("%s_%d" % (n, _NAME_SEQ[0]), s, d))
        wos = sb("wos", [128, 2, 16, 512], BF16)
        cgt = sb("cgt", [128, 2, HC, 128], BF16)
        x2t = sb("x2t", [128, D], BF16)
        rrt = sb("rrt", [128, D], F32)
        x3b = sb("x3b", [128, D], BF16)
        trash = sb("trashd", [128, D], BF16)
        ms = sb("ms", [128, 2], F32)
        va = sb("va", [128, 2], F32)
        ps_o = psb("ps_o", [128, 2, 512])
        if x3fm_o is not None:
            identx = sb("identx2", [128, 128], BF16)
            xfmob = sb("xfmob2", [128, HC, 128], BF16)
            ps_tA = psb("ps_tA2", [128, 1024], BF16)
            ps_tB = psb("ps_tB2", [128, 1024], BF16)
            ps_t = [ps_tA[:, 0:128], ps_tB[:, 0:128]]
            pst_user = [0, 0]
            idl = em.op("gpsimd", lambda e: e.dma_start(out=identx[:], in_=identb),
                        inc="IDL", incval=16)
            em.wait("tensor", "IDL", idl)

        em.wait("sync", "CC", cc)
        em.wait("scalar", "CC", cc)
        if ctx_wait is not None:
            em.wait("scalar", ctx_wait[0], ctx_wait[1])
            em.wait("gpsimd", ctx_wait[0], ctx_wait[1])

        cgt_user = [0, 0]
        wos_user = [0, 0]
        pso_user = [0, 0]
        x2t_user = 0
        x3b_user = 0
        last_st = 0
        for t in range(NT):
            k = t % 2
            em.wait("scalar", "PE", cgt_user[k])
            gl = em.op("scalar", lambda e, t=t, k=k: e.dma_start(
                out=cgt[:, k, :, :], in_=ctxsl[:, :, t * 128:(t + 1) * 128]),
                inc="CG%d" % k, incval=16)
            em.wait("gpsimd", "DV", x2t_user)
            xl2 = em.op("gpsimd", lambda e, t=t: e.dma_start(
                out=x2t[:], in_=xtm[t * 128:(t + 1) * 128, :]), inc="XT", incval=16)
            radds = 0
            for ob in range(8):
                ok = ob % 2
                obr = slice(ob * 512, (ob + 1) * 512)
                for half in range(2):
                    em.wait("sync", "PE", wos_user[half])
                    wl = em.op("sync", lambda e, ob=ob, half=half: e.dma_start(
                        out=wos[:, half, :, :], in_=wo_v[:, ob, half * 16:(half + 1) * 16, :]),
                        inc="WO%d" % half, incval=16)
                    em.wait("tensor", "WO%d" % half, wl)
                    if half == 0:
                        em.wait("tensor", "CG%d" % k, gl)
                        em.wait("tensor", "DV", pso_user[ok])
                    def mmo(e, k=k, ok=ok, half=half):
                        for hc2 in range(16):
                            hc = half * 16 + hc2
                            m = e.matmul(ps_o[:, ok, :], cgt[:, k, hc, :], wos[:, half, hc2, :],
                                         start=(hc == 0), stop=(hc == HC - 1))
                        return m
                    pe = em.op("tensor", mmo, inc="PE")
                    wos_user[half] = pe
                em.wait("vector", "PE", pe)
                if ob == 0:
                    em.wait("vector", "XT", xl2)
                v = dv_chain(em, lambda e, ok=ok, obr=obr: e.tensor_tensor(
                    out=rrt[:, obr], in0=ps_o[:, ok, :], in1=x2t[:, obr], op=ALU.add))
                pso_user[ok] = v
                radds = v
            cgt_user[k] = em.val("PE")
            x2t_user = radds
            em.wait("vector", "ST", x3b_user)
            lnv = emit_ln(em, rrt[:], trash[:], ms, va, x3b[:])
            em.wait("gpsimd", "DV", lnv)
            stv = em.op("gpsimd", lambda e, t=t: e.dma_start(
                out=x3o[t * 128:(t + 1) * 128, :], in_=x3b[:]), inc="ST", incval=16)
            if x3fm_o is not None:
                pst_user = emit_fm_out(em, x3b[:], identx, xfmob, ps_t, pst_user,
                                       x3fm_o, t * 128, lnv)
            x3b_user = em.val("ST")
            last_st = stv
        em.wait("gpsimd", "ST", last_st)
        if x3fm_o is not None:
            em.wait("gpsimd", "FM", em.val("FM"))


def build_p2b():
    nc = bass.Bass("TRN2", target_bir_lowering=False)
    io = {}
    io["ctxsl"] = nc.dram_tensor("ctxsl", [128, HC, NT * 128], BF16, kind="ExternalInput").ap()
    io["xtm"] = nc.dram_tensor("xtm", [NT * 128, D], BF16, kind="ExternalInput").ap()
    wo_sh = nc.dram_tensor("wo_sh", [16, HC * D], BF16, kind="ExternalInput").ap()
    io["x3o"] = nc.dram_tensor("x3", [NT * 128, D], BF16, kind="ExternalOutput").ap()
    wo_in = nc.dram_tensor("wo_in", [16, HC * D], BF16).ap()
    wo_f = nc.dram_tensor("wo_f", [128, HC * D], BF16).ap()
    io["wo_v"] = wo_f.rearrange("p (ob hc w) -> p ob hc w", ob=8, hc=HC)
    em = Emit(nc)
    io["cc_wait"] = emit_weight_gather(em, nc, [(wo_sh, wo_in, wo_f)])
    emit_p2b(nc, em, io)
    em.run()
    em.ctx.close()
    return nc


def p2b_host_inputs(ctx_full, x2, Wo):
    x2b = cast_bf(x2)
    wo = np.ascontiguousarray(cast_bf(np.asarray(Wo).T).reshape(HC, 128, 8, 512)
                              .transpose(1, 2, 0, 3)).reshape(128, HC * D)
    cv = np.ascontiguousarray(np.asarray(ctx_full).reshape(HC, 128, 8192)
                              .transpose(1, 0, 2))
    maps = []
    for c in range(NCORE):
        sl = core_token_idx(c)
        maps.append({
            "ctxsl": np.ascontiguousarray(cv[:, :, sl]),
            "xtm": np.ascontiguousarray(x2b[sl]),
            "wo_sh": np.ascontiguousarray(wo[16 * c:16 * c + 16]),
        })
    return maps


def rope_tables():
    import math
    DH = 128
    inv = 1.0 / 500000.0 ** (np.arange(0, DH, 2, dtype=np.float32) / DH)
    wav = np.float32(2.0 * math.pi) / inv
    inv2 = np.where(wav > 8192.0, inv / np.float32(32.0), inv)
    smooth = (np.float32(8192.0) / wav - np.float32(1.0)) / np.float32(3.0)
    smoothed = (np.float32(1.0) - smooth) * inv2 / np.float32(32.0) + smooth * inv2
    is_med = ~(wav < 8192.0) & ~(wav > 2048.0)
    inv3 = np.where(is_med, smoothed, inv2)
    pos = np.arange(8192, dtype=np.float32)
    fr = pos[:, None] * inv3[None, :]
    cos = np.cos(fr).astype(np.float32)
    sin = np.sin(fr).astype(np.float32)
    cosf = np.concatenate([cos, cos], axis=1).T.copy()
    sinf = np.concatenate([-sin, sin], axis=1).T.copy()
    return cosf, sinf


def p2_host_inputs(x2, Wq, Wk, Wv, Wo):
    x2b = cast_bf(x2)
    xfm_full = np.ascontiguousarray(
        x2b.astype(np.float32).T.astype(BF)).reshape(HC, 128, 8192).transpose(1, 0, 2)
    xfm_full = np.ascontiguousarray(xfm_full)
    wqT = np.ascontiguousarray(cast_bf(np.asarray(Wq).T).reshape(HC, 128, D).transpose(1, 0, 2))
    wkT = np.ascontiguousarray(cast_bf(np.asarray(Wk).T).reshape(HC, 128, 1024).transpose(1, 0, 2))
    wvT = np.ascontiguousarray(cast_bf(np.asarray(Wv).T).reshape(HC, 128, 1024).transpose(1, 0, 2))
    wo = np.ascontiguousarray(cast_bf(np.asarray(Wo).T).reshape(HC, 128, D)
                              .transpose(1, 0, 2)).reshape(128, HC * D)
    cosf, sinf = rope_tables()
    kk = np.arange(128)[:, None]
    qq = np.arange(512)[None, :]
    dm = np.stack([(qq >= 128 * d + kk) for d in range(4)], axis=1).astype(np.float32).astype(BF)
    ones = np.ones((128, 128), np.float32)
    maps = []
    for c in range(NCORE):
        sl = core_token_idx(c)
        maps.append({
            "xfm": xfm_full,
            "xtm": np.ascontiguousarray(x2b[sl]),
            "wq_sl": np.ascontiguousarray(wqT[:, :, 512 * c:512 * c + 512]),
            "wk_sl": np.ascontiguousarray(wkT[:, :, 128 * c:128 * c + 128]),
            "wv_sl": np.ascontiguousarray(wvT[:, :, 128 * c:128 * c + 128]),
            "cosf_d": cosf, "sinf_d": sinf, "dmask_d": np.ascontiguousarray(dm),
            "onesi": ones,
        })
    return maps



def emit_p3(nc, em, io, n_tiles=NT):
    xfm = io["xfm"]; xtm = io["xtm"]; rts_d = io["rts_d"]; a1_d = io["a1_d"]
    b1_d = io["b1_d"]; e816_d = io["e816_d"]; identf_d = io["identf_d"]
    x4o = io["x4o"]

    with ExitStack() as ctx:
        def sb(n, s, d):
            _NAME_SEQ[0] += 1
            return ctx.enter_context(nc.sbuf_tensor("%s_%d" % (n, _NAME_SEQ[0]), s, d))
        def psb(n, s, d=F32):
            _NAME_SEQ[0] += 1
            return ctx.enter_context(nc.psum_tensor("%s_%d" % (n, _NAME_SEQ[0]), s, d))
        xg = sb("xg", [128, HC, 512], BF16)
        rts = sb("rts", [128, HC, 8], BF16)
        a1s = sb("a1s", [128, HC, 128], BF16)
        b1m = sb("b1m", [128, D], BF16)
        e816 = sb("e816", [8, 128], F32)
        identf = sb("identf", [128, 128], F32)
        lg8 = sb("lg8", [8, 512], F32)
        lgt = sb("lgt", [128, 8], F32)
        val = sb("val", [128, 2], F32)
        rw_tm = sb("rw_tm", [128, 8], F32)
        rw_fm = sb("rw_fm", [8, 512], F32)
        rwe = sb("rwe", [128, 512], F32)
        zw_bf = sb("zw_bf", [128, 512], BF16)
        xt = sb("xt", [128, D], BF16)
        rrt = sb("rrt", [128, D], F32)
        x4f = sb("x4f", [128, D], F32)
        trash = sb("trashe", [128, D], BF16)
        ms = sb("ms", [128, 2], F32)
        va = sb("va", [128, 2], F32)
        ps_z = psb("ps_z", [128, 512])
        ps_l = psb("ps_l", [128, 512])
        ps_t = psb("ps_t", [128, 512])
        ps_f = psb("ps_f", [128, 2, 512])

        ld0 = 0
        for (dst, srcp) in ((rts, rts_d), (a1s, a1_d), (b1m, b1_d),
                            (e816, e816_d), (identf, identf_d)):
            ld0 = em.op("gpsimd", lambda e, dst=dst, srcp=srcp: e.dma_start(
                out=dst[:], in_=srcp), inc="IDL", incval=16)
        for eng in ("tensor", "vector", "scalar"):
            em.wait(eng, "IDL", ld0)

        psz_user = ("AC", 0)
        psl_user = ("AC", 0)
        pst_user = ("AC", 0)
        psf_user = [0, 0]
        xt_user = 0
        x4_user = 0
        xg_user = 0
        last_st = 0
        for g2 in range(n_tiles // 4):
            em.wait("gpsimd", "PE", xg_user)
            xl = em.op("gpsimd", lambda e, g2=g2: e.dma_start(
                out=xg[:], in_=xfm[:, :, g2 * 512:(g2 + 1) * 512]), inc="XC", incval=16)
            em.wait("tensor", "XC", xl)
            sn, sv = psl_user
            em.wait("tensor", sn, sv)
            def mml(e):
                for hc in range(HC):
                    m = e.matmul(ps_l[0:8, :], rts[:, hc, :], xg[:, hc, :],
                                 start=(hc == 0), stop=(hc == HC - 1))
                return m
            pe = em.op("tensor", mml, inc="PE")
            em.wait("scalar", "PE", pe)
            a = em.op("scalar", lambda e: e.copy(out=lg8[:], in_=ps_l[0:8, :]), inc="AC")
            psl_user = ("AC", a)
            sn, sv = psz_user
            em.wait("tensor", sn, sv)
            def mmz(e):
                for hc in range(HC):
                    m = e.matmul(ps_z[:, :], a1s[:, hc, :], xg[:, hc, :],
                                 start=(hc == 0), stop=(hc == HC - 1))
                return m
            pez = em.op("tensor", mmz, inc="PE")
            em.wait("tensor", "AC", a)
            for q in range(4):
                sn, sv = pst_user
                em.wait("tensor", sn, sv)
                pe = em.op("tensor", lambda e, q=q: e.transpose(
                    ps_t[0:128, 0:8], lg8[:, q * 128:(q + 1) * 128], identf[0:8, 0:8]), inc="PE")
                em.wait("scalar", "PE", pe)
                a = em.op("scalar", lambda e: e.copy(out=lgt[:], in_=ps_t[0:128, 0:8]),
                          inc="AC")
                em.wait("vector", "AC", a)
                dv_chain(em, lambda e: e.reduce_max(val[:, 0:1], lgt[:], axis=AX.X))
                dv_chain(em, lambda e: e.tensor_scalar(
                    out=rw_tm[:], in0=lgt[:], scalar1=val[:, 0:1], scalar2=None,
                    op0=ALU.is_equal))
                v = dv_chain(em, lambda e: e.tensor_scalar(
                    out=rw_tm[:], in0=rw_tm[:], scalar1=val[:, 0:1], scalar2=None,
                    op0=ALU.mult))
                em.wait("tensor", "DV", v)
                pe = em.op("tensor", lambda e, q=q: e.transpose(
                    ps_t[0:8, 0:128], rw_tm[:], identf[:]), inc="PE")
                em.wait("scalar", "PE", pe)
                a = em.op("scalar", lambda e, q=q: e.copy(
                    out=rw_fm[:, q * 128:(q + 1) * 128], in_=ps_t[0:8, 0:128]), inc="AC")
                pst_user = ("AC", a)
            em.wait("tensor", "AC", a)
            sn, sv = pst_user
            pe = em.op("tensor", lambda e: e.matmul(ps_t[:, :], e816[:], rw_fm[:],
                                                    start=True, stop=True), inc="PE")
            pst_user = ("PE", pe)
            em.wait("scalar", "PE", pe)
            a = em.op("scalar", lambda e: e.copy(out=rwe[:], in_=ps_t[:, :]), inc="AC")
            pst_user = ("AC", a)
            em.wait("vector", "AC", a)
            em.wait("vector", "PE", pez)
            v = dv_chain(em, lambda e: e.tensor_tensor(
                out=zw_bf[:], in0=ps_z[:, :], in1=rwe[:], op=ALU.mult))
            psz_user = ("DV", v)
            zw_ready = v
            for tt in range(4):
                t = g2 * 4 + tt
                tsl = slice(tt * 128, (tt + 1) * 128)
                em.wait("gpsimd", "DV", xt_user)
                xl2 = em.op("gpsimd", lambda e, t=t: e.dma_start(
                    out=xt[:], in_=xtm[t * 128:(t + 1) * 128, :]), inc="XT", incval=16)
                radds = 0
                for ob in range(8):
                    k = ob % 2
                    obr = slice(ob * 512, (ob + 1) * 512)
                    em.wait("tensor", "DV", psf_user[k])
                    if ob == 0 and tt == 0:
                        em.wait("tensor", "DV", zw_ready)
                    pe = em.op("tensor", lambda e, k=k, obr=obr, tsl=tsl: e.matmul(
                        ps_f[:, k, :], zw_bf[:, tsl], b1m[:, obr],
                        start=True, stop=True), inc="PE")
                    em.wait("vector", "PE", pe)
                    if ob == 0:
                        em.wait("vector", "XT", xl2)
                    v = dv_chain(em, lambda e, k=k, obr=obr: e.tensor_tensor(
                        out=rrt[:, obr], in0=ps_f[:, k, :], in1=xt[:, obr], op=ALU.add))
                    psf_user[k] = v
                    radds = v
                xt_user = radds
                em.wait("vector", "ST", x4_user)
                lnv = emit_ln(em, rrt[:], trash[:], ms, va, x4f[:])
                em.wait("gpsimd", "DV", lnv)
                stv = em.op("gpsimd", lambda e, t=t: e.dma_start(
                    out=x4o[t * 128:(t + 1) * 128, :], in_=x4f[:]), inc="ST", incval=16)
                x4_user = stv
                last_st = stv
            xg_user = em.val("PE")
        em.wait("gpsimd", "ST", last_st)


def build_p3(n_tiles=NT):
    nc = bass.Bass("TRN2", target_bir_lowering=False)
    io = {}
    io["xfm"] = nc.dram_tensor("xfm", [128, HC, n_tiles * 128], BF16, kind="ExternalInput").ap()
    io["xtm"] = nc.dram_tensor("xtm", [n_tiles * 128, D], BF16, kind="ExternalInput").ap()
    io["rts_d"] = nc.dram_tensor("rts_d", [128, HC, 8], BF16, kind="ExternalInput").ap()
    io["a1_d"] = nc.dram_tensor("a1_d", [128, HC, 128], BF16, kind="ExternalInput").ap()
    io["b1_d"] = nc.dram_tensor("b1_d", [128, D], BF16, kind="ExternalInput").ap()
    io["e816_d"] = nc.dram_tensor("e816_d", [8, 128], F32, kind="ExternalInput").ap()
    io["identf_d"] = nc.dram_tensor("identf_d", [128, 128], F32, kind="ExternalInput").ap()
    io["x4o"] = nc.dram_tensor("x4", [n_tiles * 128, D], F32, kind="ExternalOutput").ap()
    em = Emit(nc)
    emit_p3(nc, em, io, n_tiles=n_tiles)
    em.run()
    em.ctx.close()
    return nc


def p3_host_inputs(x3, route_w, A, Bw, n_tiles=NT):
    x3b = cast_bf(x3)
    xfm_full = np.ascontiguousarray(
        x3b.astype(np.float32).T.astype(BF)).reshape(HC, 128, 8192).transpose(1, 0, 2)
    rts = np.ascontiguousarray(cast_bf(np.asarray(route_w, np.float32).T)
                               .reshape(HC, 128, 8).transpose(1, 0, 2))
    a1 = np.ascontiguousarray(cast_bf(np.asarray(A, np.float32).reshape(128, D).T)
                              .reshape(HC, 128, 128).transpose(1, 0, 2))
    b1 = cast_bf(np.transpose(np.asarray(Bw, np.float32), (0, 2, 1)).reshape(128, D))
    e816 = np.zeros((8, 128), np.float32)
    for e in range(8):
        e816[e, e * 16:(e + 1) * 16] = 1.0
    identf = np.eye(128, dtype=np.float32)
    maps = []
    for c in range(NCORE):
        sl = core_token_idx(c, n_tiles)
        maps.append({
            "xfm": np.ascontiguousarray(xfm_full[:, :, sl]),
            "xtm": np.ascontiguousarray(x3b[sl]),
            "rts_d": rts, "a1_d": a1, "b1_d": b1, "e816_d": e816,
            "identf_d": identf,
        })
    return maps



def build_fused():
    nc = bass.Bass("TRN2", target_bir_lowering=False)
    NTOK = 8192
    ei = lambda n, s, d: nc.dram_tensor(n, s, d, kind="ExternalInput").ap()
    xfm_i = ei("xfm", [128, HC, NT * 128], BF16)
    xtm_i = ei("xtm", [NT * 128, D], BF16)
    identb = ei("identb", [128, 128], BF16)
    wg_sh = ei("wg_sh", [16, HC * I_FF], BF16)
    wu_sh = ei("wu_sh", [16, HC * I_FF], BF16)
    wd_sh = ei("wd_sh", [16, IC * D], BF16)
    wb_sh = ei("wb_sh", [16, HC * D], BF16)
    wo_sh = ei("wo_sh", [16, HC * D], BF16)
    aT = ei("aT", [128, HC, 128], BF16)
    bmat = ei("bmat", [128, D], BF16)
    rwt = ei("rwt", [128, HC, 16], BF16)
    rbp = ei("rbp", [16, 2], F32)
    actv = ei("actv", [1, 4], I32)
    brow = ei("brow", [1, D], F32)
    consts = ei("consts", [128, 232], F32)
    onesi = ei("onesi", [128, 128], F32)
    wq_sl = ei("wq_sl", [128, HC, 512], BF16)
    wk_sl = ei("wk_sl", [128, HC, 128], BF16)
    wv_sl = ei("wv_sl", [128, HC, 128], BF16)
    cosf_d = ei("cosf_d", [128, NTOK], F32)
    sinf_d = ei("sinf_d", [128, NTOK], F32)
    dmask_d = ei("dmask_d", [128, 4, 512], BF16)
    rts_d = ei("rts_d", [128, HC, 8], BF16)
    a1_d = ei("a1_d", [128, HC, 128], BF16)
    b1_d = ei("b1_d", [128, D], BF16)
    e816_d = ei("e816_d", [8, 128], F32)
    identf_d = ei("identf_d", [128, 128], F32)
    x4o = nc.dram_tensor("x4", [NT * 128, D], F32, kind="ExternalOutput").ap()

    it = lambda n, s, d: nc.dram_tensor(n, s, d).ap()
    wg_in = it("wg_in", [16, HC * I_FF], BF16)
    wu_in = it("wu_in", [16, HC * I_FF], BF16)
    wd_in = it("wd_in", [16, IC * D], BF16)
    wb_in = it("wb_in", [16, HC * D], BF16)
    wo_in = it("wo_in", [16, HC * D], BF16)
    its = lambda n, s, d: nc.dram_tensor(n, s, d, addr_space="Shared").ap()
    wg_f = its("wg_f", [128, HC * I_FF], BF16)
    wu_f = its("wu_f", [128, HC * I_FF], BF16)
    wd_f = its("wd_f", [128, IC * D], BF16)
    wb_f = its("wb_f", [128, HC * D], BF16)
    wo_f = its("wo_f", [128, HC * D], BF16)
    x1_d = it("x1_d", [NT * 128, D], BF16)
    x1fm_d = it("x1fm_d", [128, HC, NT * 128], BF16)
    x2_d = it("x2_d", [NT * 128, D], BF16)
    x2ag_in = it("x2ag_in", [4096, NT * 128], BF16)
    x2ag_out = its("x2ag_out", [8 * 4096, NT * 128], BF16)
    q_d = it("q_d", [4, 128, NTOK], BF16)
    cx_in = it("cx_in", [4096, NT * 128], BF16)
    cx_out = it("cx_out", [4096, NT * 128], BF16)
    x3_d = it("x3_d", [NT * 128, D], BF16)
    x3fm_d = it("x3fm_d", [128, HC, NT * 128], BF16)

    em = Emit(nc)
    HALF_G = 22 * HC * 256
    REST_G = HC * I_FF - HALF_G
    wg_inA = it("wg_inA", [16, HALF_G], BF16)
    wg_inB = it("wg_inB", [16, REST_G], BF16)
    wg_fA = its("wg_fA", [128, HALF_G], BF16)
    wg_fB = its("wg_fB", [128, REST_G], BF16)
    for a_, b_, c_ in ((wg_inA, wg_fA, (0, HALF_G)), (wg_inB, wg_fB, (HALF_G, HC * I_FF))):
        _NAME_SEQ[0] += 1
        wig = "WI_%d" % _NAME_SEQ[0]
        n_ = em.op("sync", lambda e, a_=a_, c_=c_: e.dma_start(
            out=a_, in_=wg_sh[:, c_[0]:c_[1]]), inc=wig, incval=16)
        em.wait("gpsimd", wig, n_)
        em.op("gpsimd", lambda e, a_=a_, b_=b_: e.collective_compute(
            "AllGather", ALU.bypass, replica_groups=[list(range(NCORE))],
            ins=[a_.opt()], outs=[b_.opt()]), inc="CC")
    emit_weight_gather(em, nc, [(wu_sh, wu_in, wu_f), (wd_sh, wd_in, wd_f)])
    cc4 = emit_weight_gather(em, nc, [(wb_sh, wb_in, wb_f)])
    cc5 = emit_weight_gather(em, nc, [(wo_sh, wo_in, wo_f)])
    cc3 = (1, 2, 3, 4)

    io0 = {
        "wg_v2": (wg_fA.rearrange("p (ib hc w) -> p ib hc w", ib=22, hc=HC),
                  wg_fB.rearrange("p (ib hc w) -> p ib hc w", ib=21, hc=HC)),
        "xfm": xfm_i, "xtm": xtm_i, "identb": identb,
        "wg_v": wg_f.rearrange("p (ib hc w) -> p ib hc w", ib=43, hc=HC),
        "wu_v": wu_f.rearrange("p (ib hc w) -> p ib hc w", ib=43, hc=HC),
        "wd_v": wd_f.rearrange("p (ob ic w) -> p ob ic w", ob=8, ic=IC),
        "x1o": x1_d, "x1fm_o": x1fm_d, "cc_wait": cc3,
    }
    emit_p0(nc, em, io0)
    em.barrier()

    io1 = {
        "xfm": x1fm_d, "xtm": x1_d,
        "wb_v": wb_f.rearrange("p (ob hc w) -> p ob hc w", ob=8, hc=HC),
        "aT": aT, "bmat": bmat, "rwt": rwt, "rbp": rbp, "actv": actv,
        "brow": brow, "consts": consts, "onesi": onesi,
        "x2o": x2_d, "x2fm_o": x2ag_in.rearrange("(hc p) t -> p hc t", p=128),
        "identb": identb, "cc_wait": cc4,
    }
    emit_p1(nc, em, io1)
    em.barrier()

    cc6 = em.op("gpsimd", lambda e: e.collective_compute(
        "AllGather", ALU.bypass, replica_groups=[list(range(NCORE))],
        ins=[x2ag_in.opt()], outs=[x2ag_out.opt()]), inc="CC")

    x2v = x2ag_out.rearrange("(r hc p) t -> r p hc t", r=8, p=128)

    def xg_src(tb):
        r = tb if tb < 8 else 15 - tb
        col0 = 0 if tb < 8 else 512
        return x2v[r, :, :, col0:col0 + 512]

    def ctx_dst(g, h4):
        dd = g if g < 8 else 15 - g
        lcol = 0 if g < 8 else 512
        return cx_in[dd * 512 + h4 * 128:dd * 512 + (h4 + 1) * 128, lcol:lcol + 512]

    io2 = {
        "xg_src": xg_src, "ctx_dst": ctx_dst, "q_d": q_d,
        "wq_sl": wq_sl, "wk_sl": wk_sl, "wv_sl": wv_sl,
        "cosf_d": cosf_d, "sinf_d": sinf_d, "dmask_d": dmask_d, "onesi": onesi,
        "xg_wait": ("CC", cc6),
    }
    emit_p2a(nc, em, io2)
    em.barrier()

    def a2afn(e):
        with allow_ici_all_to_all():
            return e.collective_compute(
                "AllToAll", ALU.bypass, replica_groups=[list(range(NCORE))],
                ins=[cx_in.opt()], outs=[cx_out.opt()])
    cc7 = em.op("gpsimd", a2afn, inc="CC")

    io2b = {
        "ctxsl": cx_out.rearrange("(hc p) t -> p hc t", p=128),
        "xtm": x2_d,
        "wo_v": wo_f.rearrange("p (ob hc w) -> p ob hc w", ob=8, hc=HC),
        "x3o": x3_d, "x3fm_o": x3fm_d, "identb": identb,
        "cc_wait": cc5, "ctx_wait": ("CC", cc7),
    }
    emit_p2b(nc, em, io2b)
    em.barrier()

    io3 = {
        "xfm": x3fm_d, "xtm": x3_d, "rts_d": rts_d, "a1_d": a1_d,
        "b1_d": b1_d, "e816_d": e816_d, "identf_d": identf_d, "x4o": x4o,
    }
    emit_p3(nc, em, io3)
    em.run()
    em.ctx.close()
    return nc


def fused_host_inputs(inputs):
    x = np.asarray(inputs["L0_attn_x"], np.float32).reshape(8192, 4096)
    m0 = p0_host_inputs(x, inputs["L0_attn_w_gate"], inputs["L0_attn_w_up"],
                        inputs["L0_attn_w_down"])
    zeros = np.zeros((8192, 4096), np.float32)
    m1 = p1_host_inputs(zeros, inputs["L0_ffn_W_base"], inputs["L0_ffn_b_base"],
                        inputs["L0_ffn_lora_A"], inputs["L0_ffn_lora_B"],
                        inputs["L0_ffn_router_W"], inputs["L0_ffn_router_b"],
                        inputs["L0_ffn_domain_prior"], inputs["L0_ffn_active"])
    m2 = p2_host_inputs(zeros, inputs["L1_attn_Wq"], inputs["L1_attn_Wk"],
                        inputs["L1_attn_Wv"], inputs["L1_attn_Wo"])
    m2b = p2b_host_inputs(np.zeros((4096, 8192), np.float32), zeros,
                          inputs["L1_attn_Wo"])
    m3 = p3_host_inputs(zeros, inputs["L1_ffn_route_w"], inputs["L1_ffn_A"],
                        inputs["L1_ffn_Bw"])
    maps = []
    for c in range(NCORE):
        mm = {}
        mm.update({k: v for k, v in m0[c].items()})
        mm.update({k: v for k, v in m1[c].items() if k not in ("xfm", "xtm")})
        mm.update({k: v for k, v in m2[c].items() if k not in ("xfm", "xtm")})
        mm.update({k: v for k, v in m2b[c].items() if k not in ("ctxsl", "xtm")})
        mm.update({k: v for k, v in m3[c].items() if k not in ("xfm", "xtm")})
        maps.append(mm)
    return maps


FUSED = True

_BUILD_CACHE = {}


def _built(name, fn):
    if name not in _BUILD_CACHE:
        _BUILD_CACHE[name] = fn()
    return _BUILD_CACHE[name]


def kernel(**inputs):
    if FUSED:
        maps = fused_host_inputs(inputs)
        nc = _built("fused", build_fused)
        res = run_bass_kernel_spmd(nc, maps, core_ids=list(range(NCORE)))
        x4 = gather_tokens(res.results, "x4")
        return np.ascontiguousarray(x4.astype(np.float32)).reshape(1, 8192, 4096)

    x = np.asarray(inputs["L0_attn_x"], np.float32).reshape(8192, 4096)

    nc0 = _built("p0", build_p0)
    maps0 = p0_host_inputs(x, inputs["L0_attn_w_gate"], inputs["L0_attn_w_up"],
                           inputs["L0_attn_w_down"])
    res0 = run_bass_kernel_spmd(nc0, maps0, core_ids=list(range(NCORE)))
    x1 = gather_tokens(res0.results, "x1")

    nc1 = _built("p1", build_p1)
    maps1 = p1_host_inputs(x1, inputs["L0_ffn_W_base"], inputs["L0_ffn_b_base"],
                           inputs["L0_ffn_lora_A"], inputs["L0_ffn_lora_B"],
                           inputs["L0_ffn_router_W"], inputs["L0_ffn_router_b"],
                           inputs["L0_ffn_domain_prior"], inputs["L0_ffn_active"])
    res1 = run_bass_kernel_spmd(nc1, maps1, core_ids=list(range(NCORE)))
    x2 = gather_tokens(res1.results, "x2")

    nc2 = _built("p2", build_p2)
    maps2 = p2_host_inputs(x2, inputs["L1_attn_Wq"], inputs["L1_attn_Wk"],
                           inputs["L1_attn_Wv"], inputs["L1_attn_Wo"])
    res2 = run_bass_kernel_spmd(nc2, maps2, core_ids=list(range(NCORE)))
    ctx_full = np.concatenate([res2.results[c]["ctxo"] for c in range(NCORE)], axis=0)

    nc2b = _built("p2b", build_p2b)
    maps2b = p2b_host_inputs(ctx_full, x2, inputs["L1_attn_Wo"])
    res2b = run_bass_kernel_spmd(nc2b, maps2b, core_ids=list(range(NCORE)))
    x3 = gather_tokens(res2b.results, "x3")

    nc3 = _built("p3", build_p3)
    maps3 = p3_host_inputs(x3, inputs["L1_ffn_route_w"], inputs["L1_ffn_A"],
                           inputs["L1_ffn_Bw"])
    res3 = run_bass_kernel_spmd(nc3, maps3, core_ids=list(range(NCORE)))
    x4 = gather_tokens(res3.results, "x4")

    return np.ascontiguousarray(x4.astype(np.float32)).reshape(1, 8192, 4096)
